# version 19
# baseline (speedup 1.0000x reference)
import math
import numpy as np
import ml_dtypes
import contextlib
import os
import concourse.bass as bass
import concourse.mybir as mybir
from concourse.bass_utils import run_bass_kernel_spmd

F32 = mybir.dt.float32
BF16 = mybir.dt.bfloat16
AF = mybir.ActivationFunctionType
ALU = mybir.AluOpType
AX = mybir.AxisListType

NEG = -1.0e30
EPS = 1.0e-5


def kt_keep(slope, q0, qc, kt):
    thr = 110.0 - slope * (qc / 2.0)
    return slope * (q0 - (128 * kt + 127)) <= max(thr, 0.0)

def b_slot(h):
    if h < 4:
        return h, 0
    if h < 8:
        return h - 4, 1
    if h < 12:
        return h - 4, 0
    return h - 8, 1


def c64(h):
    s, hi = b_slot(h)
    return 2 * s + hi


def gpos_list(S, ng=8):
    return [int(x) for x in np.linspace(0, S - 1, ng).astype(np.int64)]


def qc_size(h):
    slope = 2.0 ** (-(h + 1) / 2.0)
    for qc in (512, 256, 128):
        if slope * (qc // 2) <= 64.0:
            return qc
    return 64


def alibi_plan(S, CH):
    cols, vals = {}, []
    for h in range(16):
        qc = qc_size(h)
        slope = 2.0 ** (-(h + 1) / 2.0)
        for c in range(S // CH):
            for qci in range(CH // qc):
                q0 = c * CH + qci * qc
                for kt in range((q0 + qc + 127) // 128):
                    if not kt_keep(slope, q0, qc, kt):
                        continue
                    key = (h, 128 * kt - q0 - qc // 2)
                    if key not in cols:
                        cols[key] = len(vals)
                        vals.append(slope * (128 * kt - q0 - qc // 2
                                             + np.arange(128, dtype=np.float64)))
    tab = np.array(vals, dtype=np.float32).T if vals else np.zeros((128, 1), np.float32)
    return cols, tab


def host_consts(S, CH):
    idn = np.eye(128, dtype=np.float32)
    tri = np.full((128, 1152), NEG, np.float32)
    for p in range(128):
        tri[p, 512 + p:] = 0.0
    oh8 = np.zeros((1, 64), np.float32)
    for g in range(8):
        oh8[0, 8 * g + g] = 1.0
    rowsel = np.zeros((8, 8, 128), np.float32)
    for e in range(8):
        rowsel[e, e, :] = 1.0
    al_cols, al_tab = alibi_plan(S, CH)
    return {"idn": idn, "tri": tri, "oh8": oh8, "rowsel": rowsel,
            "alibi": al_tab, "al_cols": al_cols}


class K:
    ENG_BLOCK = {"t": "tensor", "v": "vector", "s": "scalar", "g": "gpsimd", "y": "sync"}

    FLUSH_N = 128

    def __init__(self, nc, block, sems, dsems):
        self.nc, self.block = nc, block
        self._buf = {e: [] for e in "tvsgy"}
        self.sem = sems
        self.dsem = dsems
        self.cnt = {e: 0 for e in "tvsgy"}
        self.dcnt = {q: [0] * len(dsems[q]) for q in dsems}
        self.dlane = {q: 0 for q in dsems}
        self._waited = {}
        self._had_wait = {e: True for e in "tvsgy"}

    def _blk(self, e):
        return getattr(self.block, self.ENG_BLOCK[e])

    def _emit(self, e, fn):
        self._buf[e].append(fn)
        if len(self._buf[e]) >= self.FLUSH_N:
            self.flush(e)

    def flush(self, e):
        fns = self._buf[e]
        if not fns:
            return
        self._buf[e] = []

        def body(eng):
            for f in fns:
                f(eng)
        self._blk(e)(body)

    def flush_all(self):
        for e in "tvsgy":
            self.flush(e)

    def op(self, e, fn, inc=False):
        marker = None
        if inc:
            prev = self.cnt[e]
            self.cnt[e] += 1
            marker = self.cnt[e]
            sem = self.sem[e]
            boundary = True

            def body(eng):
                if boundary:
                    eng.wait_ge(sem, prev)
                instr = fn(eng)
                assert instr is not None, "inc op must return its instruction"
                instr.then_inc(sem, 1)
            self._emit(e, body)
            self._had_wait[e] = False
        else:
            self._emit(e, fn)
        return marker

    def wait(self, e, prod, val):
        if not val or e == prod:
            return
        key = (e, "c", prod)
        if not os.environ.get("MK_NODEDUP") and self._waited.get(key, 0) >= val:
            return
        self._waited[key] = val
        sem = self.sem[prod]
        self._emit(e, lambda eng: eng.wait_ge(sem, val))
        self._had_wait[e] = True

    def dma(self, q, out, in_, **kw):
        lane = self.dlane[q]
        self.dlane[q] = (lane + 1) % len(self.dsem[q])
        sem = self.dsem[q][lane]
        prev = self.dcnt[q][lane]
        self.dcnt[q][lane] = prev + 16
        marker = (q, lane, prev + 16)

        def body(eng):
            if prev > 0:
                eng.wait_ge(sem, prev)
            eng.dma_start(out=out, in_=in_, **kw).then_inc(sem, 16)
        self._emit(q, body)
        key = (q, "d", q, lane)
        if prev > 0 and self._waited.get(key, 0) < prev:
            self._waited[key] = prev
        return marker

    def snap(self, q):
        return tuple((q, lane, c) for lane, c in enumerate(self.dcnt[q]) if c > 0)

    def wait_dmas(self, e, markers):
        for m in markers:
            self.wait_dma(e, m)

    def wait_dma(self, e, marker):
        if not marker:
            return
        q, lane, val = marker
        key = (e, "d", q, lane)
        if self._waited.get(key, 0) >= val:
            return
        self._waited[key] = val
        sem = self.dsem[q][lane]
        self._emit(e, lambda eng: eng.wait_ge(sem, val))
        self._had_wait[e] = True

    def barrier(self, engines="tvsgy"):
        marks = {}
        for e in engines:
            if e in "ygvs":
                for lane, cntv in enumerate(self.dcnt[e]):
                    self.wait_dma(e, (e, lane, cntv))
            marks[e] = self.op(e, lambda eng: eng.nop(), inc=True)
        for e in engines:
            for p in engines:
                if p != e:
                    self.wait(e, p, marks[p])
        self.flush_all()


class PsumPool:
    def __init__(self, k, tensors):
        self.k = k
        self.slots = [{"t": t, "readers": [], "wmark": 0, "held": False}
                      for t in tensors]
        self.i = 0

    def acquire(self, hold=False):
        for _ in range(len(self.slots)):
            s = self.slots[self.i % len(self.slots)]
            self.i += 1
            if s["held"]:
                continue
            for (eng, m) in s["readers"]:
                self.k.wait("t", eng, m)
            s["readers"] = []
            s["held"] = hold
            return s
        raise RuntimeError("psum pool exhausted (all held)")

    def release(self, s):
        s["held"] = False


def build_core(S=4096, CH=512, subset=(0, 1, 2, 3)):
    D, NH, HD, NKV = 1024, 16, 64, 4
    NEXP, NSH, HID = 8, 2, 512
    NE_ALL = NEXP + NSH
    NG = 8
    nch = S // CH
    NT = CH // 128
    NKT = S // 128
    gpos = gpos_list(S, NG)
    SCALE = 1.0 / math.sqrt(HD)
    al_cols, al_tab = alibi_plan(S, CH)
    NAL = al_tab.shape[1]
    run_ffn = (1 in subset) or (3 in subset)

    nc = bass.Bass("TRN2", target_bir_lowering=False, debug=False)

    x_in = nc.dram_tensor("x", [S, D], F32, kind="ExternalInput").ap()
    y_out = nc.dram_tensor("y", [S, D], F32, kind="ExternalOutput").ap()
    Xa = nc.dram_tensor("Xa", [S, D], F32).ap()
    Xb = nc.dram_tensor("Xb", [S, D], F32).ap()

    aw = {n: nc.dram_tensor("aa_" + n, [D, D], F32, kind="ExternalInput").ap()
          for n in ("Wq", "Wk", "Wv", "Wo")}
    abias = {n: nc.dram_tensor("aa_" + n, [D], F32, kind="ExternalInput").ap()
             for n in ("bq", "bk", "bv", "bo")}
    ab_Wq = nc.dram_tensor("ab_Wq", [D, D], F32, kind="ExternalInput").ap()
    ab_Wkv = nc.dram_tensor("ab_Wkv", [512, D], F32, kind="ExternalInput").ap()
    ab_Wo = nc.dram_tensor("ab_Wo", [D, D], F32, kind="ExternalInput").ap()
    f_rw = nc.dram_tensor("f_rw", [D, NEXP], F32, kind="ExternalInput").ap()
    f_rb = nc.dram_tensor("f_rb", [NEXP], F32, kind="ExternalInput").ap()
    f_w1 = nc.dram_tensor("f_w1", [NEXP, D, HID], F32, kind="ExternalInput").ap()
    f_w3 = nc.dram_tensor("f_w3", [NEXP, D, HID], F32, kind="ExternalInput").ap()
    f_w2 = nc.dram_tensor("f_w2", [NEXP, HID, D], F32, kind="ExternalInput").ap()
    f_s1 = nc.dram_tensor("f_s1", [NSH, D, HID], F32, kind="ExternalInput").ap()
    f_s3 = nc.dram_tensor("f_s3", [NSH, D, HID], F32, kind="ExternalInput").ap()
    f_s2 = nc.dram_tensor("f_s2", [NSH, HID, D], F32, kind="ExternalInput").ap()

    idn_in = nc.dram_tensor("idn", [128, 128], F32, kind="ExternalInput").ap()
    tri_in = nc.dram_tensor("tri", [128, 1152], F32, kind="ExternalInput").ap()
    oh8_in = nc.dram_tensor("oh8", [1, 64], F32, kind="ExternalInput").ap()
    rsel_in = nc.dram_tensor("rowsel", [8, 8, 128], F32, kind="ExternalInput").ap()
    alibi_in = nc.dram_tensor("alibi", [128, NAL], F32, kind="ExternalInput").ap()

    ffw13 = nc.dram_tensor("ffw13", [NE_ALL, 2, 128, 8, HID], BF16).ap()
    ffw2 = nc.dram_tensor("ffw2", [NE_ALL, 2, 128, 4, 512], BF16).ap()

    with contextlib.ExitStack() as _st:
        cvals = _st.enter_context(nc.sbuf_tensor("cvals", [128, 2], F32))
        idn_bf = _st.enter_context(nc.sbuf_tensor("idn_bf", [128, 128], BF16))
        tri_bf = _st.enter_context(nc.sbuf_tensor("tri_bf", [128, 1152], BF16))
        oh8_bf = _st.enter_context(nc.sbuf_tensor("oh8_bf", [1, 64], BF16))
        rsel_bf = _st.enter_context(nc.sbuf_tensor("rsel_bf", [8, 8, 128], BF16))
        ones_bf = _st.enter_context(nc.sbuf_tensor("ones_bf", [1, 512], BF16))
        neg_bf = _st.enter_context(nc.sbuf_tensor("neg_bf", [1, 128], BF16))
        alibi_sb = _st.enter_context(nc.sbuf_tensor("alibi_sb", [128, NAL], F32))
        brows = _st.enter_context(nc.sbuf_tensor("brows", [1, 5, 1024], BF16))
        xf0 = _st.enter_context(nc.sbuf_tensor("xf0", [128, NT, D], F32))
        xf1 = _st.enter_context(nc.sbuf_tensor("xf1", [128, NT, D], F32))
        xn_sb = _st.enter_context(nc.sbuf_tensor("xn", [128, NT, D], BF16))
        xnT0 = _st.enter_context(nc.sbuf_tensor("xnT0", [128, 8, CH], BF16))
        xnT1 = _st.enter_context(nc.sbuf_tensor("xnT1", [128, 8, CH], BF16))
        xout = _st.enter_context(nc.sbuf_tensor("xout", [128, NT, D], F32))
        stats = _st.enter_context(nc.sbuf_tensor("stats", [128, NT, 12], F32))
        mv = _st.enter_context(nc.sbuf_tensor("mv", [128, NT, 2], F32))
        rstd = _st.enter_context(nc.sbuf_tensor("rstd", [128, NT], F32))
        lntmp = _st.enter_context(nc.sbuf_tensor("lntmp", [128, NT], F32))
        catT = _st.enter_context(nc.sbuf_tensor("catT", [128, 8, CH], BF16))
        pf0 = _st.enter_context(nc.psum_tensor("pf0", [128, 512], F32))
        pf1 = _st.enter_context(nc.psum_tensor("pf1", [128, 512], F32))
        pf2 = _st.enter_context(nc.psum_tensor("pf2", [128, 512], F32))
        pf3 = _st.enter_context(nc.psum_tensor("pf3", [128, 512], F32))
        pf4 = _st.enter_context(nc.psum_tensor("pf4", [128, 512], F32))
        pf5 = _st.enter_context(nc.psum_tensor("pf5", [128, 512], F32))
        pb0 = _st.enter_context(nc.psum_tensor("pb0", [128, 1024], BF16))
        pb1 = _st.enter_context(nc.psum_tensor("pb1", [128, 1024], BF16))
        sem_t = _st.enter_context(nc.semaphore("sem_t"))
        sem_v = _st.enter_context(nc.semaphore("sem_v"))
        sem_s = _st.enter_context(nc.semaphore("sem_s"))
        sem_g = _st.enter_context(nc.semaphore("sem_g"))
        sem_y = _st.enter_context(nc.semaphore("sem_y"))
        dma_y = [_st.enter_context(nc.semaphore(f"dma_y{i}")) for i in range(6)]
        dma_g = [_st.enter_context(nc.semaphore(f"dma_g{i}")) for i in range(6)]
        dma_v = [_st.enter_context(nc.semaphore("dma_v0"))]
        dma_s = [_st.enter_context(nc.semaphore("dma_s0"))]
        block = _st.enter_context(nc.Block())
        k = K(nc, block,
              {"t": sem_t, "v": sem_v, "s": sem_s, "g": sem_g, "y": sem_y},
              {"y": dma_y, "g": dma_g, "v": dma_v, "s": dma_s})
        pool = PsumPool(k, [pf0, pf1, pf2, pf3, pf4, pf5])
        bpool = PsumPool(k, [pb0, pb1])
        xf = [xf0, xf1]
        xnT = [xnT0, xnT1]
        bufm = {}

        for src, dst in ((idn_in[:, :], idn_bf[:]), (tri_in[:, :], tri_bf[:]),
                         (oh8_in[:, :], oh8_bf[:]), (rsel_in[:, :, :], rsel_bf[:])):
            k.dma("g", dst, src)
        m_al = k.dma("y", alibi_sb[:], alibi_in[:, :])
        for i, src in enumerate((abias["bq"], abias["bk"], abias["bv"], abias["bo"])):
            k.dma("g", brows[0:1, i, :], src.rearrange("(a d) -> a d", a=1))
        k.dma("g", brows[0:1, 4, 0:NEXP], f_rb.rearrange("(a d) -> a d", a=1))
        m_consts_g = k.snap("g")
        nc.const_aps.register(F32, 0.0, cvals[:, 0:1])
        nc.const_aps.register(F32, EPS, cvals[:, 1:2])
        k.op("v", lambda v: v.memset(cvals[:, 0:1], 0.0))
        k.op("v", lambda v: v.memset(cvals[:, 1:2], EPS))
        k.op("v", lambda v: v.memset(ones_bf[:], 1.0))
        k.op("v", lambda v: v.drain())
        m_consts_v = k.op("v", lambda v: v.memset(neg_bf[:], NEG), inc=True)

        m_prestage = 0
        if run_ffn:
            for e in range(NE_ALL):
                s1 = f_w1[e] if e < NEXP else f_s1[e - NEXP]
                s3 = f_w3[e] if e < NEXP else f_s3[e - NEXP]
                s2 = f_w2[e] if e < NEXP else f_s2[e - NEXP]
                k.dma("g", ffw13[e, 0], s1.rearrange("(a p) h -> p a h", p=128))
                k.dma("g", ffw13[e, 1], s3.rearrange("(a p) h -> p a h", p=128))
                for b_ in range(2):
                    k.dma("g", ffw2[e, b_],
                          s2[:, 512 * b_:512 * b_ + 512].rearrange(
                              "(a p) d -> p a d", p=128))
            m_prestage = k.snap("g")

        def ln_chunk(src, c, slot):
            k.wait("y", "v", bufm.get(("xf_v", slot), 0))
            k.wait("y", "g", bufm.get(("xf_g", slot), 0))
            k.wait_dmas("y", bufm.get("src_stores", ()))
            mld = k.dma("y", xf[slot][:, :, :],
                        src[c * CH:(c + 1) * CH, :].rearrange("(j p) d -> p j d", p=128))
            k.wait_dma("v", mld)
            k.wait("v", "s", bufm.get("lnscalar", 0))
            for j in range(NT):
                k.op("v", lambda v, j=j: v.bn_stats(stats[:, j, 0:6], xf[slot][:, j, 0:512]))
                k.op("v", lambda v, j=j: v.bn_stats(stats[:, j, 6:12], xf[slot][:, j, 512:1024]))
            k.op("v", lambda v: v.drain())
            for j in range(NT):
                k.op("v", lambda v, j=j: v.bn_aggr(mv[:, j, :], stats[:, j, :]),
                     inc=(j == NT - 1))
            m_mv = k.cnt["v"]
            k.wait("s", "v", m_mv)
            k.op("s", lambda s: s.activation(lntmp[:, :], mv[:, :, 1], AF.Ln, bias=EPS))
            k.op("s", lambda s: s.drain())
            m_rstd = k.op("s", lambda s: s.activation(rstd[:, :], lntmp[:, :], AF.Exp,
                                                      scale=-0.5), inc=True)
            bufm["lnscalar"] = m_rstd
            k.wait("g", "s", m_rstd)
            k.wait("g", "t", bufm.get("xn_t", 0))
            k.wait_dma("g", mld)
            for j in range(NT):
                k.op("g", lambda g_, j=j, slot=slot: g_.tensor_scalar(
                    xn_sb[:, j, :], xf[slot][:, j, :], mv[:, j, 0:1], rstd[:, j:j + 1],
                    ALU.subtract, ALU.mult), inc=(j == NT - 1))
            m_xn = k.cnt["g"]
            bufm[("xf_g", slot)] = m_xn
            k.wait("t", "g", m_xn)
            for j in range(NT):
                b = bpool.acquire()
                for dc in range(8):
                    k.op("t", lambda t, j=j, dc=dc, b=b: t.matmul(
                        b["t"][:, 128 * dc:128 * dc + 128],
                        xn_sb[:, j, 128 * dc:128 * dc + 128], idn_bf[:],
                        start=True, stop=True, is_transpose=True), inc=(dc == 7))
                b["wmark"] = k.cnt["t"]
                k.wait("v", "t", b["wmark"])
                k.wait("v", "t", bufm.get(("xnT_t", slot), 0))
                mm = k.op("v", lambda v, j=j, b=b, slot=slot: v.tensor_copy(
                    xnT[slot][:, :, 128 * j:128 * j + 128],
                    b["t"][:].rearrange("p (a q) -> p a q", a=8)), inc=True)
                b["readers"].append(("v", mm))
            bufm["xn_t"] = k.cnt["t"]
            return mld, k.cnt["v"]

        def note_xnT_read(slot, marker):
            key = ("xnT_t", slot)
            bufm[key] = max(bufm.get(key, 0), marker)

        def proj_fm(wview, col0, m_xnT, slot, bias=None):
            p = pool.acquire()
            k.wait("t", "v", m_xnT)
            for ic in range(8):
                last = (ic == 7) and bias is None
                k.op("t", lambda t, ic=ic, p=p, last=last, slot=slot: t.matmul(
                    p["t"][:, 0:CH], wview[:, ic, col0:col0 + 128],
                    xnT[slot][:, ic, :], start=(ic == 0), stop=last), inc=last)
            if bias is not None:
                bi = bias
                k.op("t", lambda t, p=p: t.matmul(
                    p["t"][:, 0:CH], brows[0:1, bi, col0:col0 + 128],
                    ones_bf[0:1, 0:CH], start=False, stop=True), inc=True)
            p["wmark"] = k.cnt["t"]
            note_xnT_read(slot, p["wmark"])
            return p

        def wo_resid_store(wo_view, bias_i, dst, c, slot, m_cat_v):
            k.wait("t", "v", m_cat_v)
            last_v = 0
            for j in range(NT):
                for eh in range(2):
                    p = pool.acquire()
                    for ic in range(8):
                        last = (ic == 7) and bias_i is None
                        k.op("t", lambda t, j=j, eh=eh, ic=ic, p=p, last=last: t.matmul(
                            p["t"][:, 0:512],
                            catT[:, ic, 128 * j:128 * j + 128],
                            wo_view[:, ic, 512 * eh:512 * eh + 512],
                            start=(ic == 0), stop=last), inc=last)
                    if bias_i is not None:
                        k.op("t", lambda t, eh=eh, p=p, bi=bias_i: t.matmul(
                            p["t"][:, 0:512], ones_bf[0:1, 0:128],
                            brows[0:1, bi, 512 * eh:512 * eh + 512],
                            start=False, stop=True), inc=True)
                    p["wmark"] = k.cnt["t"]
                    k.wait("v", "t", p["wmark"])
                    k.wait_dma("v", bufm.get("xout_store"))
                    mv_ = k.op("v", lambda v, j=j, eh=eh, p=p: v.scalar_tensor_tensor(
                        xout[:, j, 512 * eh:512 * eh + 512], p["t"][:, 0:512], 1.0,
                        xf[slot][:, j, 512 * eh:512 * eh + 512],
                        ALU.mult, ALU.add), inc=True)
                    p["readers"].append(("v", mv_))
                    last_v = mv_
            bufm[("xf_v", slot)] = last_v
            k.wait("y", "v", last_v)
            mst = k.dma("y", dst[c * CH:(c + 1) * CH, :].rearrange("(j p) d -> p j d", p=128),
                        xout[:, :, :])
            bufm["xout_store"] = mst
            return mst

        def attn_a(src, dst, prologue=None):
            with contextlib.ExitStack() as _sst:
                wa = _sst.enter_context(nc.sbuf_tensor("a_w", [128, 4, 8, D], BF16))
                QT = _sst.enter_context(nc.sbuf_tensor("a_QT", [128, 8, CH], BF16))
                KT = _sst.enter_context(nc.sbuf_tensor("a_KT", [128, 8, CH], BF16))
                Vaug = _sst.enter_context(nc.sbuf_tensor("a_Vaug", [128, NT, 16, 65], BF16))
                KgT = _sst.enter_context(nc.sbuf_tensor("a_KgT", [128, 8, 8], BF16))
                Vg = _sst.enter_context(nc.sbuf_tensor("a_Vg", [8, 16, 65], BF16))
                BD = _sst.enter_context(nc.sbuf_tensor("a_BD", [128, 8, 128], BF16))
                macc = _sst.enter_context(nc.sbuf_tensor("a_macc", [128, 1024], F32))
                fs = _sst.enter_context(nc.sbuf_tensor("a_fs", [128, 8], F32))
                PTl = _sst.enter_context(nc.sbuf_tensor("a_PTl", [128, 512], BF16))
                PTg = _sst.enter_context(nc.sbuf_tensor("a_PTg", [8, 512], BF16))
                PgT = _sst.enter_context(nc.sbuf_tensor("a_PgT", [128, NT, 128], BF16))
                strip_bf = _sst.enter_context(nc.sbuf_tensor("a_strip", [128, 1024], BF16))
                atok = _sst.enter_context(nc.sbuf_tensor("a_tok", [128, NT, 64], BF16))
                rec4 = _sst.enter_context(nc.sbuf_tensor("a_rec", [128, NT], F32))
                xg = xout[0:8, 1, :]
                xgs = _sst.enter_context(nc.sbuf_tensor("a_xgs", [8, 14], F32))
                xgn = _sst.enter_context(nc.sbuf_tensor("a_xgn", [8, 1024], BF16))
                xgT = _sst.enter_context(nc.sbuf_tensor("a_xgT", [128, 8, 8], BF16))
                cglob = _sst.enter_context(nc.sbuf_tensor("a_cg", [128, 8, 8], BF16))
                saved = _sst.enter_context(nc.sbuf_tensor("a_sv", [128, 8, 8], BF16))
                grmw = xout[0:8, 0, :]
                for i, n in enumerate(("Wq", "Wk", "Wv", "Wo")):
                    k.dma("g", wa[:, i, :, :], aw[n].rearrange("(a p) o -> p a o", p=128))
                mw = k.snap("g")
                Wq, Wk, Wv, Wo = wa[:, 0], wa[:, 1], wa[:, 2], wa[:, 3]
                k.wait_dmas("t", mw)
                k.wait_dmas("t", m_consts_g)
                k.wait("t", "v", m_consts_v)
                k.wait_dma("v", m_al)

                mgl_ = [k.dma("y", xg[g:g + 1, :], src[gpos[g]:gpos[g] + 1, :])
                        for g in range(NG)]
                k.wait_dmas("v", mgl_)
                k.op("v", lambda v: v.bn_stats(xgs[:, 0:6], xg[:, 0:512]))
                k.op("v", lambda v: v.bn_stats(xgs[:, 6:12], xg[:, 512:1024]))
                k.op("v", lambda v: v.drain())
                m1 = k.op("v", lambda v: v.bn_aggr(xgs[:, 12:14], xgs[:, 0:12]), inc=True)
                k.wait("s", "v", m1)
                k.op("s", lambda s: s.activation(xgs[:, 0:1], xgs[:, 13:14], AF.Ln, bias=EPS))
                k.op("s", lambda s: s.drain())
                m2 = k.op("s", lambda s: s.activation(xgs[:, 1:2], xgs[:, 0:1], AF.Exp,
                                                      scale=-0.5), inc=True)
                k.wait("v", "s", m2)
                m3 = k.op("v", lambda v: v.tensor_scalar(
                    xgn[:, :], xg[:, :], xgs[:, 12:13], xgs[:, 1:2],
                    ALU.subtract, ALU.mult), inc=True)
                k.wait("t", "v", m3)
                b = bpool.acquire()
                for ic in range(8):
                    k.op("t", lambda t, ic=ic, b=b: t.matmul(
                        b["t"][:, 8 * ic:8 * ic + 8], xgn[:, 128 * ic:128 * ic + 128],
                        idn_bf[0:8, 0:8], start=True, stop=True, is_transpose=True),
                        inc=(ic == 7))
                b["wmark"] = k.cnt["t"]
                k.wait("v", "t", b["wmark"])
                m4 = k.op("v", lambda v, b=b: v.tensor_copy(
                    xgT[:, :, :], b["t"][:, 0:64].rearrange("p (a q) -> p a q", a=8)),
                    inc=True)
                b["readers"].append(("v", m4))

                k.op("v", lambda v: v.memset(BD[:, :, :], 0.0))
                k.op("v", lambda v: v.memset(Vg[:, :, :], 1.0))
                k.op("v", lambda v: v.drain())
                m_bd0 = k.op("v", lambda v: v.nop(), inc=True)

                k.wait("t", "v", m4)
                for dc in range(8):
                    p = pool.acquire()
                    for ic in range(8):
                        k.op("t", lambda t, dc=dc, ic=ic, p=p: t.matmul(
                            p["t"][:, 0:8], Wq[:, ic, 128 * dc:128 * dc + 128],
                            xgT[:, ic, :], start=(ic == 0), stop=False))
                    mq = k.op("t", lambda t, dc=dc, p=p: t.matmul(
                        p["t"][:, 0:8], brows[0:1, 0, 128 * dc:128 * dc + 128],
                        ones_bf[0:1, 0:8], start=False, stop=True), inc=True)
                    p["wmark"] = mq
                    k.wait("v", "t", mq)
                    for half in range(2):
                        mq2 = k.op("v", lambda v, dc=dc, half=half, p=p: v.tensor_scalar(
                            BD[64 * half:64 * half + 64, dc,
                               16 * dc + 8 * half:16 * dc + 8 * half + 8],
                            p["t"][64 * half:64 * half + 64, 0:8],
                            SCALE, None, ALU.mult), inc=(half == 1))
                        p["readers"].append(("v", mq2))
                    p = pool.acquire()
                    for ic in range(8):
                        k.op("t", lambda t, dc=dc, ic=ic, p=p: t.matmul(
                            p["t"][:, 0:8], Wk[:, ic, 128 * dc:128 * dc + 128],
                            xgT[:, ic, :], start=(ic == 0), stop=False))
                    mk_ = k.op("t", lambda t, dc=dc, p=p: t.matmul(
                        p["t"][:, 0:8], brows[0:1, 1, 128 * dc:128 * dc + 128],
                        ones_bf[0:1, 0:8], start=False, stop=True), inc=True)
                    p["wmark"] = mk_
                    k.wait("v", "t", mk_)
                    mk2 = k.op("v", lambda v, dc=dc, p=p: v.tensor_copy(
                        KgT[:, dc, :], p["t"][:, 0:8]), inc=True)
                    p["readers"].append(("v", mk2))
                k.wait("t", "v", m_bd0)
                for oh in range(2):
                    p = pool.acquire()
                    for ic in range(8):
                        k.op("t", lambda t, oh=oh, ic=ic, p=p: t.matmul(
                            p["t"][0:8, 0:512], xgT[:, ic, :],
                            Wv[:, ic, 512 * oh:512 * oh + 512],
                            start=(ic == 0), stop=False))
                    mvg = k.op("t", lambda t, oh=oh, p=p: t.matmul(
                        p["t"][0:8, 0:512], ones_bf[0:1, 0:8],
                        brows[0:1, 2, 512 * oh:512 * oh + 512],
                        start=False, stop=True), inc=True)
                    p["wmark"] = mvg
                    k.wait("v", "t", mvg)
                    mvg2 = k.op("v", lambda v, oh=oh, p=p: v.tensor_copy(
                        Vg[:, 8 * oh:8 * oh + 8, 0:64],
                        p["t"][0:8, 0:512].rearrange("p (a d) -> p a d", a=8)), inc=True)
                    p["readers"].append(("v", mvg2))

                k.op("v", lambda v: v.memset(macc[:, :], 0.0))
                k.op("v", lambda v: v.memset(fs[:, 1:2], 0.0))
                k.op("v", lambda v: v.memset(fs[:, 0:1], -3.0e38))
                k.op("v", lambda v: v.drain())
                m_f0 = k.op("v", lambda v: v.nop(), inc=True)

                store_marks = []
                lns = {0: carry.pop("ln0")} if "ln0" in carry else \
                    {0: ln_chunk(src, 0, 0)}
                for c in range(nch):
                    slot = c % 2
                    mld, m_xnT = lns.pop(c)
                    glist = [g for g in range(NG) if gpos[g] // CH == c]

                    k.wait("s", "t", bufm.get("a_qkv_t", 0))
                    k.wait("v", "t", bufm.get("a_qkv_t", 0))
                    for dc in range(8):
                        p = proj_fm(Wq, 128 * dc, m_xnT, slot, bias=0)
                        k.wait("s", "t", p["wmark"])
                        ms = k.op("s", lambda s, dc=dc, p=p: s.activation(
                            QT[:, dc, :], p["t"][:, 0:CH], AF.Copy, scale=SCALE),
                            inc=True)
                        p["readers"].append(("s", ms))
                        p = proj_fm(Wk, 128 * dc, m_xnT, slot, bias=1)
                        k.wait("v", "t", p["wmark"])
                        mv2 = k.op("v", lambda v, dc=dc, p=p: v.tensor_copy(
                            KT[:, dc, :], p["t"][:, 0:CH]), inc=True)
                        p["readers"].append(("v", mv2))
                    for j in range(NT):
                        for oh in range(2):
                            p = pool.acquire()
                            k.wait("t", "v", m_xnT)
                            for ic in range(8):
                                k.op("t", lambda t, j=j, oh=oh, ic=ic, p=p,
                                     slot=slot: t.matmul(
                                    p["t"][:, 0:512],
                                    xnT[slot][:, ic, 128 * j:128 * j + 128],
                                    Wv[:, ic, 512 * oh:512 * oh + 512],
                                    start=(ic == 0), stop=False))
                            mvp = k.op("t", lambda t, oh=oh, p=p: t.matmul(
                                p["t"][:, 0:512], ones_bf[0:1, 0:128],
                                brows[0:1, 2, 512 * oh:512 * oh + 512],
                                start=False, stop=True), inc=True)
                            p["wmark"] = mvp
                            note_xnT_read(slot, mvp)
                            k.wait("v", "t", mvp)
                            mvv = k.op("v", lambda v, j=j, oh=oh, p=p: v.tensor_copy(
                                Vaug[:, j, 8 * oh:8 * oh + 8, 0:64],
                                p["t"][:, 0:512].rearrange("p (a d) -> p a d", a=8)),
                                inc=True)
                            p["readers"].append(("v", mvv))
                    mvo = k.op("v", lambda v: v.memset(Vaug[:, :, :, 64:65], 1.0),
                               inc=True)
                    m_qkv_v = k.cnt["v"]
                    m_qkv_s = k.cnt["s"]
                    if c + 1 < nch:
                        lns[c + 1] = ln_chunk(src, c + 1, (c + 1) % 2)

                    p = pool.acquire()
                    k.wait("t", "v", m_qkv_v)
                    for ic in range(8):
                        k.op("t", lambda t, ic=ic, p=p: t.matmul(
                            p["t"][:, 0:CH], BD[:, ic, :], KT[:, ic, :],
                            start=(ic == 0), stop=(ic == 7)), inc=(ic == 7))
                    p["wmark"] = k.cnt["t"]
                    k.wait("v", "t", p["wmark"])
                    k.wait("v", "s", bufm.get("a_strip_s", 0))
                    k.op("v", lambda v, p=p: v.tensor_reduce(
                        fs[:, 2:3], p["t"][:, 0:CH], AX.X, ALU.max))
                    k.op("v", lambda v: v.drain())
                    k.op("v", lambda v: v.tensor_tensor(
                        fs[:, 3:4], fs[:, 0:1], fs[:, 2:3], ALU.max))
                    k.op("v", lambda v: v.drain())
                    mneg = k.op("v", lambda v: v.tensor_scalar(
                        fs[:, 4:5], fs[:, 3:4], -1.0, None, ALU.mult), inc=True)
                    k.wait("s", "v", mneg)
                    k.op("s", lambda s: s.activation(
                        fs[:, 5:6], fs[:, 0:1], AF.Exp, bias=fs[:, 4:5]))
                    mstr = k.op("s", lambda s, p=p: s.activation(
                        strip_bf[:, 0:CH], p["t"][:, 0:CH], AF.Exp,
                        bias=fs[:, 4:5]), inc=True)
                    bufm["a_strip_s"] = mstr
                    p["readers"].append(("s", mstr))
                    k.wait("v", "s", mstr)
                    k.op("v", lambda v: v.tensor_reduce(
                        fs[:, 6:7], strip_bf[:, 0:CH], AX.X, ALU.add))
                    k.op("v", lambda v: v.drain())
                    k.op("v", lambda v: v.scalar_tensor_tensor(
                        fs[:, 1:2], fs[:, 1:2], fs[:, 5:6], fs[:, 6:7],
                        ALU.mult, ALU.add))
                    k.op("v", lambda v: v.drain())
                    mfl1 = k.op("v", lambda v: v.tensor_copy(fs[:, 0:1], fs[:, 3:4]),
                                inc=True)
                    k.wait("t", "s", mstr)
                    b = bpool.acquire()
                    for j in range(NT):
                        k.op("t", lambda t, j=j, b=b: t.matmul(
                            b["t"][:, 128 * j:128 * j + 128],
                            strip_bf[:, 128 * j:128 * j + 128], idn_bf[:],
                            start=True, stop=True, is_transpose=True), inc=(j == NT - 1))
                    b["wmark"] = k.cnt["t"]
                    bufm["a_stripT_t"] = b["wmark"]
                    k.wait("v", "t", b["wmark"])
                    mpg = k.op("v", lambda v, b=b: v.tensor_copy(
                        PgT[:, :, :], b["t"][:, 0:128 * NT].rearrange(
                            "p (a q) -> p a q", a=NT)), inc=True)
                    b["readers"].append(("v", mpg))
                    k.wait("t", "v", mpg)
                    k.wait("t", "v", mfl1)
                    for dh in range(2):
                        p = pool.acquire()
                        for j in range(NT):
                            k.op("t", lambda t, dh=dh, j=j, p=p: t.matmul(
                                p["t"][:, 0:512].rearrange("p (a d) -> p a d", a=8),
                                PgT[:, j, :],
                                Vaug[:, j, 8 * dh:8 * dh + 8, 0:64],
                                start=(j == 0), stop=(j == NT - 1)), inc=(j == NT - 1))
                        p["wmark"] = k.cnt["t"]
                        k.wait("v", "t", p["wmark"])
                        mfl = k.op("v", lambda v, dh=dh, p=p: v.scalar_tensor_tensor(
                            macc[:, 512 * dh:512 * dh + 512],
                            macc[:, 512 * dh:512 * dh + 512], fs[:, 5:6],
                            p["t"][:, 0:512], ALU.mult, ALU.add), inc=True)
                        p["readers"].append(("v", mfl))

                    def a_tail(pO, h, dc, base):
                        k.wait("v", "t", pO["wmark"])
                        k.wait("v", "t", bufm.get("a_tok_t", 0))
                        k.op("v", lambda v, pO=pO: v.reciprocal(
                            rec4[:, :], pO["t"][:, 0:65 * NT].rearrange(
                                "p (a b) -> p a b", b=65)[:, :, 64]))
                        k.op("v", lambda v: v.drain())
                        for qt in range(NT):
                            mdv = k.op("v", lambda v, qt=qt, pO=pO: v.tensor_scalar(
                                atok[:, qt, :], pO["t"][:, 65 * qt:65 * qt + 64],
                                rec4[:, qt:qt + 1], None, ALU.mult), inc=(qt == NT - 1))
                        pO["readers"].append(("v", mdv))
                        pool.release(pO)
                        k.wait("t", "v", mdv)
                        b = bpool.acquire()
                        for qt in range(NT):
                            k.op("t", lambda t, qt=qt, b=b: t.matmul(
                                b["t"][0:64, 128 * qt:128 * qt + 128],
                                atok[:, qt, :], idn_bf[:],
                                start=True, stop=True, is_transpose=True),
                                inc=(qt == NT - 1))
                        b["wmark"] = k.cnt["t"]
                        bufm["a_tok_t"] = b["wmark"]
                        k.wait("v", "t", b["wmark"])
                        k.wait("v", "t", bufm.get("catT_t", 0))
                        mct = k.op("v", lambda v, dc=dc, base=base, b=b: v.tensor_copy(
                            catT[base:base + 64, dc, :], b["t"][0:64, 0:128 * NT]),
                            inc=True)
                        b["readers"].append(("v", mct))

                    a_pending = None

                    def a_scores(h):
                        dc, base = h // 2, 64 * (h % 2)
                        pA = pool.acquire()
                        pG = pool.acquire()
                        k.wait("t", "s", m_qkv_s)
                        for qt in range(NT):
                            k.op("t", lambda t, qt=qt, pA=pA, dc=dc, base=base: t.matmul(
                                pA["t"][:, 128 * qt:128 * qt + 128],
                                KT[base:base + 64, dc, 128 * qt:128 * qt + 128],
                                QT[base:base + 64, dc, 128 * qt:128 * qt + 128],
                                start=True, stop=False, skip_group_check=True))
                            k.op("t", lambda t, qt=qt, pA=pA: t.matmul(
                                pA["t"][0:64, 128 * qt + 64:128 * qt + 128],
                                ones_bf[0:1, 0:64], neg_bf[0:1, 0:64],
                                start=False, stop=False, skip_group_check=True))
                            k.op("t", lambda t, qt=qt, pA=pA: t.matmul(
                                pA["t"][64:128, 128 * qt:128 * qt + 64],
                                ones_bf[0:1, 0:64], neg_bf[0:1, 0:64],
                                start=False, stop=(qt == NT - 1), skip_group_check=True),
                                inc=(qt == NT - 1))
                        mA = k.cnt["t"]
                        pA["wmark"] = mA
                        for qt in range(NT):
                            last = (qt == NT - 1) and not glist
                            k.op("t", lambda t, qt=qt, pG=pG, dc=dc, base=base, last=last:
                                 t.matmul(pG["t"][0:8, 128 * qt:128 * qt + 128],
                                          KgT[base:base + 64, dc, :],
                                          QT[base:base + 64, dc, 128 * qt:128 * qt + 128],
                                          start=(qt == 0), stop=last,
                                          skip_group_check=True), inc=last)
                        for gi, g in enumerate(glist):
                            off = gpos[g] - c * CH
                            blk0 = (off // 64) * 64
                            last = (gi == len(glist) - 1)
                            k.op("t", lambda t, g=g, blk0=blk0, pG=pG, last=last: t.matmul(
                                pG["t"][0:8, blk0:blk0 + 64],
                                oh8_bf[0:1, 8 * g:8 * g + 8], neg_bf[0:1, 0:64],
                                start=False, stop=last, skip_group_check=True), inc=last)
                        mG = k.cnt["t"]
                        pG["wmark"] = mG
                        return pA, pG, mG

                    def a_headfin(h, pA, pG, mG):
                        dc, base = h // 2, 64 * (h % 2)
                        nonlocal a_pending
                        k.wait("s", "t", mG)
                        k.wait("s", "t", bufm.get("a_PT_pv", 0))
                        msl = k.op("s", lambda s, pA=pA: s.activation(
                            PTl[:, :], pA["t"][:, 0:512], AF.Exp), inc=True)
                        pA["readers"].append(("s", msl))
                        msg = k.op("s", lambda s, pG=pG: s.activation(
                            PTg[:, :], pG["t"][0:8, 0:512], AF.Exp), inc=True)
                        pG["readers"].append(("s", msg))
                        pO = pool.acquire(hold=True)
                        k.wait("t", "s", msg)
                        for qt in range(NT):
                            k.op("t", lambda t, h=h, qt=qt, pO=pO: t.matmul(
                                pO["t"][:, 65 * qt:65 * qt + 65],
                                PTl[:, 128 * qt:128 * qt + 128], Vaug[:, qt, h, :],
                                start=(qt == 0), stop=False, skip_group_check=True))
                            k.op("t", lambda t, h=h, qt=qt, pO=pO: t.matmul(
                                pO["t"][:, 65 * qt:65 * qt + 65],
                                PTg[:, 128 * qt:128 * qt + 128], Vg[:, h, :],
                                start=False, stop=(qt == NT - 1), skip_group_check=True),
                                inc=(qt == NT - 1))
                        pO["wmark"] = k.cnt["t"]
                        bufm["a_PT_pv"] = pO["wmark"]
                        if a_pending is not None:
                            a_tail(*a_pending)
                        a_pending = (pO, h, dc, base)

                    sc_pending = None
                    for h in range(NH):
                        sc = a_scores(h)
                        if sc_pending is not None:
                            a_headfin(*sc_pending)
                        sc_pending = (h,) + sc
                    if sc_pending is not None:
                        a_headfin(*sc_pending)
                        sc_pending = None
                    if a_pending is not None:
                        a_tail(*a_pending)
                        a_pending = None
                    m_cat_v = k.cnt["v"]
                    bufm["a_qkv_t"] = k.cnt["t"]

                    for g in glist:
                        col = gpos[g] - c * CH
                        m_cat_v = k.op("v", lambda v, g=g, col=col: v.tensor_copy(
                            saved[:, :, g], catT[:, :, col]), inc=True)

                    store_marks.append(wo_resid_store(Wo, 3, dst, c, slot, m_cat_v))
                    bufm["catT_t"] = k.cnt["t"]

                k.wait("v", "t", bufm.get("a_stripT_t", 0))
                k.op("v", lambda v: v.reciprocal(fs[:, 7:8], fs[:, 1:2]))
                k.op("v", lambda v: v.drain())
                mnrm = k.op("v", lambda v: v.tensor_scalar(
                    strip_bf[:, :], macc[:, :], fs[:, 7:8], None, ALU.mult), inc=True)
                k.wait("t", "v", mnrm)
                b = bpool.acquire()
                for dcb in range(8):
                    k.op("t", lambda t, dcb=dcb, b=b: t.matmul(
                        b["t"][:, 128 * dcb:128 * dcb + 128],
                        strip_bf[:, 128 * dcb:128 * dcb + 128], idn_bf[:],
                        start=True, stop=True, is_transpose=True), inc=(dcb == 7))
                b["wmark"] = k.cnt["t"]
                k.wait("v", "t", b["wmark"])
                for dcb in range(8):
                    k.op("v", lambda v, dcb=dcb, b=b: v.tensor_copy(
                        cglob[0:64, dcb, :],
                        b["t"][0:64, 128 * dcb + 16 * dcb:128 * dcb + 16 * dcb + 8]))
                    mgl = k.op("v", lambda v, dcb=dcb, b=b: v.tensor_copy(
                        cglob[64:128, dcb, :],
                        b["t"][64:128, 128 * dcb + 16 * dcb + 8:128 * dcb + 16 * dcb + 16]),
                        inc=(dcb == 7))
                b["readers"].append(("v", k.cnt["v"]))
                k.op("v", lambda v: v.drain())
                mdel = k.op("v", lambda v: v.tensor_tensor(
                    cglob[:, :, :], cglob[:, :, :], saved[:, :, :], ALU.subtract),
                    inc=True)
                k.wait("t", "v", mdel)
                k.wait_dmas("y", store_marks)
                mg2 = [k.dma("y", grmw[g:g + 1, :], dst[gpos[g]:gpos[g] + 1, :])
                       for g in range(NG)]
                mfx = 0
                for eh in range(2):
                    p = pool.acquire()
                    for ic in range(8):
                        k.op("t", lambda t, eh=eh, ic=ic, p=p: t.matmul(
                            p["t"][0:8, 0:512], cglob[:, ic, :],
                            Wo[:, ic, 512 * eh:512 * eh + 512],
                            start=(ic == 0), stop=(ic == 7)), inc=(ic == 7))
                    p["wmark"] = k.cnt["t"]
                    k.wait("v", "t", p["wmark"])
                    k.wait_dmas("v", mg2)
                    mfx = k.op("v", lambda v, eh=eh, p=p: v.tensor_tensor(
                        grmw[:, 512 * eh:512 * eh + 512],
                        grmw[:, 512 * eh:512 * eh + 512],
                        p["t"][0:8, 0:512], ALU.add), inc=True)
                    p["readers"].append(("v", mfx))
                k.wait("y", "v", mfx)
                for g in range(NG):
                    k.dma("y", dst[gpos[g]:gpos[g] + 1, :], grmw[g:g + 1, :])
                if prologue:
                    bufm["src_stores"] = k.snap("y")
                    prologue()
            k.barrier()

        def ffn(src, dst, tag, prologue=None):
            with contextlib.ExitStack() as _sst:
                w1s = _sst.enter_context(nc.sbuf_tensor(f"f_w1s{tag}", [128, 2, 8, HID], BF16))
                w3s = _sst.enter_context(nc.sbuf_tensor(f"f_w3s{tag}", [128, 2, 8, HID], BF16))
                w2s = _sst.enter_context(nc.sbuf_tensor(f"f_w2s{tag}", [128, 2, 4, 512], BF16))
                rws = _sst.enter_context(nc.sbuf_tensor(f"f_rws{tag}", [128, 8, 8], BF16))
                H1G = _sst.enter_context(nc.sbuf_tensor(f"f_H1G{tag}", [128, NE_ALL, 4, CH], BF16))
                G = _sst.enter_context(nc.sbuf_tensor(f"f_G{tag}", [128, NT, 32], F32))
                Gbf = _sst.enter_context(nc.sbuf_tensor(f"f_Gbf{tag}", [128, NT, 8], BF16))
                GT = _sst.enter_context(nc.sbuf_tensor(f"f_GT{tag}", [8, CH], BF16))
                sA = _sst.enter_context(nc.sbuf_tensor(f"f_sA{tag}", [128, 512], BF16))
                t1 = _sst.enter_context(nc.sbuf_tensor(f"f_t1{tag}", [128, 512], BF16))
                mrw = k.dma("g", rws[:, :, :], f_rw.rearrange("(a p) e -> p a e", p=128))
                k.wait_dma("t", mrw)
                k.wait_dmas("t", m_prestage)
                k.wait_dmas("t", m_consts_g)
                k.wait("t", "v", m_consts_v)

                lns = {0: carry.pop("ln0")} if "ln0" in carry else \
                    {0: ln_chunk(src, 0, 0)}
                wmk = {}
                w2mk = {}
                for c in range(nch):
                    slot = c % 2
                    mld, m_xnT = lns.pop(c)

                    k.wait("v", "s", bufm.get("f_gate_s", 0))
                    k.wait("v", "t", bufm.get("f_GT_t", 0))
                    for j in range(NT):
                        p = pool.acquire()
                        k.wait("t", "v", m_xnT)
                        for ic in range(8):
                            k.op("t", lambda t, j=j, ic=ic, p=p, slot=slot: t.matmul(
                                p["t"][:, 0:8],
                                xnT[slot][:, ic, 128 * j:128 * j + 128],
                                rws[:, ic, :], start=(ic == 0), stop=False))
                        mro = k.op("t", lambda t, p=p: t.matmul(
                            p["t"][:, 0:8], ones_bf[0:1, 0:128], brows[0:1, 4, 0:8],
                            start=False, stop=True), inc=True)
                        p["wmark"] = mro
                        note_xnT_read(slot, mro)
                        k.wait("v", "t", mro)
                        k.op("v", lambda v, j=j, p=p: v.tensor_reduce(
                            G[:, j, 24:25], p["t"][:, 0:8], AX.X, ALU.max))
                        k.op("v", lambda v: v.drain())
                        k.op("v", lambda v, j=j, p=p: v.tensor_scalar(
                            G[:, j, 0:8], p["t"][:, 0:8], G[:, j, 24:25], None, ALU.is_ge))
                        k.op("v", lambda v: v.drain())
                        k.op("v", lambda v, j=j: v.tensor_reduce(
                            G[:, j, 25:26], G[:, j, 0:8], AX.X, ALU.add))
                        k.op("v", lambda v, j=j, p=p: v.scalar_tensor_tensor(
                            G[:, j, 8:16], G[:, j, 0:8], NEG, p["t"][:, 0:8],
                            ALU.mult, ALU.add))
                        k.op("v", lambda v: v.drain())
                        k.op("v", lambda v, j=j: v.tensor_reduce(
                            G[:, j, 26:27], G[:, j, 8:16], AX.X, ALU.max))
                        k.op("v", lambda v, j=j: v.tensor_scalar(
                            G[:, j, 27:28], G[:, j, 25:26], 2.0, None, ALU.is_ge))
                        k.op("v", lambda v: v.drain())
                        k.op("v", lambda v, j=j: v.tensor_tensor(
                            G[:, j, 28:29], G[:, j, 24:25], G[:, j, 26:27], ALU.subtract))
                        k.op("v", lambda v: v.drain())
                        k.op("v", lambda v, j=j: v.tensor_tensor(
                            G[:, j, 28:29], G[:, j, 28:29], G[:, j, 27:28], ALU.mult))
                        k.op("v", lambda v: v.drain())
                        k.op("v", lambda v, j=j: v.tensor_tensor(
                            G[:, j, 28:29], G[:, j, 28:29], G[:, j, 26:27], ALU.add))
                        mneg1 = k.op("v", lambda v, j=j: v.tensor_scalar(
                            G[:, j, 29:30], G[:, j, 24:25], -1.0, None, ALU.mult),
                            inc=True)
                        k.wait("s", "v", mneg1)
                        mz = k.op("s", lambda s, j=j, p=p: s.activation(
                            G[:, j, 16:24], p["t"][:, 0:8], AF.Exp,
                            bias=G[:, j, 29:30]), inc=True)
                        bufm["f_gate_s"] = mz
                        p["readers"].append(("s", mz))
                        k.wait("v", "s", mz)
                        mke = k.op("v", lambda v, j=j, p=p: v.tensor_scalar(
                            G[:, j, 0:8], p["t"][:, 0:8], G[:, j, 28:29], None,
                            ALU.is_ge), inc=True)
                        p["readers"].append(("v", mke))
                        k.op("v", lambda v: v.drain())
                        k.op("v", lambda v, j=j: v.tensor_tensor(
                            G[:, j, 8:16], G[:, j, 16:24], G[:, j, 0:8], ALU.mult))
                        k.op("v", lambda v: v.drain())
                        k.op("v", lambda v, j=j: v.tensor_reduce(
                            G[:, j, 30:31], G[:, j, 8:16], AX.X, ALU.add))
                        k.op("v", lambda v: v.drain())
                        k.op("v", lambda v, j=j: v.reciprocal(
                            G[:, j, 31:32], G[:, j, 30:31]))
                        k.op("v", lambda v: v.drain())
                        k.op("v", lambda v, j=j: v.tensor_scalar(
                            Gbf[:, j, :], G[:, j, 8:16], G[:, j, 31:32], None,
                            ALU.mult), inc=(j == NT - 1))
                    m_gates = k.cnt["v"]
                    k.wait("t", "v", m_gates)
                    b = bpool.acquire()
                    for j in range(NT):
                        k.op("t", lambda t, j=j, b=b: t.matmul(
                            b["t"][0:8, 128 * j:128 * j + 128], Gbf[:, j, :],
                            idn_bf[:], start=True, stop=True, is_transpose=True),
                            inc=(j == NT - 1))
                    b["wmark"] = k.cnt["t"]
                    bufm["f_GT_t"] = b["wmark"]
                    k.wait("v", "t", b["wmark"])
                    mgt = k.op("v", lambda v, b=b: v.tensor_copy(
                        GT[:, :], b["t"][0:8, 0:128 * NT]), inc=True)
                    b["readers"].append(("v", mgt))
                    if c + 1 < nch:
                        lns[c + 1] = ln_chunk(src, c + 1, (c + 1) % 2)

                    EORD = [NEXP, NEXP + 1] + list(range(NEXP))
                    wmk = {}
                    for pos in range(2):
                        k.wait("g", "t", bufm.get(("f_w13_t", pos % 2), 0))
                        wmk[pos] = [
                            k.dma("g", w1s[:, pos % 2, :, :], ffw13[EORD[pos], 0]),
                            k.dma("g", w3s[:, pos % 2, :, :], ffw13[EORD[pos], 1])]
                    mhe = {}
                    for pos in range(NE_ALL):
                        e = EORD[pos]
                        sl = pos % 2
                        routed = e < NEXP
                        pGb = None
                        if routed:
                            pGb = pool.acquire(hold=True)
                            k.wait("t", "v", mgt)
                            mgb = k.op("t", lambda t, e=e, pGb=pGb: t.matmul(
                                pGb["t"][:, 0:CH], rsel_bf[:, e, :], GT[:, :],
                                start=True, stop=True), inc=True)
                            pGb["wmark"] = mgb
                        k.wait_dmas("t", wmk[pos])
                        lastw_t = 0
                        for hc in range(4):
                            pA_ = pool.acquire()
                            k.wait("t", "v", m_xnT)
                            for ic in range(8):
                                k.op("t", lambda t, sl=sl, hc=hc, ic=ic, pA_=pA_,
                                     slot=slot: t.matmul(
                                    pA_["t"][:, 0:CH],
                                    w1s[:, sl, ic, 128 * hc:128 * hc + 128],
                                    xnT[slot][:, ic, :], start=(ic == 0), stop=(ic == 7)),
                                    inc=(ic == 7))
                            pA_["wmark"] = k.cnt["t"]
                            pB_ = pool.acquire()
                            for ic in range(8):
                                k.op("t", lambda t, sl=sl, hc=hc, ic=ic, pB_=pB_,
                                     slot=slot: t.matmul(
                                    pB_["t"][:, 0:CH],
                                    w3s[:, sl, ic, 128 * hc:128 * hc + 128],
                                    xnT[slot][:, ic, :], start=(ic == 0), stop=(ic == 7)),
                                    inc=(ic == 7))
                            pB_["wmark"] = k.cnt["t"]
                            lastw_t = pB_["wmark"]
                            note_xnT_read(slot, lastw_t)
                            k.wait("s", "t", pA_["wmark"])
                            k.wait("s", "v", bufm.get("f_sA_v", 0))
                            msa = k.op("s", lambda s, pA_=pA_: s.activation(
                                sA[:, :], pA_["t"][:, 0:CH], AF.Silu), inc=True)
                            pA_["readers"].append(("s", msa))
                            k.wait("v", "s", msa)
                            k.wait("v", "t", pB_["wmark"])
                            k.wait("v", "t", bufm.get("f_H1G_t", 0))
                            k.op("v", lambda v: v.drain())
                            if routed:
                                mt1 = k.op("v", lambda v, pB_=pB_: v.scalar_tensor_tensor(
                                    t1[:, :], pB_["t"][:, 0:CH], 1.0, sA[:, :],
                                    ALU.mult, ALU.mult), inc=True)
                                pB_["readers"].append(("v", mt1))
                                bufm["f_sA_v"] = mt1
                                k.op("v", lambda v: v.drain())
                                mh = k.op("v", lambda v, e=e, hc=hc, pGb=pGb: v.tensor_tensor(
                                    H1G[:, e, hc, :], t1[:, :], pGb["t"][:, 0:CH],
                                    ALU.mult), inc=True)
                                pGb["readers"].append(("v", mh))
                            else:
                                mh = k.op("v", lambda v, e=e, hc=hc, pB_=pB_:
                                          v.scalar_tensor_tensor(
                                              H1G[:, e, hc, :], pB_["t"][:, 0:CH], 1.0,
                                              sA[:, :], ALU.mult, ALU.mult), inc=True)
                                pB_["readers"].append(("v", mh))
                                bufm["f_sA_v"] = mh
                        bufm[("f_w13_t", sl)] = lastw_t
                        mhe[e] = k.cnt["v"]
                        if pGb is not None:
                            pool.release(pGb)
                        if pos + 2 < NE_ALL:
                            np_ = pos + 2
                            k.wait("g", "t", bufm.get(("f_w13_t", np_ % 2), 0))
                            wmk[np_] = [
                                k.dma("g", w1s[:, np_ % 2, :, :], ffw13[EORD[np_], 0]),
                                k.dma("g", w3s[:, np_ % 2, :, :], ffw13[EORD[np_], 1])]
                    m_H1G = k.cnt["v"]

                    w2mk = {}
                    for dh in range(2):
                        for pos in range(2):
                            k.wait("g", "t", bufm.get(("f_w2_t", pos % 2), 0))
                            w2mk[pos] = k.dma("g", w2s[:, pos % 2, :, :],
                                              ffw2[EORD[pos], dh])
                        accs = [pool.acquire(hold=True) for _ in range(NT)]
                        m_B = 0
                        for pos in range(NE_ALL):
                            e = EORD[pos]
                            sl = pos % 2
                            k.wait("t", "v", mhe[e])
                            k.wait_dma("t", w2mk[pos])
                            for j in range(NT):
                                for hc in range(4):
                                    k.op("t", lambda t, e=e, sl=sl, j=j, hc=hc,
                                         pos=pos, a_=accs[j]: t.matmul(
                                             a_["t"][:, 0:512],
                                             H1G[:, e, hc, 128 * j:128 * j + 128],
                                             w2s[:, sl, hc, :],
                                             start=(pos == 0 and hc == 0),
                                             stop=(pos == NE_ALL - 1 and hc == 3),
                                             skip_group_check=True),
                                         inc=(j == NT - 1 and hc == 3))
                            bufm[("f_w2_t", sl)] = k.cnt["t"]
                            m_B = k.cnt["t"]
                            if pos + 2 < NE_ALL:
                                np_ = pos + 2
                                k.wait("g", "t", bufm.get(("f_w2_t", np_ % 2), 0))
                                w2mk[np_] = k.dma("g", w2s[:, np_ % 2, :, :],
                                                  ffw2[EORD[np_], dh])
                        for j in range(NT):
                            accs[j]["wmark"] = m_B
                            k.wait("v", "t", m_B)
                            k.wait_dma("v", bufm.get("xout_store"))
                            mev = k.op("v", lambda v, j=j, dh=dh, a_=accs[j],
                                       slot=slot: v.scalar_tensor_tensor(
                                           xout[:, j, 512 * dh:512 * dh + 512],
                                           a_["t"][:, 0:512], 1.0,
                                           xf[slot][:, j, 512 * dh:512 * dh + 512],
                                           ALU.mult, ALU.add), inc=True)
                            accs[j]["readers"].append(("v", mev))
                            pool.release(accs[j])
                    bufm["f_H1G_t"] = k.cnt["t"]
                    bufm[("xf_v", slot)] = k.cnt["v"]
                    last_v = k.cnt["v"]
                    k.wait("y", "v", last_v)
                    mst = k.dma("y", dst[c * CH:(c + 1) * CH, :]
                                .rearrange("(j p) d -> p j d", p=128), xout[:, :, :])
                    bufm["xout_store"] = mst
                    if prologue and c == nch - 1:
                        bufm["src_stores"] = k.snap("y")
                        prologue()
            k.barrier()

        def attn_b(src, dst, prologue=None):
            with contextlib.ExitStack() as _sst:
                wqT = _sst.enter_context(nc.sbuf_tensor("b_wqT", [128, 8, D], BF16))
                wkvK = _sst.enter_context(nc.sbuf_tensor("b_wkvK", [128, 8, 256], BF16))
                wkvV = _sst.enter_context(nc.sbuf_tensor("b_wkvV", [128, 8, 256], BF16))
                woT = _sst.enter_context(nc.sbuf_tensor("b_woT", [128, 8, D], BF16))
                Kz = _sst.enter_context(nc.sbuf_tensor("b_Kz", [128, 4, S], BF16))
                if 2 * S >= 8 * D:
                    wnat = Kz[:, 0:2, :].rearrange("p a (b d) -> p (a b) d",
                                                   b=S // D)[:, 0:8, :]
                else:
                    wnat = _sst.enter_context(
                        nc.sbuf_tensor("b_wnat", [128, 8, D], BF16))
                Vb = _sst.enter_context(nc.sbuf_tensor("b_Vaug", [128, NKT, 4, 65], BF16))
                QTb = _sst.enter_context(nc.sbuf_tensor("b_QT", [128, 8, CH], BF16))
                PT0 = _sst.enter_context(nc.sbuf_tensor("b_PT0", [128, 512], BF16))
                PT1 = _sst.enter_context(nc.sbuf_tensor("b_PT1", [128, 512], BF16))
                PT2 = _sst.enter_context(nc.sbuf_tensor("b_PT2", [128, 512], BF16))
                PT3 = _sst.enter_context(nc.sbuf_tensor("b_PT3", [128, 512], BF16))
                btok = _sst.enter_context(nc.sbuf_tensor("b_tok", [128, 4, 64], BF16))
                brec = _sst.enter_context(nc.sbuf_tensor("b_rec", [128, 4], F32))
                PTs = [PT0, PT1, PT2, PT3]

                def load_transpose(nat_src, n_oc, evac_plan, region=None,
                                   preloaded=None):
                    region = wnat if region is None else region
                    if preloaded is None:
                        k.wait("g", "t", k.cnt["t"])
                        mldw = k.dma("g", region[:, 0:n_oc, :], nat_src)
                    else:
                        mldw = preloaded
                    k.wait_dma("t", mldw)
                    wnat_ = region
                    for a in range(n_oc):
                        for icb in range(8):
                            b = bpool.acquire()
                            k.op("t", lambda t, a=a, icb=icb, b=b, wnat_=wnat_:
                                 t.matmul(
                                     b["t"][:, 0:128],
                                     wnat_[:, a, 128 * icb:128 * icb + 128],
                                     idn_bf[:], start=True, stop=True,
                                     is_transpose=True), inc=True)
                            b["wmark"] = k.cnt["t"]
                            k.wait("v", "t", b["wmark"])
                            for dst_ap, c0, c1 in evac_plan(a, icb):
                                mev = k.op("v", lambda v, dst_ap=dst_ap, c0=c0, c1=c1,
                                           b=b: v.tensor_copy(dst_ap, b["t"][:, c0:c1]),
                                           inc=True)
                                b["readers"].append(("v", mev))
                    return k.cnt["v"]

                def wq_plan(a, icb):
                    h0, h1 = 2 * a, 2 * a + 1
                    return [(wqT[:, icb, 64 * c64(h0):64 * c64(h0) + 64], 0, 64),
                            (wqT[:, icb, 64 * c64(h1):64 * c64(h1) + 64], 64, 128)]
                wkv_region = Kz[:, 2:3, :].rearrange("p a (b d) -> p (a b) d",
                                                      b=S // D)[:, 0:4, :] \
                    if S >= 4 * D else None
                mq_pre = k.dma("g", wnat[:, 0:8, :],
                               ab_Wq.rearrange("(a p) i -> p a i", p=128))
                mkv_pre = (k.dma("g", wkv_region[:, 0:4, :],
                                 ab_Wkv.rearrange("(a p) i -> p a i", p=128))
                           if wkv_region is not None else None)
                mwq = load_transpose(None, 8, wq_plan, preloaded=mq_pre)

                def wkv_plan(a, icb):
                    return [(wkvK[:, icb, 64 * a:64 * a + 64], 0, 64),
                            (wkvV[:, icb, 64 * a:64 * a + 64], 64, 128)]
                if wkv_region is not None:
                    mwkv = load_transpose(None, 4, wkv_plan, region=wkv_region,
                                          preloaded=mkv_pre)
                else:
                    mwkv = load_transpose(ab_Wkv.rearrange("(a p) i -> p a i", p=128),
                                          4, wkv_plan)

                def wo_plan(a, icb):
                    return [(woT[:, icb, 128 * a:128 * a + 128], 0, 128)]
                mwo = load_transpose(ab_Wo.rearrange("(a p) i -> p a i", p=128), 8, wo_plan)

                k.op("v", lambda v: v.memset(Vb[:, :, :, 64:65], 1.0))
                for g_ in range(4):
                    lo = 64 * (1 - (g_ % 2))
                    k.op("v", lambda v, g_=g_, lo=lo: v.memset(
                        Kz[lo:lo + 64, g_, :], 0.0))
                k.op("v", lambda v: v.drain())
                m_vb1 = k.op("v", lambda v: v.nop(), inc=True)
                k.wait_dma("v", m_al)

                lns = {0: carry.pop("ln0")} if "ln0" in carry else \
                    {0: ln_chunk(src, 0, 0)}
                for c in range(nch):
                    slot = c % 2
                    mld, m_xnT = lns.pop(c)
                    k.wait("s", "t", bufm.get("b_QT_t", 0))
                    for s_ in range(8):
                        p = proj_fm(wqT, 128 * s_, m_xnT, slot, bias=None)
                        k.wait("s", "t", p["wmark"])
                        msq = k.op("s", lambda s, s_=s_, p=p: s.activation(
                            QTb[:, s_, :], p["t"][:, 0:CH], AF.Copy, scale=SCALE),
                            inc=True)
                        p["readers"].append(("s", msq))
                    for pr in range(2):
                        p = proj_fm(wkvK, 128 * pr, m_xnT, slot, bias=None)
                        k.wait("v", "t", p["wmark"])
                        k.wait("v", "v", m_vb1)
                        k.op("v", lambda v, pr=pr, c=c, p=p: v.tensor_copy(
                            Kz[0:64, 2 * pr, c * CH:(c + 1) * CH],
                            p["t"][0:64, 0:CH]))
                        mkv = k.op("v", lambda v, pr=pr, c=c, p=p: v.tensor_copy(
                            Kz[64:128, 2 * pr + 1, c * CH:(c + 1) * CH],
                            p["t"][64:128, 0:CH]), inc=True)
                        p["readers"].append(("v", mkv))
                    for j in range(NT):
                        p = pool.acquire()
                        k.wait("t", "v", m_xnT)
                        for ic in range(8):
                            k.op("t", lambda t, j=j, ic=ic, p=p, slot=slot: t.matmul(
                                p["t"][:, 0:256],
                                xnT[slot][:, ic, 128 * j:128 * j + 128],
                                wkvV[:, ic, :], start=(ic == 0), stop=(ic == 7)),
                                inc=(ic == 7))
                        note_xnT_read(slot, k.cnt["t"])
                        p["wmark"] = k.cnt["t"]
                        k.wait("v", "t", p["wmark"])
                        k.wait("v", "v", m_vb1)
                        mkv = k.op("v", lambda v, j=j, c=c, p=p: v.tensor_copy(
                            Vb[:, NT * c + j, :, 0:64],
                            p["t"][:, 0:256].rearrange("p (a d) -> p a d", a=4)),
                            inc=True)
                        p["readers"].append(("v", mkv))
                    m_kv = k.cnt["v"]
                    m_q = k.cnt["s"]
                    if c + 1 < nch:
                        lns[c + 1] = ln_chunk(src, c + 1, (c + 1) % 2)

                    def b_tail(pO, h, qci, qc, nqt, qw):
                        k.wait("v", "t", pO["wmark"])
                        k.wait("v", "t", bufm.get("b_tok_t", 0))
                        k.op("v", lambda v, nqt=nqt, pO=pO: v.reciprocal(
                            brec[:, 0:nqt], pO["t"][:, 0:65 * nqt].rearrange(
                                "p (a b) -> p a b", b=65)[:, :, 64]))
                        k.op("v", lambda v: v.drain())
                        for qt in range(nqt):
                            mdv = k.op("v", lambda v, qt=qt, qw=qw, pO=pO:
                                       v.tensor_scalar(
                                           btok[0:qw, qt, :],
                                           pO["t"][0:qw, 65 * qt:65 * qt + 64],
                                           brec[0:qw, qt:qt + 1], None, ALU.mult),
                                       inc=(qt == nqt - 1))
                        pO["readers"].append(("v", mdv))
                        pool.release(pO)
                        k.wait("t", "v", mdv)
                        b = bpool.acquire()
                        for qt in range(nqt):
                            k.op("t", lambda t, qt=qt, qw=qw, b=b: t.matmul(
                                b["t"][0:64, 128 * qt:128 * qt + qw],
                                btok[0:qw, qt, :], idn_bf[0:qw, 0:qw],
                                start=True, stop=True, is_transpose=True),
                                inc=(qt == nqt - 1))
                        b["wmark"] = k.cnt["t"]
                        bufm["b_tok_t"] = b["wmark"]
                        k.wait("v", "t", b["wmark"])
                        k.wait("v", "t", bufm.get("catT_t", 0))
                        mct = k.op("v", lambda v, h=h, qci=qci, qc=qc, nqt=nqt,
                                   qw=qw, b=b: v.tensor_copy(
                                       catT[64 * (h % 2):64 * (h % 2) + 64, h // 2,
                                            qci * qc:qci * qc + qc].rearrange(
                                           "p (a q) -> p a q", a=nqt),
                                       b["t"][0:64, 0:128 * nqt].rearrange(
                                           "p (a q) -> p a q", a=nqt)[:, :, 0:qw]),
                                   inc=True)
                        b["readers"].append(("v", mct))

                    pending = None
                    for h in range(NH):
                        g = h // 4
                        s_, hi = b_slot(h)
                        base = 64 * hi
                        slope = 2.0 ** (-(h + 1) / 2.0)
                        qc = qc_size(h)
                        nqt = max(1, qc // 128)
                        qw = min(qc, 128)
                        for qci in range(CH // qc):
                            q0 = c * CH + qci * qc
                            kts = [kt for kt in range((q0 + qc + 127) // 128)
                                   if kt_keep(slope, q0, qc, kt)]
                            pO = pool.acquire(hold=True)
                            bank_started = [False]

                            def b_scores(ki, kt, q0=q0, qci=qci, qc=qc, g=g, s_=s_,
                                         base=base, h=h, qw=qw):
                                j0 = 128 * max(0, (128 * kt - q0 - qw + 128) // 128)
                                pS = pool.acquire()
                                k.wait("t", "v", m_kv)
                                k.wait("t", "s", m_q)
                                mS = k.op("t", lambda t, g=g, s_=s_, kt=kt, j0=j0,
                                          q0=q0, qci=qci, qc=qc, pS=pS: t.matmul(
                                              pS["t"][:, j0:qc],
                                              Kz[:, g, 128 * kt:128 * kt + 128],
                                              QTb[:, s_,
                                                  qci * qc + j0:qci * qc + qc],
                                              start=True, stop=True,
                                              skip_group_check=True), inc=True)
                                pS["wmark"] = mS
                                if 128 * kt + 127 >= q0 + j0:
                                    k.wait("v", "t", mS)
                                    mtri = k.op("v", lambda v, kt=kt, q0=q0, qc=qc,
                                                j0=j0, pS=pS: v.tensor_tensor(
                                                    pS["t"][:, j0:qc],
                                                    pS["t"][:, j0:qc],
                                                    tri_bf[:, 512 + q0 - 128 * kt + j0:
                                                           512 + q0 - 128 * kt + qc],
                                                    ALU.add), inc=True)
                                    pS["readers"].append(("v", mtri))
                                    k.wait("s", "v", mtri)
                                k.wait("s", "t", mS)
                                k.wait("s", "t", bufm.get(("b_PT_t", ki % 4), 0))
                                col = al_cols[(h, 128 * kt - q0 - qc // 2)]
                                msx = k.op("s", lambda s, qc=qc, col=col, ki=ki,
                                           j0=j0, pS=pS: s.activation(
                                               PTs[ki % 4][:, j0:qc],
                                               pS["t"][:, j0:qc],
                                               AF.Exp, bias=alibi_sb[:, col:col + 1]),
                                           inc=True)
                                pS["readers"].append(("s", msx))
                                return msx

                            def b_pv(ki, kt, msx, q0=q0, qc=qc, g=g, qw=qw, nqt=nqt,
                                     pO=pO):
                                k.wait("t", "s", msx)
                                qts = [qt for qt in range(nqt)
                                       if 128 * kt <= q0 + 128 * qt + qw - 1]
                                for qi, qt in enumerate(qts):
                                    st = not bank_started[0]
                                    bank_started[0] = True
                                    k.op("t", lambda t, qt=qt, kt=kt, ki=ki, g=g,
                                         qw=qw, st=st, pO=pO: t.matmul(
                                             pO["t"][0:qw, 65 * qt:65 * qt + 65],
                                             PTs[ki % 4][:, 128 * qt:128 * qt + qw],
                                             Vb[:, kt, g, :],
                                             start=st, stop=False,
                                             skip_group_check=True),
                                         inc=(qi == len(qts) - 1))
                                bufm[("b_PT_t", ki % 4)] = k.cnt["t"]

                            from collections import deque as _dq
                            pend = _dq()
                            for ki, kt in enumerate(kts):
                                msx = b_scores(ki, kt)
                                pend.append((ki, kt, msx))
                                if len(pend) > 3:
                                    b_pv(*pend.popleft())
                            while pend:
                                b_pv(*pend.popleft())
                            pO["wmark"] = k.cnt["t"]
                            if pending is not None:
                                b_tail(*pending)
                            pending = (pO, h, qci, qc, nqt, qw)
                    if pending is not None:
                        b_tail(*pending)
                        pending = None
                    m_cat_v = k.cnt["v"]
                    bufm["b_QT_t"] = k.cnt["t"]
                    if not os.environ.get("MK_NOWO"):
                        wo_resid_store(woT, None, dst, c, slot, m_cat_v)
                    bufm["catT_t"] = k.cnt["t"]
                    if prologue and c == nch - 1:
                        bufm["src_stores"] = k.snap("y")
                        prologue()
            k.barrier()

        srcs = {0: x_in, 1: Xa, 2: Xb, 3: Xa}
        dsts = {0: Xa, 1: Xb, 2: Xa, 3: y_out}
        order = [s_ for s_ in (0, 1, 2, 3) if s_ in subset]
        carry = {}
        plan = []
        for idx, s_ in enumerate(order):
            src = x_in if idx == 0 else srcs[s_]
            dst = y_out if idx == len(order) - 1 else dsts[s_]
            plan.append((s_, src, dst))
        for idx, (s_, src, dst) in enumerate(plan):
            if idx + 1 < len(plan):
                nxt_src = plan[idx + 1][1]

                def prologue(nxt_src=nxt_src):
                    carry["ln0"] = ln_chunk(nxt_src, 0, 0)
            else:
                prologue = None
            if s_ == 0:
                attn_a(src, dst, prologue)
            elif s_ in (1, 3):
                ffn(src, dst, s_, prologue)
            else:
                attn_b(src, dst, prologue)

        for q in "ygvs":
            for lane, cntv in enumerate(k.dcnt[q]):
                k.wait_dma(q, (q, lane, cntv))
        k.flush_all()

    return nc


def build_inputs(x_core, w, S, CH):
    hc = host_consts(S, CH)
    f32 = np.float32
    return {
        "x": np.ascontiguousarray(x_core, f32),
        "aa_Wq": w["L0_attn_a_Wq"], "aa_bq": w["L0_attn_a_bq"],
        "aa_Wk": w["L0_attn_a_Wk"], "aa_bk": w["L0_attn_a_bk"],
        "aa_Wv": w["L0_attn_a_Wv"], "aa_bv": w["L0_attn_a_bv"],
        "aa_Wo": w["L0_attn_a_Wo"], "aa_bo": w["L0_attn_a_bo"],
        "ab_Wq": w["L0_attn_b_Wq"], "ab_Wkv": w["L0_attn_b_Wkv"],
        "ab_Wo": w["L0_attn_b_Wo"],
        "f_rw": w["L0_ffn_router_w"], "f_rb": w["L0_ffn_router_b"],
        "f_w1": w["L0_ffn_routed_w1"], "f_w3": w["L0_ffn_routed_w3"],
        "f_w2": w["L0_ffn_routed_w2"],
        "f_s1": w["L0_ffn_shared_w1"], "f_s3": w["L0_ffn_shared_w3"],
        "f_s2": w["L0_ffn_shared_w2"],
        "idn": hc["idn"], "tri": hc["tri"], "oh8": hc["oh8"],
        "rowsel": hc["rowsel"], "alibi": hc["alibi"],
    }


_S, _CH = 4096, 512
_NCORES = 8


def kernel(**inputs):
    x = np.ascontiguousarray(inputs["L0_attn_a_x"], np.float32)
    hc = host_consts(_S, _CH)
    shared = {
        "aa_Wq": inputs["L0_attn_a_Wq"], "aa_bq": inputs["L0_attn_a_bq"],
        "aa_Wk": inputs["L0_attn_a_Wk"], "aa_bk": inputs["L0_attn_a_bk"],
        "aa_Wv": inputs["L0_attn_a_Wv"], "aa_bv": inputs["L0_attn_a_bv"],
        "aa_Wo": inputs["L0_attn_a_Wo"], "aa_bo": inputs["L0_attn_a_bo"],
        "ab_Wq": inputs["L0_attn_b_Wq"], "ab_Wkv": inputs["L0_attn_b_Wkv"],
        "ab_Wo": inputs["L0_attn_b_Wo"],
        "f_rw": inputs["L0_ffn_router_w"], "f_rb": inputs["L0_ffn_router_b"],
        "f_w1": inputs["L0_ffn_routed_w1"], "f_w3": inputs["L0_ffn_routed_w3"],
        "f_w2": inputs["L0_ffn_routed_w2"],
        "f_s1": inputs["L0_ffn_shared_w1"], "f_s3": inputs["L0_ffn_shared_w3"],
        "f_s2": inputs["L0_ffn_shared_w2"],
        "idn": hc["idn"], "tri": hc["tri"], "oh8": hc["oh8"],
        "rowsel": hc["rowsel"], "alibi": hc["alibi"],
    }
    shared = {k_: np.ascontiguousarray(v_, np.float32) for k_, v_ in shared.items()}
    nc = build_core(S=_S, CH=_CH, subset=(0, 1, 2, 3))
    in_maps = [dict(shared, x=x[b]) for b in range(_NCORES)]
    res = run_bass_kernel_spmd(nc, in_maps, core_ids=list(range(_NCORES)))
    return np.stack([res.results[b]["y"] for b in range(_NCORES)], axis=0)


# version 20
# speedup vs baseline: 1.4158x; 1.4158x over previous
import math
import numpy as np
import ml_dtypes
import contextlib
import os
import concourse.bass as bass
import concourse.mybir as mybir
from concourse.bass_utils import run_bass_kernel_spmd

F32 = mybir.dt.float32
BF16 = mybir.dt.bfloat16
AF = mybir.ActivationFunctionType
ALU = mybir.AluOpType
AX = mybir.AxisListType

NEG = -1.0e30
EPS = 1.0e-5


def kt_keep(slope, q0, qc, kt):
    thr = 110.0 - slope * (qc / 2.0)
    return slope * (q0 - (128 * kt + 127)) <= max(thr, 0.0)

def b_slot(h):
    if h < 4:
        return h, 0
    if h < 8:
        return h - 4, 1
    if h < 12:
        return h - 4, 0
    return h - 8, 1


def c64(h):
    s, hi = b_slot(h)
    return 2 * s + hi


def gpos_list(S, ng=8):
    return [int(x) for x in np.linspace(0, S - 1, ng).astype(np.int64)]


def qc_size(h):
    slope = 2.0 ** (-(h + 1) / 2.0)
    for qc in (512, 256, 128):
        if slope * (qc // 2) <= 64.0:
            return qc
    return 64


def alibi_plan(S, CH):
    cols, vals = {}, []
    for h in range(16):
        qc = qc_size(h)
        slope = 2.0 ** (-(h + 1) / 2.0)
        for c in range(S // CH):
            for qci in range(CH // qc):
                q0 = c * CH + qci * qc
                for kt in range((q0 + qc + 127) // 128):
                    if not kt_keep(slope, q0, qc, kt):
                        continue
                    key = (h, 128 * kt - q0 - qc // 2)
                    if key not in cols:
                        cols[key] = len(vals)
                        vals.append(slope * (128 * kt - q0 - qc // 2
                                             + np.arange(128, dtype=np.float64)))
    tab = np.array(vals, dtype=np.float32).T if vals else np.zeros((128, 1), np.float32)
    return cols, tab


def host_consts(S, CH):
    idn = np.eye(128, dtype=np.float32)
    tri = np.full((128, 1152), NEG, np.float32)
    for p in range(128):
        tri[p, 512 + p:] = 0.0
    oh8 = np.zeros((1, 64), np.float32)
    for g in range(8):
        oh8[0, 8 * g + g] = 1.0
    rowsel = np.zeros((8, 8, 128), np.float32)
    for e in range(8):
        rowsel[e, e, :] = 1.0
    al_cols, al_tab = alibi_plan(S, CH)
    return {"idn": idn, "tri": tri, "oh8": oh8, "rowsel": rowsel,
            "alibi": al_tab, "al_cols": al_cols}


class K:
    ENG_BLOCK = {"t": "tensor", "v": "vector", "s": "scalar", "g": "gpsimd", "y": "sync"}

    FLUSH_N = 128

    def __init__(self, nc, block, sems, dsems):
        self.nc, self.block = nc, block
        self._buf = {e: [] for e in "tvsgy"}
        self.sem = sems
        self.dsem = dsems
        self.cnt = {e: 0 for e in "tvsgy"}
        self.dcnt = {q: [0] * len(dsems[q]) for q in dsems}
        self.dlane = {q: 0 for q in dsems}
        self._waited = {}
        self._had_wait = {e: True for e in "tvsgy"}

    def _blk(self, e):
        return getattr(self.block, self.ENG_BLOCK[e])

    def _emit(self, e, fn):
        self._buf[e].append(fn)
        if len(self._buf[e]) >= self.FLUSH_N:
            self.flush(e)

    def flush(self, e):
        fns = self._buf[e]
        if not fns:
            return
        self._buf[e] = []

        def body(eng):
            for f in fns:
                f(eng)
        self._blk(e)(body)

    def flush_all(self):
        for e in "tvsgy":
            self.flush(e)

    def op(self, e, fn, inc=False):
        marker = None
        if inc:
            prev = self.cnt[e]
            self.cnt[e] += 1
            marker = self.cnt[e]
            sem = self.sem[e]
            boundary = True

            def body(eng):
                if boundary:
                    eng.wait_ge(sem, prev)
                instr = fn(eng)
                assert instr is not None, "inc op must return its instruction"
                instr.then_inc(sem, 1)
            self._emit(e, body)
            self._had_wait[e] = False
        else:
            self._emit(e, fn)
        return marker

    def wait(self, e, prod, val):
        if not val or e == prod:
            return
        key = (e, "c", prod)
        if not os.environ.get("MK_NODEDUP") and self._waited.get(key, 0) >= val:
            return
        self._waited[key] = val
        sem = self.sem[prod]
        self._emit(e, lambda eng: eng.wait_ge(sem, val))
        self._had_wait[e] = True

    def dma(self, q, out, in_, **kw):
        lane = self.dlane[q]
        self.dlane[q] = (lane + 1) % len(self.dsem[q])
        sem = self.dsem[q][lane]
        prev = self.dcnt[q][lane]
        self.dcnt[q][lane] = prev + 16
        marker = (q, lane, prev + 16)

        def body(eng):
            if prev > 0:
                eng.wait_ge(sem, prev)
            eng.dma_start(out=out, in_=in_, **kw).then_inc(sem, 16)
        self._emit(q, body)
        key = (q, "d", q, lane)
        if prev > 0 and self._waited.get(key, 0) < prev:
            self._waited[key] = prev
        return marker

    def snap(self, q):
        return tuple((q, lane, c) for lane, c in enumerate(self.dcnt[q]) if c > 0)

    def wait_dmas(self, e, markers):
        for m in markers:
            self.wait_dma(e, m)

    def wait_dma(self, e, marker):
        if not marker:
            return
        q, lane, val = marker
        key = (e, "d", q, lane)
        if self._waited.get(key, 0) >= val:
            return
        self._waited[key] = val
        sem = self.dsem[q][lane]
        self._emit(e, lambda eng: eng.wait_ge(sem, val))
        self._had_wait[e] = True

    def barrier(self, engines="tvsgy"):
        marks = {}
        for e in engines:
            if e in "ygvs":
                for lane, cntv in enumerate(self.dcnt[e]):
                    self.wait_dma(e, (e, lane, cntv))
            marks[e] = self.op(e, lambda eng: eng.nop(), inc=True)
        for e in engines:
            for p in engines:
                if p != e:
                    self.wait(e, p, marks[p])
        self.flush_all()


class PsumPool:
    def __init__(self, k, tensors):
        self.k = k
        self.slots = [{"t": t, "readers": [], "wmark": 0, "held": False}
                      for t in tensors]
        self.i = 0

    def acquire(self, hold=False):
        for _ in range(len(self.slots)):
            s = self.slots[self.i % len(self.slots)]
            self.i += 1
            if s["held"]:
                continue
            for (eng, m) in s["readers"]:
                self.k.wait("t", eng, m)
            s["readers"] = []
            s["held"] = hold
            return s
        raise RuntimeError("psum pool exhausted (all held)")

    def release(self, s):
        s["held"] = False


def build_core(S=4096, CH=512, subset=(0, 1, 2, 3)):
    D, NH, HD, NKV = 1024, 16, 64, 4
    NEXP, NSH, HID = 8, 2, 512
    NE_ALL = NEXP + NSH
    NG = 8
    nch = S // CH
    NT = CH // 128
    NKT = S // 128
    gpos = gpos_list(S, NG)
    SCALE = 1.0 / math.sqrt(HD)
    al_cols, al_tab = alibi_plan(S, CH)
    NAL = al_tab.shape[1]
    run_ffn = (1 in subset) or (3 in subset)

    nc = bass.Bass("TRN2", target_bir_lowering=False, debug=False)

    x_in = nc.dram_tensor("x", [S, D], F32, kind="ExternalInput").ap()
    y_out = nc.dram_tensor("y", [S, D], F32, kind="ExternalOutput").ap()
    Xa = nc.dram_tensor("Xa", [S, D], F32).ap()
    Xb = nc.dram_tensor("Xb", [S, D], F32).ap()

    aw = {n: nc.dram_tensor("aa_" + n, [D, D], F32, kind="ExternalInput").ap()
          for n in ("Wq", "Wk", "Wv", "Wo")}
    abias = {n: nc.dram_tensor("aa_" + n, [D], F32, kind="ExternalInput").ap()
             for n in ("bq", "bk", "bv", "bo")}
    ab_Wq = nc.dram_tensor("ab_Wq", [D, D], F32, kind="ExternalInput").ap()
    ab_Wkv = nc.dram_tensor("ab_Wkv", [512, D], F32, kind="ExternalInput").ap()
    ab_Wo = nc.dram_tensor("ab_Wo", [D, D], F32, kind="ExternalInput").ap()
    f_rw = nc.dram_tensor("f_rw", [D, NEXP], F32, kind="ExternalInput").ap()
    f_rb = nc.dram_tensor("f_rb", [NEXP], F32, kind="ExternalInput").ap()
    f_w1 = nc.dram_tensor("f_w1", [NEXP, D, HID], F32, kind="ExternalInput").ap()
    f_w3 = nc.dram_tensor("f_w3", [NEXP, D, HID], F32, kind="ExternalInput").ap()
    f_w2 = nc.dram_tensor("f_w2", [NEXP, HID, D], F32, kind="ExternalInput").ap()
    f_s1 = nc.dram_tensor("f_s1", [NSH, D, HID], F32, kind="ExternalInput").ap()
    f_s3 = nc.dram_tensor("f_s3", [NSH, D, HID], F32, kind="ExternalInput").ap()
    f_s2 = nc.dram_tensor("f_s2", [NSH, HID, D], F32, kind="ExternalInput").ap()

    idn_in = nc.dram_tensor("idn", [128, 128], F32, kind="ExternalInput").ap()
    tri_in = nc.dram_tensor("tri", [128, 1152], F32, kind="ExternalInput").ap()
    oh8_in = nc.dram_tensor("oh8", [1, 64], F32, kind="ExternalInput").ap()
    rsel_in = nc.dram_tensor("rowsel", [8, 8, 128], F32, kind="ExternalInput").ap()
    alibi_in = nc.dram_tensor("alibi", [128, NAL], F32, kind="ExternalInput").ap()

    ffw13 = nc.dram_tensor("ffw13", [NE_ALL, 2, 128, 8, HID], BF16).ap()
    ffw2 = nc.dram_tensor("ffw2", [NE_ALL, 2, 128, 4, 512], BF16).ap()

    with contextlib.ExitStack() as _st:
        cvals = _st.enter_context(nc.sbuf_tensor("cvals", [128, 2], F32))
        idn_bf = _st.enter_context(nc.sbuf_tensor("idn_bf", [128, 128], BF16))
        tri_bf = _st.enter_context(nc.sbuf_tensor("tri_bf", [128, 1152], BF16))
        oh8_bf = _st.enter_context(nc.sbuf_tensor("oh8_bf", [1, 64], BF16))
        rsel_bf = _st.enter_context(nc.sbuf_tensor("rsel_bf", [8, 8, 128], BF16))
        ones_bf = _st.enter_context(nc.sbuf_tensor("ones_bf", [1, 512], BF16))
        neg_bf = _st.enter_context(nc.sbuf_tensor("neg_bf", [1, 128], BF16))
        alibi_sb = _st.enter_context(nc.sbuf_tensor("alibi_sb", [128, NAL], F32))
        brows = _st.enter_context(nc.sbuf_tensor("brows", [1, 5, 1024], BF16))
        xf0 = _st.enter_context(nc.sbuf_tensor("xf0", [128, NT, D], F32))
        xf1 = _st.enter_context(nc.sbuf_tensor("xf1", [128, NT, D], F32))
        xn_sb = _st.enter_context(nc.sbuf_tensor("xn", [128, NT, D], BF16))
        xnT0 = _st.enter_context(nc.sbuf_tensor("xnT0", [128, 8, CH], BF16))
        xnT1 = _st.enter_context(nc.sbuf_tensor("xnT1", [128, 8, CH], BF16))
        xout = _st.enter_context(nc.sbuf_tensor("xout", [128, NT, D], F32))
        stats = _st.enter_context(nc.sbuf_tensor("stats", [128, NT, 12], F32))
        mv = _st.enter_context(nc.sbuf_tensor("mv", [128, NT, 2], F32))
        rstd = _st.enter_context(nc.sbuf_tensor("rstd", [128, NT], F32))
        lntmp = _st.enter_context(nc.sbuf_tensor("lntmp", [128, NT], F32))
        catT = _st.enter_context(nc.sbuf_tensor("catT", [128, 8, CH], BF16))
        pf0 = _st.enter_context(nc.psum_tensor("pf0", [128, 512], F32))
        pf1 = _st.enter_context(nc.psum_tensor("pf1", [128, 512], F32))
        pf2 = _st.enter_context(nc.psum_tensor("pf2", [128, 512], F32))
        pf3 = _st.enter_context(nc.psum_tensor("pf3", [128, 512], F32))
        pf4 = _st.enter_context(nc.psum_tensor("pf4", [128, 512], F32))
        pf5 = _st.enter_context(nc.psum_tensor("pf5", [128, 512], F32))
        pb0 = _st.enter_context(nc.psum_tensor("pb0", [128, 1024], BF16))
        pb1 = _st.enter_context(nc.psum_tensor("pb1", [128, 1024], BF16))
        sem_t = _st.enter_context(nc.semaphore("sem_t"))
        sem_v = _st.enter_context(nc.semaphore("sem_v"))
        sem_s = _st.enter_context(nc.semaphore("sem_s"))
        sem_g = _st.enter_context(nc.semaphore("sem_g"))
        sem_y = _st.enter_context(nc.semaphore("sem_y"))
        dma_y = [_st.enter_context(nc.semaphore(f"dma_y{i}")) for i in range(6)]
        dma_g = [_st.enter_context(nc.semaphore(f"dma_g{i}")) for i in range(6)]
        dma_v = [_st.enter_context(nc.semaphore("dma_v0"))]
        dma_s = [_st.enter_context(nc.semaphore("dma_s0"))]
        block = _st.enter_context(nc.Block())
        k = K(nc, block,
              {"t": sem_t, "v": sem_v, "s": sem_s, "g": sem_g, "y": sem_y},
              {"y": dma_y, "g": dma_g, "v": dma_v, "s": dma_s})
        pool = PsumPool(k, [pf0, pf1, pf2, pf3, pf4, pf5])
        bpool = PsumPool(k, [pb0, pb1])
        xf = [xf0, xf1]
        xnT = [xnT0, xnT1]
        bufm = {}

        for src, dst in ((idn_in[:, :], idn_bf[:]), (tri_in[:, :], tri_bf[:]),
                         (oh8_in[:, :], oh8_bf[:]), (rsel_in[:, :, :], rsel_bf[:])):
            k.dma("g", dst, src)
        m_al = k.dma("y", alibi_sb[:], alibi_in[:, :])
        for i, src in enumerate((abias["bq"], abias["bk"], abias["bv"], abias["bo"])):
            k.dma("g", brows[0:1, i, :], src.rearrange("(a d) -> a d", a=1))
        k.dma("g", brows[0:1, 4, 0:NEXP], f_rb.rearrange("(a d) -> a d", a=1))
        m_consts_g = k.snap("g")
        nc.const_aps.register(F32, 0.0, cvals[:, 0:1])
        nc.const_aps.register(F32, EPS, cvals[:, 1:2])
        k.op("v", lambda v: v.memset(cvals[:, 0:1], 0.0))
        k.op("v", lambda v: v.memset(cvals[:, 1:2], EPS))
        k.op("v", lambda v: v.memset(ones_bf[:], 1.0))
        k.op("v", lambda v: v.drain())
        m_consts_v = k.op("v", lambda v: v.memset(neg_bf[:], NEG), inc=True)

        m_prestage = 0
        if run_ffn:
            for e in range(NE_ALL):
                s1 = f_w1[e] if e < NEXP else f_s1[e - NEXP]
                s3 = f_w3[e] if e < NEXP else f_s3[e - NEXP]
                s2 = f_w2[e] if e < NEXP else f_s2[e - NEXP]
                k.dma("g", ffw13[e, 0], s1.rearrange("(a p) h -> p a h", p=128))
                k.dma("g", ffw13[e, 1], s3.rearrange("(a p) h -> p a h", p=128))
                for b_ in range(2):
                    k.dma("g", ffw2[e, b_],
                          s2[:, 512 * b_:512 * b_ + 512].rearrange(
                              "(a p) d -> p a d", p=128))
            m_prestage = k.snap("g")

        def ln_chunk(src, c, slot):
            k.wait("y", "v", bufm.get(("xf_v", slot), 0))
            k.wait_dmas("y", bufm.get("src_stores", ()))
            mld = k.dma("y", xf[slot][:, :, :],
                        src[c * CH:(c + 1) * CH, :].rearrange("(j p) d -> p j d", p=128))
            k.wait_dma("v", mld)
            k.wait("v", "s", bufm.get("lnscalar", 0))
            for j in range(NT):
                k.op("v", lambda v, j=j: v.bn_stats(stats[:, j, 0:6], xf[slot][:, j, 0:512]))
                k.op("v", lambda v, j=j: v.bn_stats(stats[:, j, 6:12], xf[slot][:, j, 512:1024]))
            k.op("v", lambda v: v.drain())
            for j in range(NT):
                k.op("v", lambda v, j=j: v.bn_aggr(mv[:, j, :], stats[:, j, :]),
                     inc=(j == NT - 1))
            m_mv = k.cnt["v"]
            k.wait("s", "v", m_mv)
            k.op("s", lambda s: s.activation(lntmp[:, :], mv[:, :, 1], AF.Ln, bias=EPS))
            k.op("s", lambda s: s.drain())
            m_rstd = k.op("s", lambda s: s.activation(rstd[:, :], lntmp[:, :], AF.Exp,
                                                      scale=-0.5), inc=True)
            bufm["lnscalar"] = m_rstd
            k.wait("v", "s", m_rstd)
            k.wait("v", "t", bufm.get("xn_t", 0))
            for j in range(NT):
                k.op("v", lambda v, j=j, slot=slot: v.tensor_scalar(
                    xn_sb[:, j, :], xf[slot][:, j, :], mv[:, j, 0:1], rstd[:, j:j + 1],
                    ALU.subtract, ALU.mult), inc=(j == NT - 1))
            m_xn = k.cnt["v"]
            k.wait("t", "v", m_xn)
            for j in range(NT):
                b = bpool.acquire()
                for dc in range(8):
                    k.op("t", lambda t, j=j, dc=dc, b=b: t.matmul(
                        b["t"][:, 128 * dc:128 * dc + 128],
                        xn_sb[:, j, 128 * dc:128 * dc + 128], idn_bf[:],
                        start=True, stop=True, is_transpose=True), inc=(dc == 7))
                b["wmark"] = k.cnt["t"]
                k.wait("v", "t", b["wmark"])
                k.wait("v", "t", bufm.get(("xnT_t", slot), 0))
                mm = k.op("v", lambda v, j=j, b=b, slot=slot: v.tensor_copy(
                    xnT[slot][:, :, 128 * j:128 * j + 128],
                    b["t"][:].rearrange("p (a q) -> p a q", a=8)), inc=True)
                b["readers"].append(("v", mm))
            bufm["xn_t"] = k.cnt["t"]
            return mld, k.cnt["v"]

        def note_xnT_read(slot, marker):
            key = ("xnT_t", slot)
            bufm[key] = max(bufm.get(key, 0), marker)

        def proj_fm(wview, col0, m_xnT, slot, bias=None):
            p = pool.acquire()
            k.wait("t", "v", m_xnT)
            for ic in range(8):
                last = (ic == 7) and bias is None
                k.op("t", lambda t, ic=ic, p=p, last=last, slot=slot: t.matmul(
                    p["t"][:, 0:CH], wview[:, ic, col0:col0 + 128],
                    xnT[slot][:, ic, :], start=(ic == 0), stop=last), inc=last)
            if bias is not None:
                bi = bias
                k.op("t", lambda t, p=p: t.matmul(
                    p["t"][:, 0:CH], brows[0:1, bi, col0:col0 + 128],
                    ones_bf[0:1, 0:CH], start=False, stop=True), inc=True)
            p["wmark"] = k.cnt["t"]
            note_xnT_read(slot, p["wmark"])
            return p

        def wo_resid_store(wo_view, bias_i, dst, c, slot, m_cat_v):
            k.wait("t", "v", m_cat_v)
            last_v = 0
            for j in range(NT):
                for eh in range(2):
                    p = pool.acquire()
                    for ic in range(8):
                        last = (ic == 7) and bias_i is None
                        k.op("t", lambda t, j=j, eh=eh, ic=ic, p=p, last=last: t.matmul(
                            p["t"][:, 0:512],
                            catT[:, ic, 128 * j:128 * j + 128],
                            wo_view[:, ic, 512 * eh:512 * eh + 512],
                            start=(ic == 0), stop=last), inc=last)
                    if bias_i is not None:
                        k.op("t", lambda t, eh=eh, p=p, bi=bias_i: t.matmul(
                            p["t"][:, 0:512], ones_bf[0:1, 0:128],
                            brows[0:1, bi, 512 * eh:512 * eh + 512],
                            start=False, stop=True), inc=True)
                    p["wmark"] = k.cnt["t"]
                    k.wait("v", "t", p["wmark"])
                    k.wait_dma("v", bufm.get("xout_store"))
                    mv_ = k.op("v", lambda v, j=j, eh=eh, p=p: v.scalar_tensor_tensor(
                        xout[:, j, 512 * eh:512 * eh + 512], p["t"][:, 0:512], 1.0,
                        xf[slot][:, j, 512 * eh:512 * eh + 512],
                        ALU.mult, ALU.add), inc=True)
                    p["readers"].append(("v", mv_))
                    last_v = mv_
            bufm[("xf_v", slot)] = last_v
            k.wait("y", "v", last_v)
            mst = k.dma("y", dst[c * CH:(c + 1) * CH, :].rearrange("(j p) d -> p j d", p=128),
                        xout[:, :, :])
            bufm["xout_store"] = mst
            return mst

        def attn_a(src, dst, prologue=None):
            with contextlib.ExitStack() as _sst:
                wa = _sst.enter_context(nc.sbuf_tensor("a_w", [128, 4, 8, D], BF16))
                QT = _sst.enter_context(nc.sbuf_tensor("a_QT", [128, 8, CH], BF16))
                KT = _sst.enter_context(nc.sbuf_tensor("a_KT", [128, 8, CH], BF16))
                Vaug = _sst.enter_context(nc.sbuf_tensor("a_Vaug", [128, NT, 16, 65], BF16))
                KgT = _sst.enter_context(nc.sbuf_tensor("a_KgT", [128, 8, 8], BF16))
                Vg = _sst.enter_context(nc.sbuf_tensor("a_Vg", [8, 16, 65], BF16))
                BD = _sst.enter_context(nc.sbuf_tensor("a_BD", [128, 8, 128], BF16))
                macc = _sst.enter_context(nc.sbuf_tensor("a_macc", [128, 1024], F32))
                fs = _sst.enter_context(nc.sbuf_tensor("a_fs", [128, 8], F32))
                PTl = _sst.enter_context(nc.sbuf_tensor("a_PTl", [128, 512], BF16))
                PTg = _sst.enter_context(nc.sbuf_tensor("a_PTg", [8, 512], BF16))
                PgT = _sst.enter_context(nc.sbuf_tensor("a_PgT", [128, NT, 128], BF16))
                strip_bf = _sst.enter_context(nc.sbuf_tensor("a_strip", [128, 1024], BF16))
                atok = _sst.enter_context(nc.sbuf_tensor("a_tok", [128, NT, 64], BF16))
                rec4 = _sst.enter_context(nc.sbuf_tensor("a_rec", [128, NT], F32))
                xg = xout[0:8, 1, :]
                xgs = _sst.enter_context(nc.sbuf_tensor("a_xgs", [8, 14], F32))
                xgn = _sst.enter_context(nc.sbuf_tensor("a_xgn", [8, 1024], BF16))
                xgT = _sst.enter_context(nc.sbuf_tensor("a_xgT", [128, 8, 8], BF16))
                cglob = _sst.enter_context(nc.sbuf_tensor("a_cg", [128, 8, 8], BF16))
                saved = _sst.enter_context(nc.sbuf_tensor("a_sv", [128, 8, 8], BF16))
                grmw = xout[0:8, 0, :]
                for i, n in enumerate(("Wq", "Wk", "Wv", "Wo")):
                    k.dma("g", wa[:, i, :, :], aw[n].rearrange("(a p) o -> p a o", p=128))
                mw = k.snap("g")
                Wq, Wk, Wv, Wo = wa[:, 0], wa[:, 1], wa[:, 2], wa[:, 3]
                k.wait_dmas("t", mw)
                k.wait_dmas("t", m_consts_g)
                k.wait("t", "v", m_consts_v)
                k.wait_dma("v", m_al)

                mgl_ = [k.dma("y", xg[g:g + 1, :], src[gpos[g]:gpos[g] + 1, :])
                        for g in range(NG)]
                k.wait_dmas("v", mgl_)
                k.op("v", lambda v: v.bn_stats(xgs[:, 0:6], xg[:, 0:512]))
                k.op("v", lambda v: v.bn_stats(xgs[:, 6:12], xg[:, 512:1024]))
                k.op("v", lambda v: v.drain())
                m1 = k.op("v", lambda v: v.bn_aggr(xgs[:, 12:14], xgs[:, 0:12]), inc=True)
                k.wait("s", "v", m1)
                k.op("s", lambda s: s.activation(xgs[:, 0:1], xgs[:, 13:14], AF.Ln, bias=EPS))
                k.op("s", lambda s: s.drain())
                m2 = k.op("s", lambda s: s.activation(xgs[:, 1:2], xgs[:, 0:1], AF.Exp,
                                                      scale=-0.5), inc=True)
                k.wait("v", "s", m2)
                m3 = k.op("v", lambda v: v.tensor_scalar(
                    xgn[:, :], xg[:, :], xgs[:, 12:13], xgs[:, 1:2],
                    ALU.subtract, ALU.mult), inc=True)
                k.wait("t", "v", m3)
                b = bpool.acquire()
                for ic in range(8):
                    k.op("t", lambda t, ic=ic, b=b: t.matmul(
                        b["t"][:, 8 * ic:8 * ic + 8], xgn[:, 128 * ic:128 * ic + 128],
                        idn_bf[0:8, 0:8], start=True, stop=True, is_transpose=True),
                        inc=(ic == 7))
                b["wmark"] = k.cnt["t"]
                k.wait("v", "t", b["wmark"])
                m4 = k.op("v", lambda v, b=b: v.tensor_copy(
                    xgT[:, :, :], b["t"][:, 0:64].rearrange("p (a q) -> p a q", a=8)),
                    inc=True)
                b["readers"].append(("v", m4))

                k.op("v", lambda v: v.memset(BD[:, :, :], 0.0))
                k.op("v", lambda v: v.memset(Vg[:, :, :], 1.0))
                k.op("v", lambda v: v.drain())
                m_bd0 = k.op("v", lambda v: v.nop(), inc=True)

                k.wait("t", "v", m4)
                for dc in range(8):
                    p = pool.acquire()
                    for ic in range(8):
                        k.op("t", lambda t, dc=dc, ic=ic, p=p: t.matmul(
                            p["t"][:, 0:8], Wq[:, ic, 128 * dc:128 * dc + 128],
                            xgT[:, ic, :], start=(ic == 0), stop=False))
                    mq = k.op("t", lambda t, dc=dc, p=p: t.matmul(
                        p["t"][:, 0:8], brows[0:1, 0, 128 * dc:128 * dc + 128],
                        ones_bf[0:1, 0:8], start=False, stop=True), inc=True)
                    p["wmark"] = mq
                    k.wait("v", "t", mq)
                    for half in range(2):
                        mq2 = k.op("v", lambda v, dc=dc, half=half, p=p: v.tensor_scalar(
                            BD[64 * half:64 * half + 64, dc,
                               16 * dc + 8 * half:16 * dc + 8 * half + 8],
                            p["t"][64 * half:64 * half + 64, 0:8],
                            SCALE, None, ALU.mult), inc=(half == 1))
                        p["readers"].append(("v", mq2))
                    p = pool.acquire()
                    for ic in range(8):
                        k.op("t", lambda t, dc=dc, ic=ic, p=p: t.matmul(
                            p["t"][:, 0:8], Wk[:, ic, 128 * dc:128 * dc + 128],
                            xgT[:, ic, :], start=(ic == 0), stop=False))
                    mk_ = k.op("t", lambda t, dc=dc, p=p: t.matmul(
                        p["t"][:, 0:8], brows[0:1, 1, 128 * dc:128 * dc + 128],
                        ones_bf[0:1, 0:8], start=False, stop=True), inc=True)
                    p["wmark"] = mk_
                    k.wait("v", "t", mk_)
                    mk2 = k.op("v", lambda v, dc=dc, p=p: v.tensor_copy(
                        KgT[:, dc, :], p["t"][:, 0:8]), inc=True)
                    p["readers"].append(("v", mk2))
                k.wait("t", "v", m_bd0)
                for oh in range(2):
                    p = pool.acquire()
                    for ic in range(8):
                        k.op("t", lambda t, oh=oh, ic=ic, p=p: t.matmul(
                            p["t"][0:8, 0:512], xgT[:, ic, :],
                            Wv[:, ic, 512 * oh:512 * oh + 512],
                            start=(ic == 0), stop=False))
                    mvg = k.op("t", lambda t, oh=oh, p=p: t.matmul(
                        p["t"][0:8, 0:512], ones_bf[0:1, 0:8],
                        brows[0:1, 2, 512 * oh:512 * oh + 512],
                        start=False, stop=True), inc=True)
                    p["wmark"] = mvg
                    k.wait("v", "t", mvg)
                    mvg2 = k.op("v", lambda v, oh=oh, p=p: v.tensor_copy(
                        Vg[:, 8 * oh:8 * oh + 8, 0:64],
                        p["t"][0:8, 0:512].rearrange("p (a d) -> p a d", a=8)), inc=True)
                    p["readers"].append(("v", mvg2))

                k.op("v", lambda v: v.memset(macc[:, :], 0.0))
                k.op("v", lambda v: v.memset(fs[:, 1:2], 0.0))
                k.op("v", lambda v: v.memset(fs[:, 0:1], -3.0e38))
                k.op("v", lambda v: v.drain())
                m_f0 = k.op("v", lambda v: v.nop(), inc=True)

                store_marks = []
                lns = {0: carry.pop("ln0")} if "ln0" in carry else \
                    {0: ln_chunk(src, 0, 0)}
                for c in range(nch):
                    slot = c % 2
                    mld, m_xnT = lns.pop(c)
                    glist = [g for g in range(NG) if gpos[g] // CH == c]

                    k.wait("s", "t", bufm.get("a_qkv_t", 0))
                    k.wait("v", "t", bufm.get("a_qkv_t", 0))
                    for dc in range(8):
                        p = proj_fm(Wq, 128 * dc, m_xnT, slot, bias=0)
                        k.wait("s", "t", p["wmark"])
                        ms = k.op("s", lambda s, dc=dc, p=p: s.activation(
                            QT[:, dc, :], p["t"][:, 0:CH], AF.Copy, scale=SCALE),
                            inc=True)
                        p["readers"].append(("s", ms))
                        p = proj_fm(Wk, 128 * dc, m_xnT, slot, bias=1)
                        k.wait("v", "t", p["wmark"])
                        mv2 = k.op("v", lambda v, dc=dc, p=p: v.tensor_copy(
                            KT[:, dc, :], p["t"][:, 0:CH]), inc=True)
                        p["readers"].append(("v", mv2))
                    for j in range(NT):
                        for oh in range(2):
                            p = pool.acquire()
                            k.wait("t", "v", m_xnT)
                            for ic in range(8):
                                k.op("t", lambda t, j=j, oh=oh, ic=ic, p=p,
                                     slot=slot: t.matmul(
                                    p["t"][:, 0:512],
                                    xnT[slot][:, ic, 128 * j:128 * j + 128],
                                    Wv[:, ic, 512 * oh:512 * oh + 512],
                                    start=(ic == 0), stop=False))
                            mvp = k.op("t", lambda t, oh=oh, p=p: t.matmul(
                                p["t"][:, 0:512], ones_bf[0:1, 0:128],
                                brows[0:1, 2, 512 * oh:512 * oh + 512],
                                start=False, stop=True), inc=True)
                            p["wmark"] = mvp
                            note_xnT_read(slot, mvp)
                            k.wait("v", "t", mvp)
                            mvv = k.op("v", lambda v, j=j, oh=oh, p=p: v.tensor_copy(
                                Vaug[:, j, 8 * oh:8 * oh + 8, 0:64],
                                p["t"][:, 0:512].rearrange("p (a d) -> p a d", a=8)),
                                inc=True)
                            p["readers"].append(("v", mvv))
                    mvo = k.op("v", lambda v: v.memset(Vaug[:, :, :, 64:65], 1.0),
                               inc=True)
                    m_qkv_v = k.cnt["v"]
                    m_qkv_s = k.cnt["s"]
                    if c + 1 < nch:
                        lns[c + 1] = ln_chunk(src, c + 1, (c + 1) % 2)

                    p = pool.acquire()
                    k.wait("t", "v", m_qkv_v)
                    for ic in range(8):
                        k.op("t", lambda t, ic=ic, p=p: t.matmul(
                            p["t"][:, 0:CH], BD[:, ic, :], KT[:, ic, :],
                            start=(ic == 0), stop=(ic == 7)), inc=(ic == 7))
                    p["wmark"] = k.cnt["t"]
                    k.wait("v", "t", p["wmark"])
                    k.wait("v", "s", bufm.get("a_strip_s", 0))
                    k.op("v", lambda v, p=p: v.tensor_reduce(
                        fs[:, 2:3], p["t"][:, 0:CH], AX.X, ALU.max))
                    k.op("v", lambda v: v.drain())
                    k.op("v", lambda v: v.tensor_tensor(
                        fs[:, 3:4], fs[:, 0:1], fs[:, 2:3], ALU.max))
                    k.op("v", lambda v: v.drain())
                    mneg = k.op("v", lambda v: v.tensor_scalar(
                        fs[:, 4:5], fs[:, 3:4], -1.0, None, ALU.mult), inc=True)
                    k.wait("s", "v", mneg)
                    k.op("s", lambda s: s.activation(
                        fs[:, 5:6], fs[:, 0:1], AF.Exp, bias=fs[:, 4:5]))
                    mstr = k.op("s", lambda s, p=p: s.activation(
                        strip_bf[:, 0:CH], p["t"][:, 0:CH], AF.Exp,
                        bias=fs[:, 4:5]), inc=True)
                    bufm["a_strip_s"] = mstr
                    p["readers"].append(("s", mstr))
                    k.wait("v", "s", mstr)
                    k.op("v", lambda v: v.tensor_reduce(
                        fs[:, 6:7], strip_bf[:, 0:CH], AX.X, ALU.add))
                    k.op("v", lambda v: v.drain())
                    k.op("v", lambda v: v.scalar_tensor_tensor(
                        fs[:, 1:2], fs[:, 1:2], fs[:, 5:6], fs[:, 6:7],
                        ALU.mult, ALU.add))
                    k.op("v", lambda v: v.drain())
                    mfl1 = k.op("v", lambda v: v.tensor_copy(fs[:, 0:1], fs[:, 3:4]),
                                inc=True)
                    k.wait("t", "s", mstr)
                    b = bpool.acquire()
                    for j in range(NT):
                        k.op("t", lambda t, j=j, b=b: t.matmul(
                            b["t"][:, 128 * j:128 * j + 128],
                            strip_bf[:, 128 * j:128 * j + 128], idn_bf[:],
                            start=True, stop=True, is_transpose=True), inc=(j == NT - 1))
                    b["wmark"] = k.cnt["t"]
                    bufm["a_stripT_t"] = b["wmark"]
                    k.wait("v", "t", b["wmark"])
                    mpg = k.op("v", lambda v, b=b: v.tensor_copy(
                        PgT[:, :, :], b["t"][:, 0:128 * NT].rearrange(
                            "p (a q) -> p a q", a=NT)), inc=True)
                    b["readers"].append(("v", mpg))
                    k.wait("t", "v", mpg)
                    k.wait("t", "v", mfl1)
                    for dh in range(2):
                        p = pool.acquire()
                        for j in range(NT):
                            k.op("t", lambda t, dh=dh, j=j, p=p: t.matmul(
                                p["t"][:, 0:512].rearrange("p (a d) -> p a d", a=8),
                                PgT[:, j, :],
                                Vaug[:, j, 8 * dh:8 * dh + 8, 0:64],
                                start=(j == 0), stop=(j == NT - 1)), inc=(j == NT - 1))
                        p["wmark"] = k.cnt["t"]
                        k.wait("v", "t", p["wmark"])
                        mfl = k.op("v", lambda v, dh=dh, p=p: v.scalar_tensor_tensor(
                            macc[:, 512 * dh:512 * dh + 512],
                            macc[:, 512 * dh:512 * dh + 512], fs[:, 5:6],
                            p["t"][:, 0:512], ALU.mult, ALU.add), inc=True)
                        p["readers"].append(("v", mfl))

                    def a_tail(pO, h, dc, base):
                        k.wait("v", "t", pO["wmark"])
                        k.wait("v", "t", bufm.get("a_tok_t", 0))
                        k.op("v", lambda v, pO=pO: v.reciprocal(
                            rec4[:, :], pO["t"][:, 0:65 * NT].rearrange(
                                "p (a b) -> p a b", b=65)[:, :, 64]))
                        k.op("v", lambda v: v.drain())
                        for qt in range(NT):
                            mdv = k.op("v", lambda v, qt=qt, pO=pO: v.tensor_scalar(
                                atok[:, qt, :], pO["t"][:, 65 * qt:65 * qt + 64],
                                rec4[:, qt:qt + 1], None, ALU.mult), inc=(qt == NT - 1))
                        pO["readers"].append(("v", mdv))
                        pool.release(pO)
                        k.wait("t", "v", mdv)
                        b = bpool.acquire()
                        for qt in range(NT):
                            k.op("t", lambda t, qt=qt, b=b: t.matmul(
                                b["t"][0:64, 128 * qt:128 * qt + 128],
                                atok[:, qt, :], idn_bf[:],
                                start=True, stop=True, is_transpose=True),
                                inc=(qt == NT - 1))
                        b["wmark"] = k.cnt["t"]
                        bufm["a_tok_t"] = b["wmark"]
                        k.wait("v", "t", b["wmark"])
                        k.wait("v", "t", bufm.get("catT_t", 0))
                        mct = k.op("v", lambda v, dc=dc, base=base, b=b: v.tensor_copy(
                            catT[base:base + 64, dc, :], b["t"][0:64, 0:128 * NT]),
                            inc=True)
                        b["readers"].append(("v", mct))

                    a_pending = None

                    def a_scores(h):
                        dc, base = h // 2, 64 * (h % 2)
                        pA = pool.acquire()
                        pG = pool.acquire()
                        k.wait("t", "s", m_qkv_s)
                        for qt in range(NT):
                            k.op("t", lambda t, qt=qt, pA=pA, dc=dc, base=base: t.matmul(
                                pA["t"][:, 128 * qt:128 * qt + 128],
                                KT[base:base + 64, dc, 128 * qt:128 * qt + 128],
                                QT[base:base + 64, dc, 128 * qt:128 * qt + 128],
                                start=True, stop=False, skip_group_check=True))
                            k.op("t", lambda t, qt=qt, pA=pA: t.matmul(
                                pA["t"][0:64, 128 * qt + 64:128 * qt + 128],
                                ones_bf[0:1, 0:64], neg_bf[0:1, 0:64],
                                start=False, stop=False, skip_group_check=True))
                            k.op("t", lambda t, qt=qt, pA=pA: t.matmul(
                                pA["t"][64:128, 128 * qt:128 * qt + 64],
                                ones_bf[0:1, 0:64], neg_bf[0:1, 0:64],
                                start=False, stop=(qt == NT - 1), skip_group_check=True),
                                inc=(qt == NT - 1))
                        mA = k.cnt["t"]
                        pA["wmark"] = mA
                        for qt in range(NT):
                            last = (qt == NT - 1) and not glist
                            k.op("t", lambda t, qt=qt, pG=pG, dc=dc, base=base, last=last:
                                 t.matmul(pG["t"][0:8, 128 * qt:128 * qt + 128],
                                          KgT[base:base + 64, dc, :],
                                          QT[base:base + 64, dc, 128 * qt:128 * qt + 128],
                                          start=(qt == 0), stop=last,
                                          skip_group_check=True), inc=last)
                        for gi, g in enumerate(glist):
                            off = gpos[g] - c * CH
                            blk0 = (off // 64) * 64
                            last = (gi == len(glist) - 1)
                            k.op("t", lambda t, g=g, blk0=blk0, pG=pG, last=last: t.matmul(
                                pG["t"][0:8, blk0:blk0 + 64],
                                oh8_bf[0:1, 8 * g:8 * g + 8], neg_bf[0:1, 0:64],
                                start=False, stop=last, skip_group_check=True), inc=last)
                        mG = k.cnt["t"]
                        pG["wmark"] = mG
                        return pA, pG, mG

                    def a_headfin(h, pA, pG, mG):
                        dc, base = h // 2, 64 * (h % 2)
                        nonlocal a_pending
                        k.wait("s", "t", mG)
                        k.wait("s", "t", bufm.get("a_PT_pv", 0))
                        msl = k.op("s", lambda s, pA=pA: s.activation(
                            PTl[:, :], pA["t"][:, 0:512], AF.Exp), inc=True)
                        pA["readers"].append(("s", msl))
                        msg = k.op("s", lambda s, pG=pG: s.activation(
                            PTg[:, :], pG["t"][0:8, 0:512], AF.Exp), inc=True)
                        pG["readers"].append(("s", msg))
                        pO = pool.acquire(hold=True)
                        k.wait("t", "s", msg)
                        for qt in range(NT):
                            k.op("t", lambda t, h=h, qt=qt, pO=pO: t.matmul(
                                pO["t"][:, 65 * qt:65 * qt + 65],
                                PTl[:, 128 * qt:128 * qt + 128], Vaug[:, qt, h, :],
                                start=(qt == 0), stop=False, skip_group_check=True))
                            k.op("t", lambda t, h=h, qt=qt, pO=pO: t.matmul(
                                pO["t"][:, 65 * qt:65 * qt + 65],
                                PTg[:, 128 * qt:128 * qt + 128], Vg[:, h, :],
                                start=False, stop=(qt == NT - 1), skip_group_check=True),
                                inc=(qt == NT - 1))
                        pO["wmark"] = k.cnt["t"]
                        bufm["a_PT_pv"] = pO["wmark"]
                        if a_pending is not None:
                            a_tail(*a_pending)
                        a_pending = (pO, h, dc, base)

                    sc_pending = None
                    for h in range(NH):
                        sc = a_scores(h)
                        if sc_pending is not None:
                            a_headfin(*sc_pending)
                        sc_pending = (h,) + sc
                    if sc_pending is not None:
                        a_headfin(*sc_pending)
                        sc_pending = None
                    if a_pending is not None:
                        a_tail(*a_pending)
                        a_pending = None
                    m_cat_v = k.cnt["v"]
                    bufm["a_qkv_t"] = k.cnt["t"]

                    for g in glist:
                        col = gpos[g] - c * CH
                        m_cat_v = k.op("v", lambda v, g=g, col=col: v.tensor_copy(
                            saved[:, :, g], catT[:, :, col]), inc=True)

                    store_marks.append(wo_resid_store(Wo, 3, dst, c, slot, m_cat_v))
                    bufm["catT_t"] = k.cnt["t"]

                k.wait("v", "t", bufm.get("a_stripT_t", 0))
                k.op("v", lambda v: v.reciprocal(fs[:, 7:8], fs[:, 1:2]))
                k.op("v", lambda v: v.drain())
                mnrm = k.op("v", lambda v: v.tensor_scalar(
                    strip_bf[:, :], macc[:, :], fs[:, 7:8], None, ALU.mult), inc=True)
                k.wait("t", "v", mnrm)
                b = bpool.acquire()
                for dcb in range(8):
                    k.op("t", lambda t, dcb=dcb, b=b: t.matmul(
                        b["t"][:, 128 * dcb:128 * dcb + 128],
                        strip_bf[:, 128 * dcb:128 * dcb + 128], idn_bf[:],
                        start=True, stop=True, is_transpose=True), inc=(dcb == 7))
                b["wmark"] = k.cnt["t"]
                k.wait("v", "t", b["wmark"])
                for dcb in range(8):
                    k.op("v", lambda v, dcb=dcb, b=b: v.tensor_copy(
                        cglob[0:64, dcb, :],
                        b["t"][0:64, 128 * dcb + 16 * dcb:128 * dcb + 16 * dcb + 8]))
                    mgl = k.op("v", lambda v, dcb=dcb, b=b: v.tensor_copy(
                        cglob[64:128, dcb, :],
                        b["t"][64:128, 128 * dcb + 16 * dcb + 8:128 * dcb + 16 * dcb + 16]),
                        inc=(dcb == 7))
                b["readers"].append(("v", k.cnt["v"]))
                k.op("v", lambda v: v.drain())
                mdel = k.op("v", lambda v: v.tensor_tensor(
                    cglob[:, :, :], cglob[:, :, :], saved[:, :, :], ALU.subtract),
                    inc=True)
                k.wait("t", "v", mdel)
                k.wait_dmas("y", store_marks)
                mg2 = [k.dma("y", grmw[g:g + 1, :], dst[gpos[g]:gpos[g] + 1, :])
                       for g in range(NG)]
                mfx = 0
                for eh in range(2):
                    p = pool.acquire()
                    for ic in range(8):
                        k.op("t", lambda t, eh=eh, ic=ic, p=p: t.matmul(
                            p["t"][0:8, 0:512], cglob[:, ic, :],
                            Wo[:, ic, 512 * eh:512 * eh + 512],
                            start=(ic == 0), stop=(ic == 7)), inc=(ic == 7))
                    p["wmark"] = k.cnt["t"]
                    k.wait("v", "t", p["wmark"])
                    k.wait_dmas("v", mg2)
                    mfx = k.op("v", lambda v, eh=eh, p=p: v.tensor_tensor(
                        grmw[:, 512 * eh:512 * eh + 512],
                        grmw[:, 512 * eh:512 * eh + 512],
                        p["t"][0:8, 0:512], ALU.add), inc=True)
                    p["readers"].append(("v", mfx))
                k.wait("y", "v", mfx)
                for g in range(NG):
                    k.dma("y", dst[gpos[g]:gpos[g] + 1, :], grmw[g:g + 1, :])
                if prologue:
                    bufm["src_stores"] = k.snap("y")
                    prologue()
            k.barrier()

        def ffn(src, dst, tag, prologue=None):
            with contextlib.ExitStack() as _sst:
                w1s = _sst.enter_context(nc.sbuf_tensor(f"f_w1s{tag}", [128, 2, 8, HID], BF16))
                w3s = _sst.enter_context(nc.sbuf_tensor(f"f_w3s{tag}", [128, 2, 8, HID], BF16))
                w2s = _sst.enter_context(nc.sbuf_tensor(f"f_w2s{tag}", [128, 2, 4, 512], BF16))
                rws = _sst.enter_context(nc.sbuf_tensor(f"f_rws{tag}", [128, 8, 8], BF16))
                H1G = _sst.enter_context(nc.sbuf_tensor(f"f_H1G{tag}", [128, NE_ALL, 4, CH], BF16))
                G = _sst.enter_context(nc.sbuf_tensor(f"f_G{tag}", [128, NT, 32], F32))
                Gbf = _sst.enter_context(nc.sbuf_tensor(f"f_Gbf{tag}", [128, NT, 8], BF16))
                GT = _sst.enter_context(nc.sbuf_tensor(f"f_GT{tag}", [8, CH], BF16))
                sA = _sst.enter_context(nc.sbuf_tensor(f"f_sA{tag}", [128, 512], BF16))
                t1 = _sst.enter_context(nc.sbuf_tensor(f"f_t1{tag}", [128, 512], BF16))
                mrw = k.dma("g", rws[:, :, :], f_rw.rearrange("(a p) e -> p a e", p=128))
                k.wait_dma("t", mrw)
                k.wait_dmas("t", m_prestage)
                k.wait_dmas("t", m_consts_g)
                k.wait("t", "v", m_consts_v)

                lns = {0: carry.pop("ln0")} if "ln0" in carry else \
                    {0: ln_chunk(src, 0, 0)}
                wmk = {}
                w2mk = {}
                for c in range(nch):
                    slot = c % 2
                    mld, m_xnT = lns.pop(c)

                    k.wait("v", "s", bufm.get("f_gate_s", 0))
                    k.wait("v", "t", bufm.get("f_GT_t", 0))
                    for j in range(NT):
                        p = pool.acquire()
                        k.wait("t", "v", m_xnT)
                        for ic in range(8):
                            k.op("t", lambda t, j=j, ic=ic, p=p, slot=slot: t.matmul(
                                p["t"][:, 0:8],
                                xnT[slot][:, ic, 128 * j:128 * j + 128],
                                rws[:, ic, :], start=(ic == 0), stop=False))
                        mro = k.op("t", lambda t, p=p: t.matmul(
                            p["t"][:, 0:8], ones_bf[0:1, 0:128], brows[0:1, 4, 0:8],
                            start=False, stop=True), inc=True)
                        p["wmark"] = mro
                        note_xnT_read(slot, mro)
                        k.wait("v", "t", mro)
                        k.op("v", lambda v, j=j, p=p: v.tensor_reduce(
                            G[:, j, 24:25], p["t"][:, 0:8], AX.X, ALU.max))
                        k.op("v", lambda v: v.drain())
                        k.op("v", lambda v, j=j, p=p: v.tensor_scalar(
                            G[:, j, 0:8], p["t"][:, 0:8], G[:, j, 24:25], None, ALU.is_ge))
                        k.op("v", lambda v: v.drain())
                        k.op("v", lambda v, j=j: v.tensor_reduce(
                            G[:, j, 25:26], G[:, j, 0:8], AX.X, ALU.add))
                        k.op("v", lambda v, j=j, p=p: v.scalar_tensor_tensor(
                            G[:, j, 8:16], G[:, j, 0:8], NEG, p["t"][:, 0:8],
                            ALU.mult, ALU.add))
                        k.op("v", lambda v: v.drain())
                        k.op("v", lambda v, j=j: v.tensor_reduce(
                            G[:, j, 26:27], G[:, j, 8:16], AX.X, ALU.max))
                        k.op("v", lambda v, j=j: v.tensor_scalar(
                            G[:, j, 27:28], G[:, j, 25:26], 2.0, None, ALU.is_ge))
                        k.op("v", lambda v: v.drain())
                        k.op("v", lambda v, j=j: v.tensor_tensor(
                            G[:, j, 28:29], G[:, j, 24:25], G[:, j, 26:27], ALU.subtract))
                        k.op("v", lambda v: v.drain())
                        k.op("v", lambda v, j=j: v.tensor_tensor(
                            G[:, j, 28:29], G[:, j, 28:29], G[:, j, 27:28], ALU.mult))
                        k.op("v", lambda v: v.drain())
                        k.op("v", lambda v, j=j: v.tensor_tensor(
                            G[:, j, 28:29], G[:, j, 28:29], G[:, j, 26:27], ALU.add))
                        mneg1 = k.op("v", lambda v, j=j: v.tensor_scalar(
                            G[:, j, 29:30], G[:, j, 24:25], -1.0, None, ALU.mult),
                            inc=True)
                        k.wait("s", "v", mneg1)
                        mz = k.op("s", lambda s, j=j, p=p: s.activation(
                            G[:, j, 16:24], p["t"][:, 0:8], AF.Exp,
                            bias=G[:, j, 29:30]), inc=True)
                        bufm["f_gate_s"] = mz
                        p["readers"].append(("s", mz))
                        k.wait("v", "s", mz)
                        mke = k.op("v", lambda v, j=j, p=p: v.tensor_scalar(
                            G[:, j, 0:8], p["t"][:, 0:8], G[:, j, 28:29], None,
                            ALU.is_ge), inc=True)
                        p["readers"].append(("v", mke))
                        k.op("v", lambda v: v.drain())
                        k.op("v", lambda v, j=j: v.tensor_tensor(
                            G[:, j, 8:16], G[:, j, 16:24], G[:, j, 0:8], ALU.mult))
                        k.op("v", lambda v: v.drain())
                        k.op("v", lambda v, j=j: v.tensor_reduce(
                            G[:, j, 30:31], G[:, j, 8:16], AX.X, ALU.add))
                        k.op("v", lambda v: v.drain())
                        k.op("v", lambda v, j=j: v.reciprocal(
                            G[:, j, 31:32], G[:, j, 30:31]))
                        k.op("v", lambda v: v.drain())
                        k.op("v", lambda v, j=j: v.tensor_scalar(
                            Gbf[:, j, :], G[:, j, 8:16], G[:, j, 31:32], None,
                            ALU.mult), inc=(j == NT - 1))
                    m_gates = k.cnt["v"]
                    k.wait("t", "v", m_gates)
                    b = bpool.acquire()
                    for j in range(NT):
                        k.op("t", lambda t, j=j, b=b: t.matmul(
                            b["t"][0:8, 128 * j:128 * j + 128], Gbf[:, j, :],
                            idn_bf[:], start=True, stop=True, is_transpose=True),
                            inc=(j == NT - 1))
                    b["wmark"] = k.cnt["t"]
                    bufm["f_GT_t"] = b["wmark"]
                    k.wait("v", "t", b["wmark"])
                    mgt = k.op("v", lambda v, b=b: v.tensor_copy(
                        GT[:, :], b["t"][0:8, 0:128 * NT]), inc=True)
                    b["readers"].append(("v", mgt))
                    if c + 1 < nch:
                        lns[c + 1] = ln_chunk(src, c + 1, (c + 1) % 2)

                    EORD = [NEXP, NEXP + 1] + list(range(NEXP))
                    wmk = {}
                    for pos in range(2):
                        k.wait("g", "t", bufm.get(("f_w13_t", pos % 2), 0))
                        wmk[pos] = [
                            k.dma("g", w1s[:, pos % 2, :, :], ffw13[EORD[pos], 0]),
                            k.dma("g", w3s[:, pos % 2, :, :], ffw13[EORD[pos], 1])]
                    mhe = {}
                    for pos in range(NE_ALL):
                        e = EORD[pos]
                        sl = pos % 2
                        routed = e < NEXP
                        pGb = None
                        if routed:
                            pGb = pool.acquire(hold=True)
                            k.wait("t", "v", mgt)
                            mgb = k.op("t", lambda t, e=e, pGb=pGb: t.matmul(
                                pGb["t"][:, 0:CH], rsel_bf[:, e, :], GT[:, :],
                                start=True, stop=True), inc=True)
                            pGb["wmark"] = mgb
                        k.wait_dmas("t", wmk[pos])
                        lastw_t = 0
                        for hc in range(4):
                            pA_ = pool.acquire()
                            k.wait("t", "v", m_xnT)
                            for ic in range(8):
                                k.op("t", lambda t, sl=sl, hc=hc, ic=ic, pA_=pA_,
                                     slot=slot: t.matmul(
                                    pA_["t"][:, 0:CH],
                                    w1s[:, sl, ic, 128 * hc:128 * hc + 128],
                                    xnT[slot][:, ic, :], start=(ic == 0), stop=(ic == 7)),
                                    inc=(ic == 7))
                            pA_["wmark"] = k.cnt["t"]
                            pB_ = pool.acquire()
                            for ic in range(8):
                                k.op("t", lambda t, sl=sl, hc=hc, ic=ic, pB_=pB_,
                                     slot=slot: t.matmul(
                                    pB_["t"][:, 0:CH],
                                    w3s[:, sl, ic, 128 * hc:128 * hc + 128],
                                    xnT[slot][:, ic, :], start=(ic == 0), stop=(ic == 7)),
                                    inc=(ic == 7))
                            pB_["wmark"] = k.cnt["t"]
                            lastw_t = pB_["wmark"]
                            note_xnT_read(slot, lastw_t)
                            k.wait("s", "t", pA_["wmark"])
                            k.wait("s", "v", bufm.get("f_sA_v", 0))
                            msa = k.op("s", lambda s, pA_=pA_: s.activation(
                                sA[:, :], pA_["t"][:, 0:CH], AF.Silu), inc=True)
                            pA_["readers"].append(("s", msa))
                            k.wait("v", "s", msa)
                            k.wait("v", "t", pB_["wmark"])
                            k.wait("v", "t", bufm.get("f_H1G_t", 0))
                            k.op("v", lambda v: v.drain())
                            if routed:
                                mt1 = k.op("v", lambda v, pB_=pB_: v.scalar_tensor_tensor(
                                    t1[:, :], pB_["t"][:, 0:CH], 1.0, sA[:, :],
                                    ALU.mult, ALU.mult), inc=True)
                                pB_["readers"].append(("v", mt1))
                                bufm["f_sA_v"] = mt1
                                k.op("v", lambda v: v.drain())
                                mh = k.op("v", lambda v, e=e, hc=hc, pGb=pGb: v.tensor_tensor(
                                    H1G[:, e, hc, :], t1[:, :], pGb["t"][:, 0:CH],
                                    ALU.mult), inc=True)
                                pGb["readers"].append(("v", mh))
                            else:
                                mh = k.op("v", lambda v, e=e, hc=hc, pB_=pB_:
                                          v.scalar_tensor_tensor(
                                              H1G[:, e, hc, :], pB_["t"][:, 0:CH], 1.0,
                                              sA[:, :], ALU.mult, ALU.mult), inc=True)
                                pB_["readers"].append(("v", mh))
                                bufm["f_sA_v"] = mh
                        bufm[("f_w13_t", sl)] = lastw_t
                        mhe[e] = k.cnt["v"]
                        if pGb is not None:
                            pool.release(pGb)
                        if pos + 2 < NE_ALL:
                            np_ = pos + 2
                            k.wait("g", "t", bufm.get(("f_w13_t", np_ % 2), 0))
                            wmk[np_] = [
                                k.dma("g", w1s[:, np_ % 2, :, :], ffw13[EORD[np_], 0]),
                                k.dma("g", w3s[:, np_ % 2, :, :], ffw13[EORD[np_], 1])]
                    m_H1G = k.cnt["v"]

                    w2mk = {}
                    for dh in range(2):
                        for pos in range(2):
                            k.wait("g", "t", bufm.get(("f_w2_t", pos % 2), 0))
                            w2mk[pos] = k.dma("g", w2s[:, pos % 2, :, :],
                                              ffw2[EORD[pos], dh])
                        accs = [pool.acquire(hold=True) for _ in range(NT)]
                        m_B = 0
                        for pos in range(NE_ALL):
                            e = EORD[pos]
                            sl = pos % 2
                            k.wait("t", "v", mhe[e])
                            k.wait_dma("t", w2mk[pos])
                            for j in range(NT):
                                for hc in range(4):
                                    k.op("t", lambda t, e=e, sl=sl, j=j, hc=hc,
                                         pos=pos, a_=accs[j]: t.matmul(
                                             a_["t"][:, 0:512],
                                             H1G[:, e, hc, 128 * j:128 * j + 128],
                                             w2s[:, sl, hc, :],
                                             start=(pos == 0 and hc == 0),
                                             stop=(pos == NE_ALL - 1 and hc == 3),
                                             skip_group_check=True),
                                         inc=(j == NT - 1 and hc == 3))
                            bufm[("f_w2_t", sl)] = k.cnt["t"]
                            m_B = k.cnt["t"]
                            if pos + 2 < NE_ALL:
                                np_ = pos + 2
                                k.wait("g", "t", bufm.get(("f_w2_t", np_ % 2), 0))
                                w2mk[np_] = k.dma("g", w2s[:, np_ % 2, :, :],
                                                  ffw2[EORD[np_], dh])
                        for j in range(NT):
                            accs[j]["wmark"] = m_B
                            k.wait("v", "t", m_B)
                            k.wait_dma("v", bufm.get("xout_store"))
                            mev = k.op("v", lambda v, j=j, dh=dh, a_=accs[j],
                                       slot=slot: v.scalar_tensor_tensor(
                                           xout[:, j, 512 * dh:512 * dh + 512],
                                           a_["t"][:, 0:512], 1.0,
                                           xf[slot][:, j, 512 * dh:512 * dh + 512],
                                           ALU.mult, ALU.add), inc=True)
                            accs[j]["readers"].append(("v", mev))
                            pool.release(accs[j])
                    bufm["f_H1G_t"] = k.cnt["t"]
                    bufm[("xf_v", slot)] = k.cnt["v"]
                    last_v = k.cnt["v"]
                    k.wait("y", "v", last_v)
                    mst = k.dma("y", dst[c * CH:(c + 1) * CH, :]
                                .rearrange("(j p) d -> p j d", p=128), xout[:, :, :])
                    bufm["xout_store"] = mst
                    if prologue and c == nch - 1:
                        bufm["src_stores"] = k.snap("y")
                        prologue()
            k.barrier()

        def attn_b(src, dst, prologue=None):
            with contextlib.ExitStack() as _sst:
                wqT = _sst.enter_context(nc.sbuf_tensor("b_wqT", [128, 8, D], BF16))
                wkvK = _sst.enter_context(nc.sbuf_tensor("b_wkvK", [128, 8, 256], BF16))
                wkvV = _sst.enter_context(nc.sbuf_tensor("b_wkvV", [128, 8, 256], BF16))
                woT = _sst.enter_context(nc.sbuf_tensor("b_woT", [128, 8, D], BF16))
                Kz = _sst.enter_context(nc.sbuf_tensor("b_Kz", [128, 4, S], BF16))
                if 2 * S >= 8 * D:
                    wnat = Kz[:, 0:2, :].rearrange("p a (b d) -> p (a b) d",
                                                   b=S // D)[:, 0:8, :]
                else:
                    wnat = _sst.enter_context(
                        nc.sbuf_tensor("b_wnat", [128, 8, D], BF16))
                Vb = _sst.enter_context(nc.sbuf_tensor("b_Vaug", [128, NKT, 4, 65], BF16))
                QTb = _sst.enter_context(nc.sbuf_tensor("b_QT", [128, 8, CH], BF16))
                PT0 = _sst.enter_context(nc.sbuf_tensor("b_PT0", [128, 512], BF16))
                PT1 = _sst.enter_context(nc.sbuf_tensor("b_PT1", [128, 512], BF16))
                PT2 = _sst.enter_context(nc.sbuf_tensor("b_PT2", [128, 512], BF16))
                PT3 = _sst.enter_context(nc.sbuf_tensor("b_PT3", [128, 512], BF16))
                btok = _sst.enter_context(nc.sbuf_tensor("b_tok", [128, 4, 64], BF16))
                brec = _sst.enter_context(nc.sbuf_tensor("b_rec", [128, 4], F32))
                PTs = [PT0, PT1, PT2, PT3]

                def load_transpose(nat_src, n_oc, evac_plan, region=None,
                                   preloaded=None):
                    region = wnat if region is None else region
                    if preloaded is None:
                        k.wait("g", "t", k.cnt["t"])
                        mldw = k.dma("g", region[:, 0:n_oc, :], nat_src)
                    else:
                        mldw = preloaded
                    k.wait_dma("t", mldw)
                    wnat_ = region
                    for a in range(n_oc):
                        for icb in range(8):
                            b = bpool.acquire()
                            k.op("t", lambda t, a=a, icb=icb, b=b, wnat_=wnat_:
                                 t.matmul(
                                     b["t"][:, 0:128],
                                     wnat_[:, a, 128 * icb:128 * icb + 128],
                                     idn_bf[:], start=True, stop=True,
                                     is_transpose=True), inc=True)
                            b["wmark"] = k.cnt["t"]
                            k.wait("v", "t", b["wmark"])
                            for dst_ap, c0, c1 in evac_plan(a, icb):
                                mev = k.op("v", lambda v, dst_ap=dst_ap, c0=c0, c1=c1,
                                           b=b: v.tensor_copy(dst_ap, b["t"][:, c0:c1]),
                                           inc=True)
                                b["readers"].append(("v", mev))
                    return k.cnt["v"]

                def wq_plan(a, icb):
                    h0, h1 = 2 * a, 2 * a + 1
                    return [(wqT[:, icb, 64 * c64(h0):64 * c64(h0) + 64], 0, 64),
                            (wqT[:, icb, 64 * c64(h1):64 * c64(h1) + 64], 64, 128)]
                wkv_region = Kz[:, 2:3, :].rearrange("p a (b d) -> p (a b) d",
                                                      b=S // D)[:, 0:4, :] \
                    if S >= 4 * D else None
                mq_pre = k.dma("g", wnat[:, 0:8, :],
                               ab_Wq.rearrange("(a p) i -> p a i", p=128))
                mkv_pre = (k.dma("g", wkv_region[:, 0:4, :],
                                 ab_Wkv.rearrange("(a p) i -> p a i", p=128))
                           if wkv_region is not None else None)
                mwq = load_transpose(None, 8, wq_plan, preloaded=mq_pre)

                def wkv_plan(a, icb):
                    return [(wkvK[:, icb, 64 * a:64 * a + 64], 0, 64),
                            (wkvV[:, icb, 64 * a:64 * a + 64], 64, 128)]
                if wkv_region is not None:
                    mwkv = load_transpose(None, 4, wkv_plan, region=wkv_region,
                                          preloaded=mkv_pre)
                else:
                    mwkv = load_transpose(ab_Wkv.rearrange("(a p) i -> p a i", p=128),
                                          4, wkv_plan)

                def wo_plan(a, icb):
                    return [(woT[:, icb, 128 * a:128 * a + 128], 0, 128)]
                mwo = load_transpose(ab_Wo.rearrange("(a p) i -> p a i", p=128), 8, wo_plan)

                k.op("v", lambda v: v.memset(Vb[:, :, :, 64:65], 1.0))
                for g_ in range(4):
                    lo = 64 * (1 - (g_ % 2))
                    k.op("v", lambda v, g_=g_, lo=lo: v.memset(
                        Kz[lo:lo + 64, g_, :], 0.0))
                k.op("v", lambda v: v.drain())
                m_vb1 = k.op("v", lambda v: v.nop(), inc=True)
                k.wait_dma("v", m_al)

                lns = {0: carry.pop("ln0")} if "ln0" in carry else \
                    {0: ln_chunk(src, 0, 0)}
                for c in range(nch):
                    slot = c % 2
                    mld, m_xnT = lns.pop(c)
                    k.wait("s", "t", bufm.get("b_QT_t", 0))
                    for s_ in range(8):
                        p = proj_fm(wqT, 128 * s_, m_xnT, slot, bias=None)
                        k.wait("s", "t", p["wmark"])
                        msq = k.op("s", lambda s, s_=s_, p=p: s.activation(
                            QTb[:, s_, :], p["t"][:, 0:CH], AF.Copy, scale=SCALE),
                            inc=True)
                        p["readers"].append(("s", msq))
                    for pr in range(2):
                        p = proj_fm(wkvK, 128 * pr, m_xnT, slot, bias=None)
                        k.wait("v", "t", p["wmark"])
                        k.wait("v", "v", m_vb1)
                        k.op("v", lambda v, pr=pr, c=c, p=p: v.tensor_copy(
                            Kz[0:64, 2 * pr, c * CH:(c + 1) * CH],
                            p["t"][0:64, 0:CH]))
                        mkv = k.op("v", lambda v, pr=pr, c=c, p=p: v.tensor_copy(
                            Kz[64:128, 2 * pr + 1, c * CH:(c + 1) * CH],
                            p["t"][64:128, 0:CH]), inc=True)
                        p["readers"].append(("v", mkv))
                    for j in range(NT):
                        p = pool.acquire()
                        k.wait("t", "v", m_xnT)
                        for ic in range(8):
                            k.op("t", lambda t, j=j, ic=ic, p=p, slot=slot: t.matmul(
                                p["t"][:, 0:256],
                                xnT[slot][:, ic, 128 * j:128 * j + 128],
                                wkvV[:, ic, :], start=(ic == 0), stop=(ic == 7)),
                                inc=(ic == 7))
                        note_xnT_read(slot, k.cnt["t"])
                        p["wmark"] = k.cnt["t"]
                        k.wait("v", "t", p["wmark"])
                        k.wait("v", "v", m_vb1)
                        mkv = k.op("v", lambda v, j=j, c=c, p=p: v.tensor_copy(
                            Vb[:, NT * c + j, :, 0:64],
                            p["t"][:, 0:256].rearrange("p (a d) -> p a d", a=4)),
                            inc=True)
                        p["readers"].append(("v", mkv))
                    m_kv = k.cnt["v"]
                    m_q = k.cnt["s"]
                    if c + 1 < nch:
                        lns[c + 1] = ln_chunk(src, c + 1, (c + 1) % 2)

                    def b_tail(pO, h, qci, qc, nqt, qw):
                        k.wait("v", "t", pO["wmark"])
                        k.wait("v", "t", bufm.get("b_tok_t", 0))
                        k.op("v", lambda v, nqt=nqt, pO=pO: v.reciprocal(
                            brec[:, 0:nqt], pO["t"][:, 0:65 * nqt].rearrange(
                                "p (a b) -> p a b", b=65)[:, :, 64]))
                        k.op("v", lambda v: v.drain())
                        for qt in range(nqt):
                            mdv = k.op("v", lambda v, qt=qt, qw=qw, pO=pO:
                                       v.tensor_scalar(
                                           btok[0:qw, qt, :],
                                           pO["t"][0:qw, 65 * qt:65 * qt + 64],
                                           brec[0:qw, qt:qt + 1], None, ALU.mult),
                                       inc=(qt == nqt - 1))
                        pO["readers"].append(("v", mdv))
                        pool.release(pO)
                        k.wait("t", "v", mdv)
                        b = bpool.acquire()
                        for qt in range(nqt):
                            k.op("t", lambda t, qt=qt, qw=qw, b=b: t.matmul(
                                b["t"][0:64, 128 * qt:128 * qt + qw],
                                btok[0:qw, qt, :], idn_bf[0:qw, 0:qw],
                                start=True, stop=True, is_transpose=True),
                                inc=(qt == nqt - 1))
                        b["wmark"] = k.cnt["t"]
                        bufm["b_tok_t"] = b["wmark"]
                        k.wait("v", "t", b["wmark"])
                        k.wait("v", "t", bufm.get("catT_t", 0))
                        mct = k.op("v", lambda v, h=h, qci=qci, qc=qc, nqt=nqt,
                                   qw=qw, b=b: v.tensor_copy(
                                       catT[64 * (h % 2):64 * (h % 2) + 64, h // 2,
                                            qci * qc:qci * qc + qc].rearrange(
                                           "p (a q) -> p a q", a=nqt),
                                       b["t"][0:64, 0:128 * nqt].rearrange(
                                           "p (a q) -> p a q", a=nqt)[:, :, 0:qw]),
                                   inc=True)
                        b["readers"].append(("v", mct))

                    pending = None
                    for h in range(NH):
                        g = h // 4
                        s_, hi = b_slot(h)
                        base = 64 * hi
                        slope = 2.0 ** (-(h + 1) / 2.0)
                        qc = qc_size(h)
                        nqt = max(1, qc // 128)
                        qw = min(qc, 128)
                        for qci in range(CH // qc):
                            q0 = c * CH + qci * qc
                            kts = [kt for kt in range((q0 + qc + 127) // 128)
                                   if kt_keep(slope, q0, qc, kt)]
                            pO = pool.acquire(hold=True)
                            bank_started = [False]

                            def b_scores(ki, kt, q0=q0, qci=qci, qc=qc, g=g, s_=s_,
                                         base=base, h=h, qw=qw):
                                j0 = 128 * max(0, (128 * kt - q0 - qw + 128) // 128)
                                pS = pool.acquire()
                                k.wait("t", "v", m_kv)
                                k.wait("t", "s", m_q)
                                mS = k.op("t", lambda t, g=g, s_=s_, kt=kt, j0=j0,
                                          q0=q0, qci=qci, qc=qc, pS=pS: t.matmul(
                                              pS["t"][:, j0:qc],
                                              Kz[:, g, 128 * kt:128 * kt + 128],
                                              QTb[:, s_,
                                                  qci * qc + j0:qci * qc + qc],
                                              start=True, stop=True,
                                              skip_group_check=True), inc=True)
                                pS["wmark"] = mS
                                if 128 * kt + 127 >= q0 + j0:
                                    k.wait("v", "t", mS)
                                    mtri = k.op("v", lambda v, kt=kt, q0=q0, qc=qc,
                                                j0=j0, pS=pS: v.tensor_tensor(
                                                    pS["t"][:, j0:qc],
                                                    pS["t"][:, j0:qc],
                                                    tri_bf[:, 512 + q0 - 128 * kt + j0:
                                                           512 + q0 - 128 * kt + qc],
                                                    ALU.add), inc=True)
                                    pS["readers"].append(("v", mtri))
                                    k.wait("s", "v", mtri)
                                k.wait("s", "t", mS)
                                k.wait("s", "t", bufm.get(("b_PT_t", ki % 4), 0))
                                col = al_cols[(h, 128 * kt - q0 - qc // 2)]
                                msx = k.op("s", lambda s, qc=qc, col=col, ki=ki,
                                           j0=j0, pS=pS: s.activation(
                                               PTs[ki % 4][:, j0:qc],
                                               pS["t"][:, j0:qc],
                                               AF.Exp, bias=alibi_sb[:, col:col + 1]),
                                           inc=True)
                                pS["readers"].append(("s", msx))
                                return msx

                            def b_pv(ki, kt, msx, q0=q0, qc=qc, g=g, qw=qw, nqt=nqt,
                                     pO=pO):
                                k.wait("t", "s", msx)
                                qts = [qt for qt in range(nqt)
                                       if 128 * kt <= q0 + 128 * qt + qw - 1]
                                for qi, qt in enumerate(qts):
                                    st = not bank_started[0]
                                    bank_started[0] = True
                                    k.op("t", lambda t, qt=qt, kt=kt, ki=ki, g=g,
                                         qw=qw, st=st, pO=pO: t.matmul(
                                             pO["t"][0:qw, 65 * qt:65 * qt + 65],
                                             PTs[ki % 4][:, 128 * qt:128 * qt + qw],
                                             Vb[:, kt, g, :],
                                             start=st, stop=False,
                                             skip_group_check=True),
                                         inc=(qi == len(qts) - 1))
                                bufm[("b_PT_t", ki % 4)] = k.cnt["t"]

                            from collections import deque as _dq
                            pend = _dq()
                            for ki, kt in enumerate(kts):
                                msx = b_scores(ki, kt)
                                pend.append((ki, kt, msx))
                                if len(pend) > 3:
                                    b_pv(*pend.popleft())
                            while pend:
                                b_pv(*pend.popleft())
                            pO["wmark"] = k.cnt["t"]
                            if pending is not None:
                                b_tail(*pending)
                            pending = (pO, h, qci, qc, nqt, qw)
                    if pending is not None:
                        b_tail(*pending)
                        pending = None
                    m_cat_v = k.cnt["v"]
                    bufm["b_QT_t"] = k.cnt["t"]
                    if not os.environ.get("MK_NOWO"):
                        wo_resid_store(woT, None, dst, c, slot, m_cat_v)
                    bufm["catT_t"] = k.cnt["t"]
                    if prologue and c == nch - 1:
                        bufm["src_stores"] = k.snap("y")
                        prologue()
            k.barrier()

        srcs = {0: x_in, 1: Xa, 2: Xb, 3: Xa}
        dsts = {0: Xa, 1: Xb, 2: Xa, 3: y_out}
        order = [s_ for s_ in (0, 1, 2, 3) if s_ in subset]
        carry = {}
        plan = []
        for idx, s_ in enumerate(order):
            src = x_in if idx == 0 else srcs[s_]
            dst = y_out if idx == len(order) - 1 else dsts[s_]
            plan.append((s_, src, dst))
        for idx, (s_, src, dst) in enumerate(plan):
            if idx + 1 < len(plan):
                nxt_src = plan[idx + 1][1]

                def prologue(nxt_src=nxt_src):
                    carry["ln0"] = ln_chunk(nxt_src, 0, 0)
            else:
                prologue = None
            if s_ == 0:
                attn_a(src, dst, prologue)
            elif s_ in (1, 3):
                ffn(src, dst, s_, prologue)
            else:
                attn_b(src, dst, prologue)

        for q in "ygvs":
            for lane, cntv in enumerate(k.dcnt[q]):
                k.wait_dma(q, (q, lane, cntv))
        k.flush_all()

    return nc


def build_inputs(x_core, w, S, CH):
    hc = host_consts(S, CH)
    f32 = np.float32
    return {
        "x": np.ascontiguousarray(x_core, f32),
        "aa_Wq": w["L0_attn_a_Wq"], "aa_bq": w["L0_attn_a_bq"],
        "aa_Wk": w["L0_attn_a_Wk"], "aa_bk": w["L0_attn_a_bk"],
        "aa_Wv": w["L0_attn_a_Wv"], "aa_bv": w["L0_attn_a_bv"],
        "aa_Wo": w["L0_attn_a_Wo"], "aa_bo": w["L0_attn_a_bo"],
        "ab_Wq": w["L0_attn_b_Wq"], "ab_Wkv": w["L0_attn_b_Wkv"],
        "ab_Wo": w["L0_attn_b_Wo"],
        "f_rw": w["L0_ffn_router_w"], "f_rb": w["L0_ffn_router_b"],
        "f_w1": w["L0_ffn_routed_w1"], "f_w3": w["L0_ffn_routed_w3"],
        "f_w2": w["L0_ffn_routed_w2"],
        "f_s1": w["L0_ffn_shared_w1"], "f_s3": w["L0_ffn_shared_w3"],
        "f_s2": w["L0_ffn_shared_w2"],
        "idn": hc["idn"], "tri": hc["tri"], "oh8": hc["oh8"],
        "rowsel": hc["rowsel"], "alibi": hc["alibi"],
    }


_S, _CH = 4096, 512
_NCORES = 8


def kernel(**inputs):
    x = np.ascontiguousarray(inputs["L0_attn_a_x"], np.float32)
    hc = host_consts(_S, _CH)
    shared = {
        "aa_Wq": inputs["L0_attn_a_Wq"], "aa_bq": inputs["L0_attn_a_bq"],
        "aa_Wk": inputs["L0_attn_a_Wk"], "aa_bk": inputs["L0_attn_a_bk"],
        "aa_Wv": inputs["L0_attn_a_Wv"], "aa_bv": inputs["L0_attn_a_bv"],
        "aa_Wo": inputs["L0_attn_a_Wo"], "aa_bo": inputs["L0_attn_a_bo"],
        "ab_Wq": inputs["L0_attn_b_Wq"], "ab_Wkv": inputs["L0_attn_b_Wkv"],
        "ab_Wo": inputs["L0_attn_b_Wo"],
        "f_rw": inputs["L0_ffn_router_w"], "f_rb": inputs["L0_ffn_router_b"],
        "f_w1": inputs["L0_ffn_routed_w1"], "f_w3": inputs["L0_ffn_routed_w3"],
        "f_w2": inputs["L0_ffn_routed_w2"],
        "f_s1": inputs["L0_ffn_shared_w1"], "f_s3": inputs["L0_ffn_shared_w3"],
        "f_s2": inputs["L0_ffn_shared_w2"],
        "idn": hc["idn"], "tri": hc["tri"], "oh8": hc["oh8"],
        "rowsel": hc["rowsel"], "alibi": hc["alibi"],
    }
    shared = {k_: np.ascontiguousarray(v_, np.float32) for k_, v_ in shared.items()}
    nc = build_core(S=_S, CH=_CH, subset=(0, 1, 2, 3))
    in_maps = [dict(shared, x=x[b]) for b in range(_NCORES)]
    res = run_bass_kernel_spmd(nc, in_maps, core_ids=list(range(_NCORES)))
    return np.stack([res.results[b]["y"] for b in range(_NCORES)], axis=0)


# version 22
# speedup vs baseline: 1.4378x; 1.0155x over previous
import math
import numpy as np
import ml_dtypes
import contextlib
import os
import concourse.bass as bass
import concourse.mybir as mybir
from concourse.bass_utils import run_bass_kernel_spmd

F32 = mybir.dt.float32
BF16 = mybir.dt.bfloat16
AF = mybir.ActivationFunctionType
ALU = mybir.AluOpType
AX = mybir.AxisListType

NEG = -1.0e30
EPS = 1.0e-5


def kt_keep(slope, q0, qc, kt):
    thr = 110.0 - slope * (qc / 2.0)
    return slope * (q0 - (128 * kt + 127)) <= max(thr, 0.0)

def b_slot(h):
    if h < 4:
        return h, 0
    if h < 8:
        return h - 4, 1
    if h < 12:
        return h - 4, 0
    return h - 8, 1


def c64(h):
    s, hi = b_slot(h)
    return 2 * s + hi


def gpos_list(S, ng=8):
    return [int(x) for x in np.linspace(0, S - 1, ng).astype(np.int64)]


def qc_size(h):
    slope = 2.0 ** (-(h + 1) / 2.0)
    for qc in (512, 256, 128):
        if slope * (qc // 2) <= 64.0:
            return qc
    return 64


def alibi_plan(S, CH):
    cols, vals = {}, []
    for h in range(16):
        qc = qc_size(h)
        slope = 2.0 ** (-(h + 1) / 2.0)
        for c in range(S // CH):
            for qci in range(CH // qc):
                q0 = c * CH + qci * qc
                for kt in range((q0 + qc + 127) // 128):
                    if not kt_keep(slope, q0, qc, kt):
                        continue
                    key = (h, 128 * kt - q0 - qc // 2)
                    if key not in cols:
                        cols[key] = len(vals)
                        vals.append(slope * (128 * kt - q0 - qc // 2
                                             + np.arange(128, dtype=np.float64)))
    tab = np.array(vals, dtype=np.float32).T if vals else np.zeros((128, 1), np.float32)
    return cols, tab


def host_consts(S, CH):
    idn = np.eye(128, dtype=np.float32)
    tri = np.full((128, 1152), NEG, np.float32)
    for p in range(128):
        tri[p, 512 + p:] = 0.0
    oh8 = np.zeros((1, 64), np.float32)
    for g in range(8):
        oh8[0, 8 * g + g] = 1.0
    rowsel = np.zeros((8, 8, 128), np.float32)
    for e in range(8):
        rowsel[e, e, :] = 1.0
    al_cols, al_tab = alibi_plan(S, CH)
    return {"idn": idn, "tri": tri, "oh8": oh8, "rowsel": rowsel,
            "alibi": al_tab, "al_cols": al_cols}


class K:
    ENG_BLOCK = {"t": "tensor", "v": "vector", "s": "scalar", "g": "gpsimd", "y": "sync"}

    FLUSH_N = 128

    def __init__(self, nc, block, sems, dsems):
        self.nc, self.block = nc, block
        self._buf = {e: [] for e in "tvsgy"}
        self.sem = sems
        self.dsem = dsems
        self.cnt = {e: 0 for e in "tvsgy"}
        self.dcnt = {q: [0] * len(dsems[q]) for q in dsems}
        self.dlane = {q: 0 for q in dsems}
        self._waited = {}
        self._had_wait = {e: True for e in "tvsgy"}

    def _blk(self, e):
        return getattr(self.block, self.ENG_BLOCK[e])

    def _emit(self, e, fn):
        self._buf[e].append(fn)
        if len(self._buf[e]) >= self.FLUSH_N:
            self.flush(e)

    def flush(self, e):
        fns = self._buf[e]
        if not fns:
            return
        self._buf[e] = []

        def body(eng):
            for f in fns:
                f(eng)
        self._blk(e)(body)

    def flush_all(self):
        for e in "tvsgy":
            self.flush(e)

    def op(self, e, fn, inc=False):
        marker = None
        if inc:
            prev = self.cnt[e]
            self.cnt[e] += 1
            marker = self.cnt[e]
            sem = self.sem[e]
            boundary = not self._had_wait[e]

            def body(eng):
                if boundary:
                    eng.wait_ge(sem, prev)
                instr = fn(eng)
                assert instr is not None, "inc op must return its instruction"
                instr.then_inc(sem, 1)
            self._emit(e, body)
            self._had_wait[e] = False
        else:
            self._emit(e, fn)
        return marker

    def wait(self, e, prod, val):
        if not val or e == prod:
            return
        key = (e, "c", prod)
        if not os.environ.get("MK_NODEDUP") and self._waited.get(key, 0) >= val:
            return
        self._waited[key] = val
        sem = self.sem[prod]
        self._emit(e, lambda eng: eng.wait_ge(sem, val))
        self._had_wait[e] = True

    def dma(self, q, out, in_, **kw):
        lane = self.dlane[q]
        self.dlane[q] = (lane + 1) % len(self.dsem[q])
        sem = self.dsem[q][lane]
        prev = self.dcnt[q][lane]
        self.dcnt[q][lane] = prev + 16
        marker = (q, lane, prev + 16)

        def body(eng):
            if prev > 0:
                eng.wait_ge(sem, prev)
            eng.dma_start(out=out, in_=in_, **kw).then_inc(sem, 16)
        self._emit(q, body)
        key = (q, "d", q, lane)
        if prev > 0 and self._waited.get(key, 0) < prev:
            self._waited[key] = prev
        return marker

    def snap(self, q):
        return tuple((q, lane, c) for lane, c in enumerate(self.dcnt[q]) if c > 0)

    def wait_dmas(self, e, markers):
        for m in markers:
            self.wait_dma(e, m)

    def wait_dma(self, e, marker):
        if not marker:
            return
        q, lane, val = marker
        key = (e, "d", q, lane)
        if self._waited.get(key, 0) >= val:
            return
        self._waited[key] = val
        sem = self.dsem[q][lane]
        self._emit(e, lambda eng: eng.wait_ge(sem, val))
        self._had_wait[e] = True

    def barrier(self, engines="tvsgy"):
        marks = {}
        for e in engines:
            if e in "ygvs":
                for lane, cntv in enumerate(self.dcnt[e]):
                    self.wait_dma(e, (e, lane, cntv))
            marks[e] = self.op(e, lambda eng: eng.nop(), inc=True)
        for e in engines:
            for p in engines:
                if p != e:
                    self.wait(e, p, marks[p])
        self.flush_all()


class PsumPool:
    def __init__(self, k, tensors):
        self.k = k
        self.slots = [{"t": t, "readers": [], "wmark": 0, "held": False}
                      for t in tensors]
        self.i = 0

    def acquire(self, hold=False):
        for _ in range(len(self.slots)):
            s = self.slots[self.i % len(self.slots)]
            self.i += 1
            if s["held"]:
                continue
            for (eng, m) in s["readers"]:
                self.k.wait("t", eng, m)
            s["readers"] = []
            s["held"] = hold
            return s
        raise RuntimeError("psum pool exhausted (all held)")

    def release(self, s):
        s["held"] = False


def build_core(S=4096, CH=512, subset=(0, 1, 2, 3)):
    D, NH, HD, NKV = 1024, 16, 64, 4
    NEXP, NSH, HID = 8, 2, 512
    NE_ALL = NEXP + NSH
    NG = 8
    nch = S // CH
    NT = CH // 128
    NKT = S // 128
    gpos = gpos_list(S, NG)
    SCALE = 1.0 / math.sqrt(HD)
    al_cols, al_tab = alibi_plan(S, CH)
    NAL = al_tab.shape[1]
    run_ffn = (1 in subset) or (3 in subset)

    nc = bass.Bass("TRN2", target_bir_lowering=False, debug=False)

    x_in = nc.dram_tensor("x", [S, D], F32, kind="ExternalInput").ap()
    y_out = nc.dram_tensor("y", [S, D], F32, kind="ExternalOutput").ap()
    Xa = nc.dram_tensor("Xa", [S, D], F32).ap()
    Xb = nc.dram_tensor("Xb", [S, D], F32).ap()

    aw = {n: nc.dram_tensor("aa_" + n, [D, D], F32, kind="ExternalInput").ap()
          for n in ("Wq", "Wk", "Wv", "Wo")}
    abias = {n: nc.dram_tensor("aa_" + n, [D], F32, kind="ExternalInput").ap()
             for n in ("bq", "bk", "bv", "bo")}
    ab_Wq = nc.dram_tensor("ab_Wq", [D, D], F32, kind="ExternalInput").ap()
    ab_Wkv = nc.dram_tensor("ab_Wkv", [512, D], F32, kind="ExternalInput").ap()
    ab_Wo = nc.dram_tensor("ab_Wo", [D, D], F32, kind="ExternalInput").ap()
    f_rw = nc.dram_tensor("f_rw", [D, NEXP], F32, kind="ExternalInput").ap()
    f_rb = nc.dram_tensor("f_rb", [NEXP], F32, kind="ExternalInput").ap()
    f_w1 = nc.dram_tensor("f_w1", [NEXP, D, HID], F32, kind="ExternalInput").ap()
    f_w3 = nc.dram_tensor("f_w3", [NEXP, D, HID], F32, kind="ExternalInput").ap()
    f_w2 = nc.dram_tensor("f_w2", [NEXP, HID, D], F32, kind="ExternalInput").ap()
    f_s1 = nc.dram_tensor("f_s1", [NSH, D, HID], F32, kind="ExternalInput").ap()
    f_s3 = nc.dram_tensor("f_s3", [NSH, D, HID], F32, kind="ExternalInput").ap()
    f_s2 = nc.dram_tensor("f_s2", [NSH, HID, D], F32, kind="ExternalInput").ap()

    idn_in = nc.dram_tensor("idn", [128, 128], F32, kind="ExternalInput").ap()
    tri_in = nc.dram_tensor("tri", [128, 1152], F32, kind="ExternalInput").ap()
    oh8_in = nc.dram_tensor("oh8", [1, 64], F32, kind="ExternalInput").ap()
    rsel_in = nc.dram_tensor("rowsel", [8, 8, 128], F32, kind="ExternalInput").ap()
    alibi_in = nc.dram_tensor("alibi", [128, NAL], F32, kind="ExternalInput").ap()

    ffw13 = nc.dram_tensor("ffw13", [NE_ALL, 2, 128, 8, HID], BF16).ap()
    ffw2 = nc.dram_tensor("ffw2", [NE_ALL, 2, 128, 4, 512], BF16).ap()

    with contextlib.ExitStack() as _st:
        cvals = _st.enter_context(nc.sbuf_tensor("cvals", [128, 2], F32))
        idn_bf = _st.enter_context(nc.sbuf_tensor("idn_bf", [128, 128], BF16))
        tri_bf = _st.enter_context(nc.sbuf_tensor("tri_bf", [128, 1152], BF16))
        oh8_bf = _st.enter_context(nc.sbuf_tensor("oh8_bf", [1, 64], BF16))
        rsel_bf = _st.enter_context(nc.sbuf_tensor("rsel_bf", [8, 8, 128], BF16))
        ones_bf = _st.enter_context(nc.sbuf_tensor("ones_bf", [1, 512], BF16))
        neg_bf = _st.enter_context(nc.sbuf_tensor("neg_bf", [1, 128], BF16))
        alibi_sb = _st.enter_context(nc.sbuf_tensor("alibi_sb", [128, NAL], F32))
        brows = _st.enter_context(nc.sbuf_tensor("brows", [1, 5, 1024], BF16))
        xf0 = _st.enter_context(nc.sbuf_tensor("xf0", [128, NT, D], F32))
        xf1 = _st.enter_context(nc.sbuf_tensor("xf1", [128, NT, D], F32))
        xn_sb = _st.enter_context(nc.sbuf_tensor("xn", [128, NT, D], BF16))
        xnT0 = _st.enter_context(nc.sbuf_tensor("xnT0", [128, 8, CH], BF16))
        xnT1 = _st.enter_context(nc.sbuf_tensor("xnT1", [128, 8, CH], BF16))
        xout = _st.enter_context(nc.sbuf_tensor("xout", [128, NT, D], F32))
        stats = _st.enter_context(nc.sbuf_tensor("stats", [128, NT, 12], F32))
        mv = _st.enter_context(nc.sbuf_tensor("mv", [128, NT, 2], F32))
        rstd = _st.enter_context(nc.sbuf_tensor("rstd", [128, NT], F32))
        lntmp = _st.enter_context(nc.sbuf_tensor("lntmp", [128, NT], F32))
        catT = _st.enter_context(nc.sbuf_tensor("catT", [128, 8, CH], BF16))
        pf0 = _st.enter_context(nc.psum_tensor("pf0", [128, 512], F32))
        pf1 = _st.enter_context(nc.psum_tensor("pf1", [128, 512], F32))
        pf2 = _st.enter_context(nc.psum_tensor("pf2", [128, 512], F32))
        pf3 = _st.enter_context(nc.psum_tensor("pf3", [128, 512], F32))
        pf4 = _st.enter_context(nc.psum_tensor("pf4", [128, 512], F32))
        pf5 = _st.enter_context(nc.psum_tensor("pf5", [128, 512], F32))
        pb0 = _st.enter_context(nc.psum_tensor("pb0", [128, 1024], BF16))
        pb1 = _st.enter_context(nc.psum_tensor("pb1", [128, 1024], BF16))
        sem_t = _st.enter_context(nc.semaphore("sem_t"))
        sem_v = _st.enter_context(nc.semaphore("sem_v"))
        sem_s = _st.enter_context(nc.semaphore("sem_s"))
        sem_g = _st.enter_context(nc.semaphore("sem_g"))
        sem_y = _st.enter_context(nc.semaphore("sem_y"))
        dma_y = [_st.enter_context(nc.semaphore(f"dma_y{i}")) for i in range(6)]
        dma_g = [_st.enter_context(nc.semaphore(f"dma_g{i}")) for i in range(6)]
        dma_v = [_st.enter_context(nc.semaphore("dma_v0"))]
        dma_s = [_st.enter_context(nc.semaphore("dma_s0"))]
        block = _st.enter_context(nc.Block())
        k = K(nc, block,
              {"t": sem_t, "v": sem_v, "s": sem_s, "g": sem_g, "y": sem_y},
              {"y": dma_y, "g": dma_g, "v": dma_v, "s": dma_s})
        pool = PsumPool(k, [pf0, pf1, pf2, pf3, pf4, pf5])
        bpool = PsumPool(k, [pb0, pb1])
        xf = [xf0, xf1]
        xnT = [xnT0, xnT1]
        bufm = {}

        for src, dst in ((idn_in[:, :], idn_bf[:]), (tri_in[:, :], tri_bf[:]),
                         (oh8_in[:, :], oh8_bf[:]), (rsel_in[:, :, :], rsel_bf[:])):
            k.dma("g", dst, src)
        m_al = k.dma("y", alibi_sb[:], alibi_in[:, :])
        for i, src in enumerate((abias["bq"], abias["bk"], abias["bv"], abias["bo"])):
            k.dma("g", brows[0:1, i, :], src.rearrange("(a d) -> a d", a=1))
        k.dma("g", brows[0:1, 4, 0:NEXP], f_rb.rearrange("(a d) -> a d", a=1))
        m_consts_g = k.snap("g")
        nc.const_aps.register(F32, 0.0, cvals[:, 0:1])
        nc.const_aps.register(F32, EPS, cvals[:, 1:2])
        k.op("v", lambda v: v.memset(cvals[:, 0:1], 0.0))
        k.op("v", lambda v: v.memset(cvals[:, 1:2], EPS))
        k.op("v", lambda v: v.memset(ones_bf[:], 1.0))
        k.op("v", lambda v: v.drain())
        m_consts_v = k.op("v", lambda v: v.memset(neg_bf[:], NEG), inc=True)

        m_prestage = 0
        if run_ffn:
            for e in range(NE_ALL):
                s1 = f_w1[e] if e < NEXP else f_s1[e - NEXP]
                s3 = f_w3[e] if e < NEXP else f_s3[e - NEXP]
                s2 = f_w2[e] if e < NEXP else f_s2[e - NEXP]
                k.dma("g", ffw13[e, 0], s1.rearrange("(a p) h -> p a h", p=128))
                k.dma("g", ffw13[e, 1], s3.rearrange("(a p) h -> p a h", p=128))
                for b_ in range(2):
                    k.dma("g", ffw2[e, b_],
                          s2[:, 512 * b_:512 * b_ + 512].rearrange(
                              "(a p) d -> p a d", p=128))
            m_prestage = k.snap("g")

        def ln_chunk(src, c, slot):
            k.wait("y", "v", bufm.get(("xf_v", slot), 0))
            k.wait_dmas("y", bufm.get("src_stores", ()))
            mld = k.dma("y", xf[slot][:, :, :],
                        src[c * CH:(c + 1) * CH, :].rearrange("(j p) d -> p j d", p=128))
            k.wait_dma("v", mld)
            k.wait("v", "s", bufm.get("lnscalar", 0))
            for j in range(NT):
                k.op("v", lambda v, j=j: v.bn_stats(stats[:, j, 0:6], xf[slot][:, j, 0:512]))
                k.op("v", lambda v, j=j: v.bn_stats(stats[:, j, 6:12], xf[slot][:, j, 512:1024]))
            k.op("v", lambda v: v.drain())
            for j in range(NT):
                k.op("v", lambda v, j=j: v.bn_aggr(mv[:, j, :], stats[:, j, :]),
                     inc=(j == NT - 1))
            m_mv = k.cnt["v"]
            k.wait("s", "v", m_mv)
            k.op("s", lambda s: s.activation(lntmp[:, :], mv[:, :, 1], AF.Ln, bias=EPS))
            k.op("s", lambda s: s.drain())
            m_rstd = k.op("s", lambda s: s.activation(rstd[:, :], lntmp[:, :], AF.Exp,
                                                      scale=-0.5), inc=True)
            bufm["lnscalar"] = m_rstd
            k.wait("v", "s", m_rstd)
            k.wait("v", "t", bufm.get("xn_t", 0))
            for j in range(NT):
                k.op("v", lambda v, j=j, slot=slot: v.tensor_scalar(
                    xn_sb[:, j, :], xf[slot][:, j, :], mv[:, j, 0:1], rstd[:, j:j + 1],
                    ALU.subtract, ALU.mult), inc=(j == NT - 1))
            m_xn = k.cnt["v"]
            k.wait("t", "v", m_xn)
            for j in range(NT):
                b = bpool.acquire()
                for dc in range(8):
                    k.op("t", lambda t, j=j, dc=dc, b=b: t.matmul(
                        b["t"][:, 128 * dc:128 * dc + 128],
                        xn_sb[:, j, 128 * dc:128 * dc + 128], idn_bf[:],
                        start=True, stop=True, is_transpose=True), inc=(dc == 7))
                b["wmark"] = k.cnt["t"]
                k.wait("v", "t", b["wmark"])
                k.wait("v", "t", bufm.get(("xnT_t", slot), 0))
                mm = k.op("v", lambda v, j=j, b=b, slot=slot: v.tensor_copy(
                    xnT[slot][:, :, 128 * j:128 * j + 128],
                    b["t"][:].rearrange("p (a q) -> p a q", a=8)), inc=True)
                b["readers"].append(("v", mm))
            bufm["xn_t"] = k.cnt["t"]
            return mld, k.cnt["v"]

        def note_xnT_read(slot, marker):
            key = ("xnT_t", slot)
            bufm[key] = max(bufm.get(key, 0), marker)

        def proj_fm(wview, col0, m_xnT, slot, bias=None):
            p = pool.acquire()
            k.wait("t", "v", m_xnT)
            for ic in range(8):
                last = (ic == 7) and bias is None
                k.op("t", lambda t, ic=ic, p=p, last=last, slot=slot: t.matmul(
                    p["t"][:, 0:CH], wview[:, ic, col0:col0 + 128],
                    xnT[slot][:, ic, :], start=(ic == 0), stop=last), inc=last)
            if bias is not None:
                bi = bias
                k.op("t", lambda t, p=p: t.matmul(
                    p["t"][:, 0:CH], brows[0:1, bi, col0:col0 + 128],
                    ones_bf[0:1, 0:CH], start=False, stop=True), inc=True)
            p["wmark"] = k.cnt["t"]
            note_xnT_read(slot, p["wmark"])
            return p

        def wo_resid_store(wo_view, bias_i, dst, c, slot, m_cat_v):
            k.wait("t", "v", m_cat_v)
            last_v = 0
            for j in range(NT):
                for eh in range(2):
                    p = pool.acquire()
                    for ic in range(8):
                        last = (ic == 7) and bias_i is None
                        k.op("t", lambda t, j=j, eh=eh, ic=ic, p=p, last=last: t.matmul(
                            p["t"][:, 0:512],
                            catT[:, ic, 128 * j:128 * j + 128],
                            wo_view[:, ic, 512 * eh:512 * eh + 512],
                            start=(ic == 0), stop=last), inc=last)
                    if bias_i is not None:
                        k.op("t", lambda t, eh=eh, p=p, bi=bias_i: t.matmul(
                            p["t"][:, 0:512], ones_bf[0:1, 0:128],
                            brows[0:1, bi, 512 * eh:512 * eh + 512],
                            start=False, stop=True), inc=True)
                    p["wmark"] = k.cnt["t"]
                    k.wait("v", "t", p["wmark"])
                    k.wait_dma("v", bufm.get("xout_store"))
                    mv_ = k.op("v", lambda v, j=j, eh=eh, p=p: v.scalar_tensor_tensor(
                        xout[:, j, 512 * eh:512 * eh + 512], p["t"][:, 0:512], 1.0,
                        xf[slot][:, j, 512 * eh:512 * eh + 512],
                        ALU.mult, ALU.add), inc=True)
                    p["readers"].append(("v", mv_))
                    last_v = mv_
            bufm[("xf_v", slot)] = last_v
            k.wait("y", "v", last_v)
            mst = k.dma("y", dst[c * CH:(c + 1) * CH, :].rearrange("(j p) d -> p j d", p=128),
                        xout[:, :, :])
            bufm["xout_store"] = mst
            return mst

        def attn_a(src, dst, prologue=None):
            with contextlib.ExitStack() as _sst:
                wa = _sst.enter_context(nc.sbuf_tensor("a_w", [128, 4, 8, D], BF16))
                QT = _sst.enter_context(nc.sbuf_tensor("a_QT", [128, 8, CH], BF16))
                KT = _sst.enter_context(nc.sbuf_tensor("a_KT", [128, 8, CH], BF16))
                Vaug = _sst.enter_context(nc.sbuf_tensor("a_Vaug", [128, NT, 16, 65], BF16))
                KgT = _sst.enter_context(nc.sbuf_tensor("a_KgT", [128, 8, 8], BF16))
                Vg = _sst.enter_context(nc.sbuf_tensor("a_Vg", [8, 16, 65], BF16))
                BD = _sst.enter_context(nc.sbuf_tensor("a_BD", [128, 8, 128], BF16))
                macc = _sst.enter_context(nc.sbuf_tensor("a_macc", [128, 1024], F32))
                fs = _sst.enter_context(nc.sbuf_tensor("a_fs", [128, 8], F32))
                PTl = _sst.enter_context(nc.sbuf_tensor("a_PTl", [128, 512], BF16))
                PTg = _sst.enter_context(nc.sbuf_tensor("a_PTg", [8, 512], BF16))
                PgT = _sst.enter_context(nc.sbuf_tensor("a_PgT", [128, NT, 128], BF16))
                strip_bf = _sst.enter_context(nc.sbuf_tensor("a_strip", [128, 1024], BF16))
                atok = _sst.enter_context(nc.sbuf_tensor("a_tok", [128, NT, 64], BF16))
                rec4 = _sst.enter_context(nc.sbuf_tensor("a_rec", [128, NT], F32))
                xg = xout[0:8, 1, :]
                xgs = _sst.enter_context(nc.sbuf_tensor("a_xgs", [8, 14], F32))
                xgn = _sst.enter_context(nc.sbuf_tensor("a_xgn", [8, 1024], BF16))
                xgT = _sst.enter_context(nc.sbuf_tensor("a_xgT", [128, 8, 8], BF16))
                cglob = _sst.enter_context(nc.sbuf_tensor("a_cg", [128, 8, 8], BF16))
                saved = _sst.enter_context(nc.sbuf_tensor("a_sv", [128, 8, 8], BF16))
                grmw = xout[0:8, 0, :]
                for i, n in enumerate(("Wq", "Wk", "Wv", "Wo")):
                    k.dma("g", wa[:, i, :, :], aw[n].rearrange("(a p) o -> p a o", p=128))
                mw = k.snap("g")
                Wq, Wk, Wv, Wo = wa[:, 0], wa[:, 1], wa[:, 2], wa[:, 3]
                k.wait_dmas("t", mw)
                k.wait_dmas("t", m_consts_g)
                k.wait("t", "v", m_consts_v)
                k.wait_dma("v", m_al)

                mgl_ = [k.dma("y", xg[g:g + 1, :], src[gpos[g]:gpos[g] + 1, :])
                        for g in range(NG)]
                k.wait_dmas("v", mgl_)
                k.op("v", lambda v: v.bn_stats(xgs[:, 0:6], xg[:, 0:512]))
                k.op("v", lambda v: v.bn_stats(xgs[:, 6:12], xg[:, 512:1024]))
                k.op("v", lambda v: v.drain())
                m1 = k.op("v", lambda v: v.bn_aggr(xgs[:, 12:14], xgs[:, 0:12]), inc=True)
                k.wait("s", "v", m1)
                k.op("s", lambda s: s.activation(xgs[:, 0:1], xgs[:, 13:14], AF.Ln, bias=EPS))
                k.op("s", lambda s: s.drain())
                m2 = k.op("s", lambda s: s.activation(xgs[:, 1:2], xgs[:, 0:1], AF.Exp,
                                                      scale=-0.5), inc=True)
                k.wait("v", "s", m2)
                m3 = k.op("v", lambda v: v.tensor_scalar(
                    xgn[:, :], xg[:, :], xgs[:, 12:13], xgs[:, 1:2],
                    ALU.subtract, ALU.mult), inc=True)
                k.wait("t", "v", m3)
                b = bpool.acquire()
                for ic in range(8):
                    k.op("t", lambda t, ic=ic, b=b: t.matmul(
                        b["t"][:, 8 * ic:8 * ic + 8], xgn[:, 128 * ic:128 * ic + 128],
                        idn_bf[0:8, 0:8], start=True, stop=True, is_transpose=True),
                        inc=(ic == 7))
                b["wmark"] = k.cnt["t"]
                k.wait("v", "t", b["wmark"])
                m4 = k.op("v", lambda v, b=b: v.tensor_copy(
                    xgT[:, :, :], b["t"][:, 0:64].rearrange("p (a q) -> p a q", a=8)),
                    inc=True)
                b["readers"].append(("v", m4))

                k.op("v", lambda v: v.memset(BD[:, :, :], 0.0))
                k.op("v", lambda v: v.memset(Vg[:, :, :], 1.0))
                k.op("v", lambda v: v.drain())
                m_bd0 = k.op("v", lambda v: v.nop(), inc=True)

                k.wait("t", "v", m4)
                for dc in range(8):
                    p = pool.acquire()
                    for ic in range(8):
                        k.op("t", lambda t, dc=dc, ic=ic, p=p: t.matmul(
                            p["t"][:, 0:8], Wq[:, ic, 128 * dc:128 * dc + 128],
                            xgT[:, ic, :], start=(ic == 0), stop=False))
                    mq = k.op("t", lambda t, dc=dc, p=p: t.matmul(
                        p["t"][:, 0:8], brows[0:1, 0, 128 * dc:128 * dc + 128],
                        ones_bf[0:1, 0:8], start=False, stop=True), inc=True)
                    p["wmark"] = mq
                    k.wait("v", "t", mq)
                    for half in range(2):
                        mq2 = k.op("v", lambda v, dc=dc, half=half, p=p: v.tensor_scalar(
                            BD[64 * half:64 * half + 64, dc,
                               16 * dc + 8 * half:16 * dc + 8 * half + 8],
                            p["t"][64 * half:64 * half + 64, 0:8],
                            SCALE, None, ALU.mult), inc=(half == 1))
                        p["readers"].append(("v", mq2))
                    p = pool.acquire()
                    for ic in range(8):
                        k.op("t", lambda t, dc=dc, ic=ic, p=p: t.matmul(
                            p["t"][:, 0:8], Wk[:, ic, 128 * dc:128 * dc + 128],
                            xgT[:, ic, :], start=(ic == 0), stop=False))
                    mk_ = k.op("t", lambda t, dc=dc, p=p: t.matmul(
                        p["t"][:, 0:8], brows[0:1, 1, 128 * dc:128 * dc + 128],
                        ones_bf[0:1, 0:8], start=False, stop=True), inc=True)
                    p["wmark"] = mk_
                    k.wait("v", "t", mk_)
                    mk2 = k.op("v", lambda v, dc=dc, p=p: v.tensor_copy(
                        KgT[:, dc, :], p["t"][:, 0:8]), inc=True)
                    p["readers"].append(("v", mk2))
                k.wait("t", "v", m_bd0)
                for oh in range(2):
                    p = pool.acquire()
                    for ic in range(8):
                        k.op("t", lambda t, oh=oh, ic=ic, p=p: t.matmul(
                            p["t"][0:8, 0:512], xgT[:, ic, :],
                            Wv[:, ic, 512 * oh:512 * oh + 512],
                            start=(ic == 0), stop=False))
                    mvg = k.op("t", lambda t, oh=oh, p=p: t.matmul(
                        p["t"][0:8, 0:512], ones_bf[0:1, 0:8],
                        brows[0:1, 2, 512 * oh:512 * oh + 512],
                        start=False, stop=True), inc=True)
                    p["wmark"] = mvg
                    k.wait("v", "t", mvg)
                    mvg2 = k.op("v", lambda v, oh=oh, p=p: v.tensor_copy(
                        Vg[:, 8 * oh:8 * oh + 8, 0:64],
                        p["t"][0:8, 0:512].rearrange("p (a d) -> p a d", a=8)), inc=True)
                    p["readers"].append(("v", mvg2))

                k.op("v", lambda v: v.memset(macc[:, :], 0.0))
                k.op("v", lambda v: v.memset(fs[:, 1:2], 0.0))
                k.op("v", lambda v: v.memset(fs[:, 0:1], -3.0e38))
                k.op("v", lambda v: v.drain())
                m_f0 = k.op("v", lambda v: v.nop(), inc=True)

                store_marks = []
                lns = {0: carry.pop("ln0")} if "ln0" in carry else \
                    {0: ln_chunk(src, 0, 0)}
                for c in range(nch):
                    slot = c % 2
                    mld, m_xnT = lns.pop(c)
                    glist = [g for g in range(NG) if gpos[g] // CH == c]

                    k.wait("s", "t", bufm.get("a_qkv_t", 0))
                    k.wait("v", "t", bufm.get("a_qkv_t", 0))
                    for dc in range(8):
                        p = proj_fm(Wq, 128 * dc, m_xnT, slot, bias=0)
                        k.wait("s", "t", p["wmark"])
                        ms = k.op("s", lambda s, dc=dc, p=p: s.activation(
                            QT[:, dc, :], p["t"][:, 0:CH], AF.Copy, scale=SCALE),
                            inc=True)
                        p["readers"].append(("s", ms))
                        p = proj_fm(Wk, 128 * dc, m_xnT, slot, bias=1)
                        k.wait("v", "t", p["wmark"])
                        mv2 = k.op("v", lambda v, dc=dc, p=p: v.tensor_copy(
                            KT[:, dc, :], p["t"][:, 0:CH]), inc=True)
                        p["readers"].append(("v", mv2))
                    for j in range(NT):
                        for oh in range(2):
                            p = pool.acquire()
                            k.wait("t", "v", m_xnT)
                            for ic in range(8):
                                k.op("t", lambda t, j=j, oh=oh, ic=ic, p=p,
                                     slot=slot: t.matmul(
                                    p["t"][:, 0:512],
                                    xnT[slot][:, ic, 128 * j:128 * j + 128],
                                    Wv[:, ic, 512 * oh:512 * oh + 512],
                                    start=(ic == 0), stop=False))
                            mvp = k.op("t", lambda t, oh=oh, p=p: t.matmul(
                                p["t"][:, 0:512], ones_bf[0:1, 0:128],
                                brows[0:1, 2, 512 * oh:512 * oh + 512],
                                start=False, stop=True), inc=True)
                            p["wmark"] = mvp
                            note_xnT_read(slot, mvp)
                            k.wait("v", "t", mvp)
                            mvv = k.op("v", lambda v, j=j, oh=oh, p=p: v.tensor_copy(
                                Vaug[:, j, 8 * oh:8 * oh + 8, 0:64],
                                p["t"][:, 0:512].rearrange("p (a d) -> p a d", a=8)),
                                inc=True)
                            p["readers"].append(("v", mvv))
                    mvo = k.op("v", lambda v: v.memset(Vaug[:, :, :, 64:65], 1.0),
                               inc=True)
                    m_qkv_v = k.cnt["v"]
                    m_qkv_s = k.cnt["s"]
                    if c + 1 < nch:
                        lns[c + 1] = ln_chunk(src, c + 1, (c + 1) % 2)

                    p = pool.acquire()
                    k.wait("t", "v", m_qkv_v)
                    for ic in range(8):
                        k.op("t", lambda t, ic=ic, p=p: t.matmul(
                            p["t"][:, 0:CH], BD[:, ic, :], KT[:, ic, :],
                            start=(ic == 0), stop=(ic == 7)), inc=(ic == 7))
                    p["wmark"] = k.cnt["t"]
                    k.wait("v", "t", p["wmark"])
                    k.wait("v", "s", bufm.get("a_strip_s", 0))
                    k.op("v", lambda v, p=p: v.tensor_reduce(
                        fs[:, 2:3], p["t"][:, 0:CH], AX.X, ALU.max))
                    k.op("v", lambda v: v.drain())
                    k.op("v", lambda v: v.tensor_tensor(
                        fs[:, 3:4], fs[:, 0:1], fs[:, 2:3], ALU.max))
                    k.op("v", lambda v: v.drain())
                    mneg = k.op("v", lambda v: v.tensor_scalar(
                        fs[:, 4:5], fs[:, 3:4], -1.0, None, ALU.mult), inc=True)
                    k.wait("s", "v", mneg)
                    k.op("s", lambda s: s.activation(
                        fs[:, 5:6], fs[:, 0:1], AF.Exp, bias=fs[:, 4:5]))
                    mstr = k.op("s", lambda s, p=p: s.activation(
                        strip_bf[:, 0:CH], p["t"][:, 0:CH], AF.Exp,
                        bias=fs[:, 4:5]), inc=True)
                    bufm["a_strip_s"] = mstr
                    p["readers"].append(("s", mstr))
                    k.wait("v", "s", mstr)
                    k.op("v", lambda v: v.tensor_reduce(
                        fs[:, 6:7], strip_bf[:, 0:CH], AX.X, ALU.add))
                    k.op("v", lambda v: v.drain())
                    k.op("v", lambda v: v.scalar_tensor_tensor(
                        fs[:, 1:2], fs[:, 1:2], fs[:, 5:6], fs[:, 6:7],
                        ALU.mult, ALU.add))
                    k.op("v", lambda v: v.drain())
                    mfl1 = k.op("v", lambda v: v.tensor_copy(fs[:, 0:1], fs[:, 3:4]),
                                inc=True)
                    k.wait("t", "s", mstr)
                    b = bpool.acquire()
                    for j in range(NT):
                        k.op("t", lambda t, j=j, b=b: t.matmul(
                            b["t"][:, 128 * j:128 * j + 128],
                            strip_bf[:, 128 * j:128 * j + 128], idn_bf[:],
                            start=True, stop=True, is_transpose=True), inc=(j == NT - 1))
                    b["wmark"] = k.cnt["t"]
                    bufm["a_stripT_t"] = b["wmark"]
                    k.wait("v", "t", b["wmark"])
                    mpg = k.op("v", lambda v, b=b: v.tensor_copy(
                        PgT[:, :, :], b["t"][:, 0:128 * NT].rearrange(
                            "p (a q) -> p a q", a=NT)), inc=True)
                    b["readers"].append(("v", mpg))
                    k.wait("t", "v", mpg)
                    k.wait("t", "v", mfl1)
                    for dh in range(2):
                        p = pool.acquire()
                        for j in range(NT):
                            k.op("t", lambda t, dh=dh, j=j, p=p: t.matmul(
                                p["t"][:, 0:512].rearrange("p (a d) -> p a d", a=8),
                                PgT[:, j, :],
                                Vaug[:, j, 8 * dh:8 * dh + 8, 0:64],
                                start=(j == 0), stop=(j == NT - 1)), inc=(j == NT - 1))
                        p["wmark"] = k.cnt["t"]
                        k.wait("v", "t", p["wmark"])
                        mfl = k.op("v", lambda v, dh=dh, p=p: v.scalar_tensor_tensor(
                            macc[:, 512 * dh:512 * dh + 512],
                            macc[:, 512 * dh:512 * dh + 512], fs[:, 5:6],
                            p["t"][:, 0:512], ALU.mult, ALU.add), inc=True)
                        p["readers"].append(("v", mfl))

                    def a_tail(pO, h, dc, base):
                        k.wait("v", "t", pO["wmark"])
                        k.wait("v", "t", bufm.get("a_tok_t", 0))
                        k.op("v", lambda v, pO=pO: v.reciprocal(
                            rec4[:, :], pO["t"][:, 0:65 * NT].rearrange(
                                "p (a b) -> p a b", b=65)[:, :, 64]))
                        k.op("v", lambda v: v.drain())
                        for qt in range(NT):
                            mdv = k.op("v", lambda v, qt=qt, pO=pO: v.tensor_scalar(
                                atok[:, qt, :], pO["t"][:, 65 * qt:65 * qt + 64],
                                rec4[:, qt:qt + 1], None, ALU.mult), inc=(qt == NT - 1))
                        pO["readers"].append(("v", mdv))
                        pool.release(pO)
                        k.wait("t", "v", mdv)
                        b = bpool.acquire()
                        for qt in range(NT):
                            k.op("t", lambda t, qt=qt, b=b: t.matmul(
                                b["t"][0:64, 128 * qt:128 * qt + 128],
                                atok[:, qt, :], idn_bf[:],
                                start=True, stop=True, is_transpose=True),
                                inc=(qt == NT - 1))
                        b["wmark"] = k.cnt["t"]
                        bufm["a_tok_t"] = b["wmark"]
                        k.wait("v", "t", b["wmark"])
                        k.wait("v", "t", bufm.get("catT_t", 0))
                        mct = k.op("v", lambda v, dc=dc, base=base, b=b: v.tensor_copy(
                            catT[base:base + 64, dc, :], b["t"][0:64, 0:128 * NT]),
                            inc=True)
                        b["readers"].append(("v", mct))

                    a_pending = None

                    def a_scores(h):
                        dc, base = h // 2, 64 * (h % 2)
                        pA = pool.acquire()
                        pG = pool.acquire()
                        k.wait("t", "s", m_qkv_s)
                        for qt in range(NT):
                            k.op("t", lambda t, qt=qt, pA=pA, dc=dc, base=base: t.matmul(
                                pA["t"][:, 128 * qt:128 * qt + 128],
                                KT[base:base + 64, dc, 128 * qt:128 * qt + 128],
                                QT[base:base + 64, dc, 128 * qt:128 * qt + 128],
                                start=True, stop=False, skip_group_check=True))
                            k.op("t", lambda t, qt=qt, pA=pA: t.matmul(
                                pA["t"][0:64, 128 * qt + 64:128 * qt + 128],
                                ones_bf[0:1, 0:64], neg_bf[0:1, 0:64],
                                start=False, stop=False, skip_group_check=True))
                            k.op("t", lambda t, qt=qt, pA=pA: t.matmul(
                                pA["t"][64:128, 128 * qt:128 * qt + 64],
                                ones_bf[0:1, 0:64], neg_bf[0:1, 0:64],
                                start=False, stop=(qt == NT - 1), skip_group_check=True),
                                inc=(qt == NT - 1))
                        mA = k.cnt["t"]
                        pA["wmark"] = mA
                        for qt in range(NT):
                            last = (qt == NT - 1) and not glist
                            k.op("t", lambda t, qt=qt, pG=pG, dc=dc, base=base, last=last:
                                 t.matmul(pG["t"][0:8, 128 * qt:128 * qt + 128],
                                          KgT[base:base + 64, dc, :],
                                          QT[base:base + 64, dc, 128 * qt:128 * qt + 128],
                                          start=(qt == 0), stop=last,
                                          skip_group_check=True), inc=last)
                        for gi, g in enumerate(glist):
                            off = gpos[g] - c * CH
                            blk0 = (off // 64) * 64
                            last = (gi == len(glist) - 1)
                            k.op("t", lambda t, g=g, blk0=blk0, pG=pG, last=last: t.matmul(
                                pG["t"][0:8, blk0:blk0 + 64],
                                oh8_bf[0:1, 8 * g:8 * g + 8], neg_bf[0:1, 0:64],
                                start=False, stop=last, skip_group_check=True), inc=last)
                        mG = k.cnt["t"]
                        pG["wmark"] = mG
                        return pA, pG, mG

                    def a_headfin(h, pA, pG, mG):
                        dc, base = h // 2, 64 * (h % 2)
                        nonlocal a_pending
                        k.wait("s", "t", mG)
                        k.wait("s", "t", bufm.get("a_PT_pv", 0))
                        msl = k.op("s", lambda s, pA=pA: s.activation(
                            PTl[:, :], pA["t"][:, 0:512], AF.Exp), inc=True)
                        pA["readers"].append(("s", msl))
                        msg = k.op("s", lambda s, pG=pG: s.activation(
                            PTg[:, :], pG["t"][0:8, 0:512], AF.Exp), inc=True)
                        pG["readers"].append(("s", msg))
                        pO = pool.acquire(hold=True)
                        k.wait("t", "s", msg)
                        for qt in range(NT):
                            k.op("t", lambda t, h=h, qt=qt, pO=pO: t.matmul(
                                pO["t"][:, 65 * qt:65 * qt + 65],
                                PTl[:, 128 * qt:128 * qt + 128], Vaug[:, qt, h, :],
                                start=(qt == 0), stop=False, skip_group_check=True))
                            k.op("t", lambda t, h=h, qt=qt, pO=pO: t.matmul(
                                pO["t"][:, 65 * qt:65 * qt + 65],
                                PTg[:, 128 * qt:128 * qt + 128], Vg[:, h, :],
                                start=False, stop=(qt == NT - 1), skip_group_check=True),
                                inc=(qt == NT - 1))
                        pO["wmark"] = k.cnt["t"]
                        bufm["a_PT_pv"] = pO["wmark"]
                        if a_pending is not None:
                            a_tail(*a_pending)
                        a_pending = (pO, h, dc, base)

                    sc_pending = None
                    for h in range(NH):
                        sc = a_scores(h)
                        if sc_pending is not None:
                            a_headfin(*sc_pending)
                        sc_pending = (h,) + sc
                    if sc_pending is not None:
                        a_headfin(*sc_pending)
                        sc_pending = None
                    if a_pending is not None:
                        a_tail(*a_pending)
                        a_pending = None
                    m_cat_v = k.cnt["v"]
                    bufm["a_qkv_t"] = k.cnt["t"]

                    for g in glist:
                        col = gpos[g] - c * CH
                        m_cat_v = k.op("v", lambda v, g=g, col=col: v.tensor_copy(
                            saved[:, :, g], catT[:, :, col]), inc=True)

                    store_marks.append(wo_resid_store(Wo, 3, dst, c, slot, m_cat_v))
                    bufm["catT_t"] = k.cnt["t"]

                k.wait("v", "t", bufm.get("a_stripT_t", 0))
                k.op("v", lambda v: v.reciprocal(fs[:, 7:8], fs[:, 1:2]))
                k.op("v", lambda v: v.drain())
                mnrm = k.op("v", lambda v: v.tensor_scalar(
                    strip_bf[:, :], macc[:, :], fs[:, 7:8], None, ALU.mult), inc=True)
                k.wait("t", "v", mnrm)
                b = bpool.acquire()
                for dcb in range(8):
                    k.op("t", lambda t, dcb=dcb, b=b: t.matmul(
                        b["t"][:, 128 * dcb:128 * dcb + 128],
                        strip_bf[:, 128 * dcb:128 * dcb + 128], idn_bf[:],
                        start=True, stop=True, is_transpose=True), inc=(dcb == 7))
                b["wmark"] = k.cnt["t"]
                k.wait("v", "t", b["wmark"])
                for dcb in range(8):
                    k.op("v", lambda v, dcb=dcb, b=b: v.tensor_copy(
                        cglob[0:64, dcb, :],
                        b["t"][0:64, 128 * dcb + 16 * dcb:128 * dcb + 16 * dcb + 8]))
                    mgl = k.op("v", lambda v, dcb=dcb, b=b: v.tensor_copy(
                        cglob[64:128, dcb, :],
                        b["t"][64:128, 128 * dcb + 16 * dcb + 8:128 * dcb + 16 * dcb + 16]),
                        inc=(dcb == 7))
                b["readers"].append(("v", k.cnt["v"]))
                k.op("v", lambda v: v.drain())
                mdel = k.op("v", lambda v: v.tensor_tensor(
                    cglob[:, :, :], cglob[:, :, :], saved[:, :, :], ALU.subtract),
                    inc=True)
                k.wait("t", "v", mdel)
                k.wait_dmas("y", store_marks)
                mg2 = [k.dma("y", grmw[g:g + 1, :], dst[gpos[g]:gpos[g] + 1, :])
                       for g in range(NG)]
                mfx = 0
                for eh in range(2):
                    p = pool.acquire()
                    for ic in range(8):
                        k.op("t", lambda t, eh=eh, ic=ic, p=p: t.matmul(
                            p["t"][0:8, 0:512], cglob[:, ic, :],
                            Wo[:, ic, 512 * eh:512 * eh + 512],
                            start=(ic == 0), stop=(ic == 7)), inc=(ic == 7))
                    p["wmark"] = k.cnt["t"]
                    k.wait("v", "t", p["wmark"])
                    k.wait_dmas("v", mg2)
                    mfx = k.op("v", lambda v, eh=eh, p=p: v.tensor_tensor(
                        grmw[:, 512 * eh:512 * eh + 512],
                        grmw[:, 512 * eh:512 * eh + 512],
                        p["t"][0:8, 0:512], ALU.add), inc=True)
                    p["readers"].append(("v", mfx))
                k.wait("y", "v", mfx)
                for g in range(NG):
                    k.dma("y", dst[gpos[g]:gpos[g] + 1, :], grmw[g:g + 1, :])
                if prologue:
                    bufm["src_stores"] = k.snap("y")
                    prologue()
            k.barrier()

        def ffn(src, dst, tag, prologue=None):
            with contextlib.ExitStack() as _sst:
                w1s = _sst.enter_context(nc.sbuf_tensor(f"f_w1s{tag}", [128, 2, 8, HID], BF16))
                w3s = _sst.enter_context(nc.sbuf_tensor(f"f_w3s{tag}", [128, 2, 8, HID], BF16))
                w2s = _sst.enter_context(nc.sbuf_tensor(f"f_w2s{tag}", [128, 2, 4, 512], BF16))
                rws = _sst.enter_context(nc.sbuf_tensor(f"f_rws{tag}", [128, 8, 8], BF16))
                H1G = _sst.enter_context(nc.sbuf_tensor(f"f_H1G{tag}", [128, NE_ALL, 4, CH], BF16))
                G = _sst.enter_context(nc.sbuf_tensor(f"f_G{tag}", [128, NT, 32], F32))
                Gbf = _sst.enter_context(nc.sbuf_tensor(f"f_Gbf{tag}", [128, NT, 8], BF16))
                GT = _sst.enter_context(nc.sbuf_tensor(f"f_GT{tag}", [8, CH], BF16))
                sA = _sst.enter_context(nc.sbuf_tensor(f"f_sA{tag}", [128, 512], BF16))
                t1 = _sst.enter_context(nc.sbuf_tensor(f"f_t1{tag}", [128, 512], BF16))
                mrw = k.dma("g", rws[:, :, :], f_rw.rearrange("(a p) e -> p a e", p=128))
                k.wait_dma("t", mrw)
                k.wait_dmas("t", m_prestage)
                k.wait_dmas("t", m_consts_g)
                k.wait("t", "v", m_consts_v)

                lns = {0: carry.pop("ln0")} if "ln0" in carry else \
                    {0: ln_chunk(src, 0, 0)}
                wmk = {}
                w2mk = {}
                for c in range(nch):
                    slot = c % 2
                    mld, m_xnT = lns.pop(c)

                    k.wait("v", "s", bufm.get("f_gate_s", 0))
                    k.wait("v", "t", bufm.get("f_GT_t", 0))
                    for j in range(NT):
                        p = pool.acquire()
                        k.wait("t", "v", m_xnT)
                        for ic in range(8):
                            k.op("t", lambda t, j=j, ic=ic, p=p, slot=slot: t.matmul(
                                p["t"][:, 0:8],
                                xnT[slot][:, ic, 128 * j:128 * j + 128],
                                rws[:, ic, :], start=(ic == 0), stop=False))
                        mro = k.op("t", lambda t, p=p: t.matmul(
                            p["t"][:, 0:8], ones_bf[0:1, 0:128], brows[0:1, 4, 0:8],
                            start=False, stop=True), inc=True)
                        p["wmark"] = mro
                        note_xnT_read(slot, mro)
                        k.wait("v", "t", mro)
                        k.op("v", lambda v, j=j, p=p: v.tensor_reduce(
                            G[:, j, 24:25], p["t"][:, 0:8], AX.X, ALU.max))
                        k.op("v", lambda v: v.drain())
                        k.op("v", lambda v, j=j, p=p: v.tensor_scalar(
                            G[:, j, 0:8], p["t"][:, 0:8], G[:, j, 24:25], None, ALU.is_ge))
                        k.op("v", lambda v: v.drain())
                        k.op("v", lambda v, j=j: v.tensor_reduce(
                            G[:, j, 25:26], G[:, j, 0:8], AX.X, ALU.add))
                        k.op("v", lambda v, j=j, p=p: v.scalar_tensor_tensor(
                            G[:, j, 8:16], G[:, j, 0:8], NEG, p["t"][:, 0:8],
                            ALU.mult, ALU.add))
                        k.op("v", lambda v: v.drain())
                        k.op("v", lambda v, j=j: v.tensor_reduce(
                            G[:, j, 26:27], G[:, j, 8:16], AX.X, ALU.max))
                        k.op("v", lambda v, j=j: v.tensor_scalar(
                            G[:, j, 27:28], G[:, j, 25:26], 2.0, None, ALU.is_ge))
                        k.op("v", lambda v: v.drain())
                        k.op("v", lambda v, j=j: v.tensor_tensor(
                            G[:, j, 28:29], G[:, j, 24:25], G[:, j, 26:27], ALU.subtract))
                        k.op("v", lambda v: v.drain())
                        k.op("v", lambda v, j=j: v.tensor_tensor(
                            G[:, j, 28:29], G[:, j, 28:29], G[:, j, 27:28], ALU.mult))
                        k.op("v", lambda v: v.drain())
                        k.op("v", lambda v, j=j: v.tensor_tensor(
                            G[:, j, 28:29], G[:, j, 28:29], G[:, j, 26:27], ALU.add))
                        mneg1 = k.op("v", lambda v, j=j: v.tensor_scalar(
                            G[:, j, 29:30], G[:, j, 24:25], -1.0, None, ALU.mult),
                            inc=True)
                        k.wait("s", "v", mneg1)
                        mz = k.op("s", lambda s, j=j, p=p: s.activation(
                            G[:, j, 16:24], p["t"][:, 0:8], AF.Exp,
                            bias=G[:, j, 29:30]), inc=True)
                        bufm["f_gate_s"] = mz
                        p["readers"].append(("s", mz))
                        k.wait("v", "s", mz)
                        mke = k.op("v", lambda v, j=j, p=p: v.tensor_scalar(
                            G[:, j, 0:8], p["t"][:, 0:8], G[:, j, 28:29], None,
                            ALU.is_ge), inc=True)
                        p["readers"].append(("v", mke))
                        k.op("v", lambda v: v.drain())
                        k.op("v", lambda v, j=j: v.tensor_tensor(
                            G[:, j, 8:16], G[:, j, 16:24], G[:, j, 0:8], ALU.mult))
                        k.op("v", lambda v: v.drain())
                        k.op("v", lambda v, j=j: v.tensor_reduce(
                            G[:, j, 30:31], G[:, j, 8:16], AX.X, ALU.add))
                        k.op("v", lambda v: v.drain())
                        k.op("v", lambda v, j=j: v.reciprocal(
                            G[:, j, 31:32], G[:, j, 30:31]))
                        k.op("v", lambda v: v.drain())
                        k.op("v", lambda v, j=j: v.tensor_scalar(
                            Gbf[:, j, :], G[:, j, 8:16], G[:, j, 31:32], None,
                            ALU.mult), inc=(j == NT - 1))
                    m_gates = k.cnt["v"]
                    k.wait("t", "v", m_gates)
                    b = bpool.acquire()
                    for j in range(NT):
                        k.op("t", lambda t, j=j, b=b: t.matmul(
                            b["t"][0:8, 128 * j:128 * j + 128], Gbf[:, j, :],
                            idn_bf[:], start=True, stop=True, is_transpose=True),
                            inc=(j == NT - 1))
                    b["wmark"] = k.cnt["t"]
                    bufm["f_GT_t"] = b["wmark"]
                    k.wait("v", "t", b["wmark"])
                    mgt = k.op("v", lambda v, b=b: v.tensor_copy(
                        GT[:, :], b["t"][0:8, 0:128 * NT]), inc=True)
                    b["readers"].append(("v", mgt))
                    if c + 1 < nch:
                        lns[c + 1] = ln_chunk(src, c + 1, (c + 1) % 2)

                    EORD = [NEXP, NEXP + 1] + list(range(NEXP))
                    wmk = {}
                    for pos in range(2):
                        k.wait("g", "t", bufm.get(("f_w13_t", pos % 2), 0))
                        wmk[pos] = [
                            k.dma("g", w1s[:, pos % 2, :, :], ffw13[EORD[pos], 0]),
                            k.dma("g", w3s[:, pos % 2, :, :], ffw13[EORD[pos], 1])]
                    mhe = {}
                    for pos in range(NE_ALL):
                        e = EORD[pos]
                        sl = pos % 2
                        routed = e < NEXP
                        pGb = None
                        if routed:
                            pGb = pool.acquire(hold=True)
                            k.wait("t", "v", mgt)
                            mgb = k.op("t", lambda t, e=e, pGb=pGb: t.matmul(
                                pGb["t"][:, 0:CH], rsel_bf[:, e, :], GT[:, :],
                                start=True, stop=True), inc=True)
                            pGb["wmark"] = mgb
                        k.wait_dmas("t", wmk[pos])
                        lastw_t = 0
                        for hc in range(4):
                            pA_ = pool.acquire()
                            k.wait("t", "v", m_xnT)
                            for ic in range(8):
                                k.op("t", lambda t, sl=sl, hc=hc, ic=ic, pA_=pA_,
                                     slot=slot: t.matmul(
                                    pA_["t"][:, 0:CH],
                                    w1s[:, sl, ic, 128 * hc:128 * hc + 128],
                                    xnT[slot][:, ic, :], start=(ic == 0), stop=(ic == 7)),
                                    inc=(ic == 7))
                            pA_["wmark"] = k.cnt["t"]
                            pB_ = pool.acquire()
                            for ic in range(8):
                                k.op("t", lambda t, sl=sl, hc=hc, ic=ic, pB_=pB_,
                                     slot=slot: t.matmul(
                                    pB_["t"][:, 0:CH],
                                    w3s[:, sl, ic, 128 * hc:128 * hc + 128],
                                    xnT[slot][:, ic, :], start=(ic == 0), stop=(ic == 7)),
                                    inc=(ic == 7))
                            pB_["wmark"] = k.cnt["t"]
                            lastw_t = pB_["wmark"]
                            note_xnT_read(slot, lastw_t)
                            k.wait("s", "t", pA_["wmark"])
                            k.wait("s", "v", bufm.get("f_sA_v", 0))
                            msa = k.op("s", lambda s, pA_=pA_: s.activation(
                                sA[:, :], pA_["t"][:, 0:CH], AF.Silu), inc=True)
                            pA_["readers"].append(("s", msa))
                            k.wait("v", "s", msa)
                            k.wait("v", "t", pB_["wmark"])
                            k.wait("v", "t", bufm.get("f_H1G_t", 0))
                            k.op("v", lambda v: v.drain())
                            if routed:
                                mt1 = k.op("v", lambda v, pB_=pB_: v.scalar_tensor_tensor(
                                    t1[:, :], pB_["t"][:, 0:CH], 1.0, sA[:, :],
                                    ALU.mult, ALU.mult), inc=True)
                                pB_["readers"].append(("v", mt1))
                                bufm["f_sA_v"] = mt1
                                k.op("v", lambda v: v.drain())
                                mh = k.op("v", lambda v, e=e, hc=hc, pGb=pGb: v.tensor_tensor(
                                    H1G[:, e, hc, :], t1[:, :], pGb["t"][:, 0:CH],
                                    ALU.mult), inc=True)
                                pGb["readers"].append(("v", mh))
                            else:
                                mh = k.op("v", lambda v, e=e, hc=hc, pB_=pB_:
                                          v.scalar_tensor_tensor(
                                              H1G[:, e, hc, :], pB_["t"][:, 0:CH], 1.0,
                                              sA[:, :], ALU.mult, ALU.mult), inc=True)
                                pB_["readers"].append(("v", mh))
                                bufm["f_sA_v"] = mh
                        bufm[("f_w13_t", sl)] = lastw_t
                        mhe[e] = k.cnt["v"]
                        if pGb is not None:
                            pool.release(pGb)
                        if pos + 2 < NE_ALL:
                            np_ = pos + 2
                            k.wait("g", "t", bufm.get(("f_w13_t", np_ % 2), 0))
                            wmk[np_] = [
                                k.dma("g", w1s[:, np_ % 2, :, :], ffw13[EORD[np_], 0]),
                                k.dma("g", w3s[:, np_ % 2, :, :], ffw13[EORD[np_], 1])]
                    m_H1G = k.cnt["v"]

                    w2mk = {}
                    for dh in range(2):
                        for pos in range(2):
                            k.wait("g", "t", bufm.get(("f_w2_t", pos % 2), 0))
                            w2mk[pos] = k.dma("g", w2s[:, pos % 2, :, :],
                                              ffw2[EORD[pos], dh])
                        accs = [pool.acquire(hold=True) for _ in range(NT)]
                        m_B = 0
                        for pos in range(NE_ALL):
                            e = EORD[pos]
                            sl = pos % 2
                            k.wait("t", "v", mhe[e])
                            k.wait_dma("t", w2mk[pos])
                            for j in range(NT):
                                for hc in range(4):
                                    k.op("t", lambda t, e=e, sl=sl, j=j, hc=hc,
                                         pos=pos, a_=accs[j]: t.matmul(
                                             a_["t"][:, 0:512],
                                             H1G[:, e, hc, 128 * j:128 * j + 128],
                                             w2s[:, sl, hc, :],
                                             start=(pos == 0 and hc == 0),
                                             stop=(pos == NE_ALL - 1 and hc == 3),
                                             skip_group_check=True),
                                         inc=(j == NT - 1 and hc == 3))
                            bufm[("f_w2_t", sl)] = k.cnt["t"]
                            m_B = k.cnt["t"]
                            if pos + 2 < NE_ALL:
                                np_ = pos + 2
                                k.wait("g", "t", bufm.get(("f_w2_t", np_ % 2), 0))
                                w2mk[np_] = k.dma("g", w2s[:, np_ % 2, :, :],
                                                  ffw2[EORD[np_], dh])
                        for j in range(NT):
                            accs[j]["wmark"] = m_B
                            k.wait("v", "t", m_B)
                            k.wait_dma("v", bufm.get("xout_store"))
                            mev = k.op("v", lambda v, j=j, dh=dh, a_=accs[j],
                                       slot=slot: v.scalar_tensor_tensor(
                                           xout[:, j, 512 * dh:512 * dh + 512],
                                           a_["t"][:, 0:512], 1.0,
                                           xf[slot][:, j, 512 * dh:512 * dh + 512],
                                           ALU.mult, ALU.add), inc=True)
                            accs[j]["readers"].append(("v", mev))
                            pool.release(accs[j])
                    bufm["f_H1G_t"] = k.cnt["t"]
                    bufm[("xf_v", slot)] = k.cnt["v"]
                    last_v = k.cnt["v"]
                    k.wait("y", "v", last_v)
                    mst = k.dma("y", dst[c * CH:(c + 1) * CH, :]
                                .rearrange("(j p) d -> p j d", p=128), xout[:, :, :])
                    bufm["xout_store"] = mst
                    if prologue and c == nch - 1:
                        bufm["src_stores"] = k.snap("y")
                        prologue()
            k.barrier()

        def attn_b(src, dst, prologue=None):
            with contextlib.ExitStack() as _sst:
                wqT = _sst.enter_context(nc.sbuf_tensor("b_wqT", [128, 8, D], BF16))
                wkvK = _sst.enter_context(nc.sbuf_tensor("b_wkvK", [128, 8, 256], BF16))
                wkvV = _sst.enter_context(nc.sbuf_tensor("b_wkvV", [128, 8, 256], BF16))
                woT = _sst.enter_context(nc.sbuf_tensor("b_woT", [128, 8, D], BF16))
                Kz = _sst.enter_context(nc.sbuf_tensor("b_Kz", [128, 4, S], BF16))
                if 2 * S >= 8 * D:
                    wnat = Kz[:, 0:2, :].rearrange("p a (b d) -> p (a b) d",
                                                   b=S // D)[:, 0:8, :]
                else:
                    wnat = _sst.enter_context(
                        nc.sbuf_tensor("b_wnat", [128, 8, D], BF16))
                Vb = _sst.enter_context(nc.sbuf_tensor("b_Vaug", [128, NKT, 4, 65], BF16))
                QTb = _sst.enter_context(nc.sbuf_tensor("b_QT", [128, 8, CH], BF16))
                PT0 = _sst.enter_context(nc.sbuf_tensor("b_PT0", [128, 512], BF16))
                PT1 = _sst.enter_context(nc.sbuf_tensor("b_PT1", [128, 512], BF16))
                PT2 = _sst.enter_context(nc.sbuf_tensor("b_PT2", [128, 512], BF16))
                PT3 = _sst.enter_context(nc.sbuf_tensor("b_PT3", [128, 512], BF16))
                btok = _sst.enter_context(nc.sbuf_tensor("b_tok", [128, 4, 64], BF16))
                brec = _sst.enter_context(nc.sbuf_tensor("b_rec", [128, 4], F32))
                PTs = [PT0, PT1, PT2, PT3]

                def load_transpose(nat_src, n_oc, evac_plan, region=None,
                                   preloaded=None):
                    region = wnat if region is None else region
                    if preloaded is None:
                        k.wait("g", "t", k.cnt["t"])
                        mldw = k.dma("g", region[:, 0:n_oc, :], nat_src)
                    else:
                        mldw = preloaded
                    k.wait_dma("t", mldw)
                    wnat_ = region
                    for a in range(n_oc):
                        for icb in range(8):
                            b = bpool.acquire()
                            k.op("t", lambda t, a=a, icb=icb, b=b, wnat_=wnat_:
                                 t.matmul(
                                     b["t"][:, 0:128],
                                     wnat_[:, a, 128 * icb:128 * icb + 128],
                                     idn_bf[:], start=True, stop=True,
                                     is_transpose=True), inc=True)
                            b["wmark"] = k.cnt["t"]
                            k.wait("v", "t", b["wmark"])
                            for dst_ap, c0, c1 in evac_plan(a, icb):
                                mev = k.op("v", lambda v, dst_ap=dst_ap, c0=c0, c1=c1,
                                           b=b: v.tensor_copy(dst_ap, b["t"][:, c0:c1]),
                                           inc=True)
                                b["readers"].append(("v", mev))
                    return k.cnt["v"]

                def wq_plan(a, icb):
                    h0, h1 = 2 * a, 2 * a + 1
                    return [(wqT[:, icb, 64 * c64(h0):64 * c64(h0) + 64], 0, 64),
                            (wqT[:, icb, 64 * c64(h1):64 * c64(h1) + 64], 64, 128)]
                wkv_region = Kz[:, 2:3, :].rearrange("p a (b d) -> p (a b) d",
                                                      b=S // D)[:, 0:4, :] \
                    if S >= 4 * D else None
                mq_pre = k.dma("g", wnat[:, 0:8, :],
                               ab_Wq.rearrange("(a p) i -> p a i", p=128))
                mkv_pre = (k.dma("g", wkv_region[:, 0:4, :],
                                 ab_Wkv.rearrange("(a p) i -> p a i", p=128))
                           if wkv_region is not None else None)
                mwq = load_transpose(None, 8, wq_plan, preloaded=mq_pre)

                def wkv_plan(a, icb):
                    return [(wkvK[:, icb, 64 * a:64 * a + 64], 0, 64),
                            (wkvV[:, icb, 64 * a:64 * a + 64], 64, 128)]
                if wkv_region is not None:
                    mwkv = load_transpose(None, 4, wkv_plan, region=wkv_region,
                                          preloaded=mkv_pre)
                else:
                    mwkv = load_transpose(ab_Wkv.rearrange("(a p) i -> p a i", p=128),
                                          4, wkv_plan)

                def wo_plan(a, icb):
                    return [(woT[:, icb, 128 * a:128 * a + 128], 0, 128)]
                mwo = load_transpose(ab_Wo.rearrange("(a p) i -> p a i", p=128), 8, wo_plan)

                k.op("v", lambda v: v.memset(Vb[:, :, :, 64:65], 1.0))
                for g_ in range(4):
                    lo = 64 * (1 - (g_ % 2))
                    k.op("v", lambda v, g_=g_, lo=lo: v.memset(
                        Kz[lo:lo + 64, g_, :], 0.0))
                k.op("v", lambda v: v.drain())
                m_vb1 = k.op("v", lambda v: v.nop(), inc=True)
                k.wait_dma("v", m_al)

                lns = {0: carry.pop("ln0")} if "ln0" in carry else \
                    {0: ln_chunk(src, 0, 0)}
                for c in range(nch):
                    slot = c % 2
                    mld, m_xnT = lns.pop(c)
                    k.wait("s", "t", bufm.get("b_QT_t", 0))
                    for s_ in range(8):
                        p = proj_fm(wqT, 128 * s_, m_xnT, slot, bias=None)
                        k.wait("s", "t", p["wmark"])
                        msq = k.op("s", lambda s, s_=s_, p=p: s.activation(
                            QTb[:, s_, :], p["t"][:, 0:CH], AF.Copy, scale=SCALE),
                            inc=True)
                        p["readers"].append(("s", msq))
                    for pr in range(2):
                        p = proj_fm(wkvK, 128 * pr, m_xnT, slot, bias=None)
                        k.wait("v", "t", p["wmark"])
                        k.wait("v", "v", m_vb1)
                        k.op("v", lambda v, pr=pr, c=c, p=p: v.tensor_copy(
                            Kz[0:64, 2 * pr, c * CH:(c + 1) * CH],
                            p["t"][0:64, 0:CH]))
                        mkv = k.op("v", lambda v, pr=pr, c=c, p=p: v.tensor_copy(
                            Kz[64:128, 2 * pr + 1, c * CH:(c + 1) * CH],
                            p["t"][64:128, 0:CH]), inc=True)
                        p["readers"].append(("v", mkv))
                    for j in range(NT):
                        p = pool.acquire()
                        k.wait("t", "v", m_xnT)
                        for ic in range(8):
                            k.op("t", lambda t, j=j, ic=ic, p=p, slot=slot: t.matmul(
                                p["t"][:, 0:256],
                                xnT[slot][:, ic, 128 * j:128 * j + 128],
                                wkvV[:, ic, :], start=(ic == 0), stop=(ic == 7)),
                                inc=(ic == 7))
                        note_xnT_read(slot, k.cnt["t"])
                        p["wmark"] = k.cnt["t"]
                        k.wait("v", "t", p["wmark"])
                        k.wait("v", "v", m_vb1)
                        mkv = k.op("v", lambda v, j=j, c=c, p=p: v.tensor_copy(
                            Vb[:, NT * c + j, :, 0:64],
                            p["t"][:, 0:256].rearrange("p (a d) -> p a d", a=4)),
                            inc=True)
                        p["readers"].append(("v", mkv))
                    m_kv = k.cnt["v"]
                    m_q = k.cnt["s"]
                    if c + 1 < nch:
                        lns[c + 1] = ln_chunk(src, c + 1, (c + 1) % 2)

                    def b_tail(pO, h, qci, qc, nqt, qw):
                        k.wait("v", "t", pO["wmark"])
                        k.wait("v", "t", bufm.get("b_tok_t", 0))
                        k.op("v", lambda v, nqt=nqt, pO=pO: v.reciprocal(
                            brec[:, 0:nqt], pO["t"][:, 0:65 * nqt].rearrange(
                                "p (a b) -> p a b", b=65)[:, :, 64]))
                        k.op("v", lambda v: v.drain())
                        for qt in range(nqt):
                            mdv = k.op("v", lambda v, qt=qt, qw=qw, pO=pO:
                                       v.tensor_scalar(
                                           btok[0:qw, qt, :],
                                           pO["t"][0:qw, 65 * qt:65 * qt + 64],
                                           brec[0:qw, qt:qt + 1], None, ALU.mult),
                                       inc=(qt == nqt - 1))
                        pO["readers"].append(("v", mdv))
                        pool.release(pO)
                        k.wait("t", "v", mdv)
                        b = bpool.acquire()
                        for qt in range(nqt):
                            k.op("t", lambda t, qt=qt, qw=qw, b=b: t.matmul(
                                b["t"][0:64, 128 * qt:128 * qt + qw],
                                btok[0:qw, qt, :], idn_bf[0:qw, 0:qw],
                                start=True, stop=True, is_transpose=True),
                                inc=(qt == nqt - 1))
                        b["wmark"] = k.cnt["t"]
                        bufm["b_tok_t"] = b["wmark"]
                        k.wait("v", "t", b["wmark"])
                        k.wait("v", "t", bufm.get("catT_t", 0))
                        mct = k.op("v", lambda v, h=h, qci=qci, qc=qc, nqt=nqt,
                                   qw=qw, b=b: v.tensor_copy(
                                       catT[64 * (h % 2):64 * (h % 2) + 64, h // 2,
                                            qci * qc:qci * qc + qc].rearrange(
                                           "p (a q) -> p a q", a=nqt),
                                       b["t"][0:64, 0:128 * nqt].rearrange(
                                           "p (a q) -> p a q", a=nqt)[:, :, 0:qw]),
                                   inc=True)
                        b["readers"].append(("v", mct))

                    pending = None
                    for h in range(NH):
                        g = h // 4
                        s_, hi = b_slot(h)
                        base = 64 * hi
                        slope = 2.0 ** (-(h + 1) / 2.0)
                        qc = qc_size(h)
                        nqt = max(1, qc // 128)
                        qw = min(qc, 128)
                        for qci in range(CH // qc):
                            q0 = c * CH + qci * qc
                            kts = [kt for kt in range((q0 + qc + 127) // 128)
                                   if kt_keep(slope, q0, qc, kt)]
                            pO = pool.acquire(hold=True)
                            bank_started = [False]

                            def b_scores(ki, kt, q0=q0, qci=qci, qc=qc, g=g, s_=s_,
                                         base=base, h=h, qw=qw):
                                j0 = 128 * max(0, (128 * kt - q0 - qw + 128) // 128)
                                pS = pool.acquire()
                                k.wait("t", "v", m_kv)
                                k.wait("t", "s", m_q)
                                mS = k.op("t", lambda t, g=g, s_=s_, kt=kt, j0=j0,
                                          q0=q0, qci=qci, qc=qc, pS=pS: t.matmul(
                                              pS["t"][:, j0:qc],
                                              Kz[:, g, 128 * kt:128 * kt + 128],
                                              QTb[:, s_,
                                                  qci * qc + j0:qci * qc + qc],
                                              start=True, stop=True,
                                              skip_group_check=True), inc=True)
                                pS["wmark"] = mS
                                if 128 * kt + 127 >= q0 + j0:
                                    k.wait("v", "t", mS)
                                    mtri = k.op("v", lambda v, kt=kt, q0=q0, qc=qc,
                                                j0=j0, pS=pS: v.tensor_tensor(
                                                    pS["t"][:, j0:qc],
                                                    pS["t"][:, j0:qc],
                                                    tri_bf[:, 512 + q0 - 128 * kt + j0:
                                                           512 + q0 - 128 * kt + qc],
                                                    ALU.add), inc=True)
                                    pS["readers"].append(("v", mtri))
                                    k.wait("s", "v", mtri)
                                k.wait("s", "t", mS)
                                k.wait("s", "t", bufm.get(("b_PT_t", ki % 4), 0))
                                col = al_cols[(h, 128 * kt - q0 - qc // 2)]
                                msx = k.op("s", lambda s, qc=qc, col=col, ki=ki,
                                           j0=j0, pS=pS: s.activation(
                                               PTs[ki % 4][:, j0:qc],
                                               pS["t"][:, j0:qc],
                                               AF.Exp, bias=alibi_sb[:, col:col + 1]),
                                           inc=True)
                                pS["readers"].append(("s", msx))
                                return msx

                            def b_pv(ki, kt, msx, q0=q0, qc=qc, g=g, qw=qw, nqt=nqt,
                                     pO=pO):
                                k.wait("t", "s", msx)
                                qts = [qt for qt in range(nqt)
                                       if 128 * kt <= q0 + 128 * qt + qw - 1]
                                for qi, qt in enumerate(qts):
                                    st = not bank_started[0]
                                    bank_started[0] = True
                                    k.op("t", lambda t, qt=qt, kt=kt, ki=ki, g=g,
                                         qw=qw, st=st, pO=pO: t.matmul(
                                             pO["t"][0:qw, 65 * qt:65 * qt + 65],
                                             PTs[ki % 4][:, 128 * qt:128 * qt + qw],
                                             Vb[:, kt, g, :],
                                             start=st, stop=False,
                                             skip_group_check=True),
                                         inc=(qi == len(qts) - 1))
                                bufm[("b_PT_t", ki % 4)] = k.cnt["t"]

                            from collections import deque as _dq
                            pend = _dq()
                            for ki, kt in enumerate(kts):
                                msx = b_scores(ki, kt)
                                pend.append((ki, kt, msx))
                                if len(pend) > 2:
                                    b_pv(*pend.popleft())
                            while pend:
                                b_pv(*pend.popleft())
                            pO["wmark"] = k.cnt["t"]
                            if pending is not None:
                                b_tail(*pending)
                            pending = (pO, h, qci, qc, nqt, qw)
                    if pending is not None:
                        b_tail(*pending)
                        pending = None
                    m_cat_v = k.cnt["v"]
                    bufm["b_QT_t"] = k.cnt["t"]
                    if not os.environ.get("MK_NOWO"):
                        wo_resid_store(woT, None, dst, c, slot, m_cat_v)
                    bufm["catT_t"] = k.cnt["t"]
                    if prologue and c == nch - 1:
                        bufm["src_stores"] = k.snap("y")
                        prologue()
            k.barrier()

        srcs = {0: x_in, 1: Xa, 2: Xb, 3: Xa}
        dsts = {0: Xa, 1: Xb, 2: Xa, 3: y_out}
        order = [s_ for s_ in (0, 1, 2, 3) if s_ in subset]
        carry = {}
        plan = []
        for idx, s_ in enumerate(order):
            src = x_in if idx == 0 else srcs[s_]
            dst = y_out if idx == len(order) - 1 else dsts[s_]
            plan.append((s_, src, dst))
        for idx, (s_, src, dst) in enumerate(plan):
            if idx + 1 < len(plan):
                nxt_src = plan[idx + 1][1]

                def prologue(nxt_src=nxt_src):
                    carry["ln0"] = ln_chunk(nxt_src, 0, 0)
            else:
                prologue = None
            if s_ == 0:
                attn_a(src, dst, prologue)
            elif s_ in (1, 3):
                ffn(src, dst, s_, prologue)
            else:
                attn_b(src, dst, prologue)

        for q in "ygvs":
            for lane, cntv in enumerate(k.dcnt[q]):
                k.wait_dma(q, (q, lane, cntv))
        k.flush_all()

    return nc


def build_inputs(x_core, w, S, CH):
    hc = host_consts(S, CH)
    f32 = np.float32
    return {
        "x": np.ascontiguousarray(x_core, f32),
        "aa_Wq": w["L0_attn_a_Wq"], "aa_bq": w["L0_attn_a_bq"],
        "aa_Wk": w["L0_attn_a_Wk"], "aa_bk": w["L0_attn_a_bk"],
        "aa_Wv": w["L0_attn_a_Wv"], "aa_bv": w["L0_attn_a_bv"],
        "aa_Wo": w["L0_attn_a_Wo"], "aa_bo": w["L0_attn_a_bo"],
        "ab_Wq": w["L0_attn_b_Wq"], "ab_Wkv": w["L0_attn_b_Wkv"],
        "ab_Wo": w["L0_attn_b_Wo"],
        "f_rw": w["L0_ffn_router_w"], "f_rb": w["L0_ffn_router_b"],
        "f_w1": w["L0_ffn_routed_w1"], "f_w3": w["L0_ffn_routed_w3"],
        "f_w2": w["L0_ffn_routed_w2"],
        "f_s1": w["L0_ffn_shared_w1"], "f_s3": w["L0_ffn_shared_w3"],
        "f_s2": w["L0_ffn_shared_w2"],
        "idn": hc["idn"], "tri": hc["tri"], "oh8": hc["oh8"],
        "rowsel": hc["rowsel"], "alibi": hc["alibi"],
    }


_S, _CH = 4096, 512
_NCORES = 8


def kernel(**inputs):
    x = np.ascontiguousarray(inputs["L0_attn_a_x"], np.float32)
    hc = host_consts(_S, _CH)
    shared = {
        "aa_Wq": inputs["L0_attn_a_Wq"], "aa_bq": inputs["L0_attn_a_bq"],
        "aa_Wk": inputs["L0_attn_a_Wk"], "aa_bk": inputs["L0_attn_a_bk"],
        "aa_Wv": inputs["L0_attn_a_Wv"], "aa_bv": inputs["L0_attn_a_bv"],
        "aa_Wo": inputs["L0_attn_a_Wo"], "aa_bo": inputs["L0_attn_a_bo"],
        "ab_Wq": inputs["L0_attn_b_Wq"], "ab_Wkv": inputs["L0_attn_b_Wkv"],
        "ab_Wo": inputs["L0_attn_b_Wo"],
        "f_rw": inputs["L0_ffn_router_w"], "f_rb": inputs["L0_ffn_router_b"],
        "f_w1": inputs["L0_ffn_routed_w1"], "f_w3": inputs["L0_ffn_routed_w3"],
        "f_w2": inputs["L0_ffn_routed_w2"],
        "f_s1": inputs["L0_ffn_shared_w1"], "f_s3": inputs["L0_ffn_shared_w3"],
        "f_s2": inputs["L0_ffn_shared_w2"],
        "idn": hc["idn"], "tri": hc["tri"], "oh8": hc["oh8"],
        "rowsel": hc["rowsel"], "alibi": hc["alibi"],
    }
    shared = {k_: np.ascontiguousarray(v_, np.float32) for k_, v_ in shared.items()}
    nc = build_core(S=_S, CH=_CH, subset=(0, 1, 2, 3))
    in_maps = [dict(shared, x=x[b]) for b in range(_NCORES)]
    res = run_bass_kernel_spmd(nc, in_maps, core_ids=list(range(_NCORES)))
    return np.stack([res.results[b]["y"] for b in range(_NCORES)], axis=0)


# version 23
# speedup vs baseline: 1.4785x; 1.0283x over previous
import math
import numpy as np
import ml_dtypes
import contextlib
import os
import concourse.bass as bass
import concourse.mybir as mybir
from concourse.bass_utils import run_bass_kernel_spmd

F32 = mybir.dt.float32
BF16 = mybir.dt.bfloat16
AF = mybir.ActivationFunctionType
ALU = mybir.AluOpType
AX = mybir.AxisListType

NEG = -1.0e30
EPS = 1.0e-5


def kt_keep(slope, q0, qc, kt):
    thr = 110.0 - slope * (qc / 2.0)
    return slope * (q0 - (128 * kt + 127)) <= max(thr, 0.0)

def b_slot(h):
    if h < 4:
        return h, 0
    if h < 8:
        return h - 4, 1
    if h < 12:
        return h - 4, 0
    return h - 8, 1


def c64(h):
    s, hi = b_slot(h)
    return 2 * s + hi


def gpos_list(S, ng=8):
    return [int(x) for x in np.linspace(0, S - 1, ng).astype(np.int64)]


def qc_size(h):
    slope = 2.0 ** (-(h + 1) / 2.0)
    for qc in (512, 256, 128):
        if slope * (qc // 2) <= 64.0:
            return qc
    return 64


def alibi_plan(S, CH):
    cols, vals = {}, []
    for h in range(16):
        qc = qc_size(h)
        slope = 2.0 ** (-(h + 1) / 2.0)
        for c in range(S // CH):
            for qci in range(CH // qc):
                q0 = c * CH + qci * qc
                for kt in range((q0 + qc + 127) // 128):
                    if not kt_keep(slope, q0, qc, kt):
                        continue
                    key = (h, 128 * kt - q0 - qc // 2)
                    if key not in cols:
                        cols[key] = len(vals)
                        vals.append(slope * (128 * kt - q0 - qc // 2
                                             + np.arange(128, dtype=np.float64)))
    tab = np.array(vals, dtype=np.float32).T if vals else np.zeros((128, 1), np.float32)
    return cols, tab


def host_consts(S, CH):
    idn = np.eye(128, dtype=np.float32)
    tri = np.full((128, 1152), NEG, np.float32)
    for p in range(128):
        tri[p, 512 + p:] = 0.0
    oh8 = np.zeros((1, 64), np.float32)
    for g in range(8):
        oh8[0, 8 * g + g] = 1.0
    rowsel = np.zeros((8, 8, 128), np.float32)
    for e in range(8):
        rowsel[e, e, :] = 1.0
    al_cols, al_tab = alibi_plan(S, CH)
    return {"idn": idn, "tri": tri, "oh8": oh8, "rowsel": rowsel,
            "alibi": al_tab, "al_cols": al_cols}


class K:
    ENG_BLOCK = {"t": "tensor", "v": "vector", "s": "scalar", "g": "gpsimd", "y": "sync"}

    FLUSH_N = 128

    def __init__(self, nc, block, sems, dsems):
        self.nc, self.block = nc, block
        self._buf = {e: [] for e in "tvsgy"}
        self.sem = sems
        self.dsem = dsems
        self.cnt = {e: 0 for e in "tvsgy"}
        self.dcnt = {q: [0] * len(dsems[q]) for q in dsems}
        self.dlane = {q: 0 for q in dsems}
        self._waited = {}
        self._had_wait = {e: True for e in "tvsgy"}

    def _blk(self, e):
        return getattr(self.block, self.ENG_BLOCK[e])

    def _emit(self, e, fn):
        self._buf[e].append(fn)
        if len(self._buf[e]) >= self.FLUSH_N:
            self.flush(e)

    def flush(self, e):
        fns = self._buf[e]
        if not fns:
            return
        self._buf[e] = []

        def body(eng):
            for f in fns:
                f(eng)
        self._blk(e)(body)

    def flush_all(self):
        for e in "tvsgy":
            self.flush(e)

    def op(self, e, fn, inc=False):
        marker = None
        if inc:
            prev = self.cnt[e]
            self.cnt[e] += 1
            marker = self.cnt[e]
            sem = self.sem[e]
            boundary = not self._had_wait[e]

            def body(eng):
                if boundary:
                    eng.wait_ge(sem, prev)
                instr = fn(eng)
                assert instr is not None, "inc op must return its instruction"
                instr.then_inc(sem, 1)
            self._emit(e, body)
            self._had_wait[e] = False
        else:
            self._emit(e, fn)
        return marker

    def wait(self, e, prod, val):
        if not val or e == prod:
            return
        key = (e, "c", prod)
        if not os.environ.get("MK_NODEDUP") and self._waited.get(key, 0) >= val:
            return
        self._waited[key] = val
        sem = self.sem[prod]
        self._emit(e, lambda eng: eng.wait_ge(sem, val))
        self._had_wait[e] = True

    def dma(self, q, out, in_, **kw):
        lane = self.dlane[q]
        self.dlane[q] = (lane + 1) % len(self.dsem[q])
        sem = self.dsem[q][lane]
        prev = self.dcnt[q][lane]
        self.dcnt[q][lane] = prev + 16
        marker = (q, lane, prev + 16)

        def body(eng):
            if prev > 0:
                eng.wait_ge(sem, prev)
            eng.dma_start(out=out, in_=in_, **kw).then_inc(sem, 16)
        self._emit(q, body)
        key = (q, "d", q, lane)
        if prev > 0 and self._waited.get(key, 0) < prev:
            self._waited[key] = prev
        return marker

    def snap(self, q):
        return tuple((q, lane, c) for lane, c in enumerate(self.dcnt[q]) if c > 0)

    def wait_dmas(self, e, markers):
        for m in markers:
            self.wait_dma(e, m)

    def wait_dma(self, e, marker):
        if not marker:
            return
        q, lane, val = marker
        key = (e, "d", q, lane)
        if self._waited.get(key, 0) >= val:
            return
        self._waited[key] = val
        sem = self.dsem[q][lane]
        self._emit(e, lambda eng: eng.wait_ge(sem, val))
        self._had_wait[e] = True

    def barrier(self, engines="tvsgy"):
        marks = {}
        for e in engines:
            if e in "ygvs":
                for lane, cntv in enumerate(self.dcnt[e]):
                    self.wait_dma(e, (e, lane, cntv))
            marks[e] = self.op(e, lambda eng: eng.nop(), inc=True)
        for e in engines:
            for p in engines:
                if p != e:
                    self.wait(e, p, marks[p])
        self.flush_all()


class PsumPool:
    def __init__(self, k, tensors):
        self.k = k
        self.slots = [{"t": t, "readers": [], "wmark": 0, "held": False}
                      for t in tensors]
        self.i = 0

    def acquire(self, hold=False):
        for _ in range(len(self.slots)):
            s = self.slots[self.i % len(self.slots)]
            self.i += 1
            if s["held"]:
                continue
            for (eng, m) in s["readers"]:
                self.k.wait("t", eng, m)
            s["readers"] = []
            s["held"] = hold
            return s
        raise RuntimeError("psum pool exhausted (all held)")

    def release(self, s):
        s["held"] = False


def build_core(S=4096, CH=512, subset=(0, 1, 2, 3)):
    D, NH, HD, NKV = 1024, 16, 64, 4
    NEXP, NSH, HID = 8, 2, 512
    NE_ALL = NEXP + NSH
    NG = 8
    nch = S // CH
    NT = CH // 128
    NKT = S // 128
    gpos = gpos_list(S, NG)
    SCALE = 1.0 / math.sqrt(HD)
    al_cols, al_tab = alibi_plan(S, CH)
    NAL = al_tab.shape[1]
    run_ffn = (1 in subset) or (3 in subset)

    nc = bass.Bass("TRN2", target_bir_lowering=False, debug=False)

    x_in = nc.dram_tensor("x", [S, D], F32, kind="ExternalInput").ap()
    y_out = nc.dram_tensor("y", [S, D], F32, kind="ExternalOutput").ap()
    Xa = nc.dram_tensor("Xa", [S, D], F32).ap()
    Xb = nc.dram_tensor("Xb", [S, D], F32).ap()

    aw = {n: nc.dram_tensor("aa_" + n, [D, D], F32, kind="ExternalInput").ap()
          for n in ("Wq", "Wk", "Wv", "Wo")}
    abias = {n: nc.dram_tensor("aa_" + n, [D], F32, kind="ExternalInput").ap()
             for n in ("bq", "bk", "bv", "bo")}
    ab_Wq = nc.dram_tensor("ab_Wq", [D, D], F32, kind="ExternalInput").ap()
    ab_Wkv = nc.dram_tensor("ab_Wkv", [512, D], F32, kind="ExternalInput").ap()
    ab_Wo = nc.dram_tensor("ab_Wo", [D, D], F32, kind="ExternalInput").ap()
    f_rw = nc.dram_tensor("f_rw", [D, NEXP], F32, kind="ExternalInput").ap()
    f_rb = nc.dram_tensor("f_rb", [NEXP], F32, kind="ExternalInput").ap()
    f_w1 = nc.dram_tensor("f_w1", [NEXP, D, HID], F32, kind="ExternalInput").ap()
    f_w3 = nc.dram_tensor("f_w3", [NEXP, D, HID], F32, kind="ExternalInput").ap()
    f_w2 = nc.dram_tensor("f_w2", [NEXP, HID, D], F32, kind="ExternalInput").ap()
    f_s1 = nc.dram_tensor("f_s1", [NSH, D, HID], F32, kind="ExternalInput").ap()
    f_s3 = nc.dram_tensor("f_s3", [NSH, D, HID], F32, kind="ExternalInput").ap()
    f_s2 = nc.dram_tensor("f_s2", [NSH, HID, D], F32, kind="ExternalInput").ap()

    idn_in = nc.dram_tensor("idn", [128, 128], F32, kind="ExternalInput").ap()
    tri_in = nc.dram_tensor("tri", [128, 1152], F32, kind="ExternalInput").ap()
    oh8_in = nc.dram_tensor("oh8", [1, 64], F32, kind="ExternalInput").ap()
    rsel_in = nc.dram_tensor("rowsel", [8, 8, 128], F32, kind="ExternalInput").ap()
    alibi_in = nc.dram_tensor("alibi", [128, NAL], F32, kind="ExternalInput").ap()

    ffw13 = nc.dram_tensor("ffw13", [NE_ALL, 2, 128, 8, HID], BF16).ap()
    ffw2 = nc.dram_tensor("ffw2", [NE_ALL, 2, 128, 4, 512], BF16).ap()

    with contextlib.ExitStack() as _st:
        cvals = _st.enter_context(nc.sbuf_tensor("cvals", [128, 2], F32))
        idn_bf = _st.enter_context(nc.sbuf_tensor("idn_bf", [128, 128], BF16))
        tri_bf = _st.enter_context(nc.sbuf_tensor("tri_bf", [128, 1152], BF16))
        oh8_bf = _st.enter_context(nc.sbuf_tensor("oh8_bf", [1, 64], BF16))
        rsel_bf = _st.enter_context(nc.sbuf_tensor("rsel_bf", [8, 8, 128], BF16))
        ones_bf = _st.enter_context(nc.sbuf_tensor("ones_bf", [1, 512], BF16))
        neg_bf = _st.enter_context(nc.sbuf_tensor("neg_bf", [1, 128], BF16))
        alibi_sb = _st.enter_context(nc.sbuf_tensor("alibi_sb", [128, NAL], F32))
        brows = _st.enter_context(nc.sbuf_tensor("brows", [1, 5, 1024], BF16))
        xf0 = _st.enter_context(nc.sbuf_tensor("xf0", [128, NT, D], F32))
        xf1 = _st.enter_context(nc.sbuf_tensor("xf1", [128, NT, D], F32))
        xn_sb = _st.enter_context(nc.sbuf_tensor("xn", [128, NT, D], BF16))
        xnT0 = _st.enter_context(nc.sbuf_tensor("xnT0", [128, 8, CH], BF16))
        xnT1 = _st.enter_context(nc.sbuf_tensor("xnT1", [128, 8, CH], BF16))
        xout = _st.enter_context(nc.sbuf_tensor("xout", [128, NT, D], F32))
        stats = _st.enter_context(nc.sbuf_tensor("stats", [128, NT, 12], F32))
        mv = _st.enter_context(nc.sbuf_tensor("mv", [128, NT, 2], F32))
        rstd = _st.enter_context(nc.sbuf_tensor("rstd", [128, NT], F32))
        lntmp = _st.enter_context(nc.sbuf_tensor("lntmp", [128, NT], F32))
        catT = _st.enter_context(nc.sbuf_tensor("catT", [128, 8, CH], BF16))
        pf0 = _st.enter_context(nc.psum_tensor("pf0", [128, 512], F32))
        pf1 = _st.enter_context(nc.psum_tensor("pf1", [128, 512], F32))
        pf2 = _st.enter_context(nc.psum_tensor("pf2", [128, 512], F32))
        pf3 = _st.enter_context(nc.psum_tensor("pf3", [128, 512], F32))
        pf4 = _st.enter_context(nc.psum_tensor("pf4", [128, 512], F32))
        pf5 = _st.enter_context(nc.psum_tensor("pf5", [128, 512], F32))
        pb0 = _st.enter_context(nc.psum_tensor("pb0", [128, 1024], BF16))
        pb1 = _st.enter_context(nc.psum_tensor("pb1", [128, 1024], BF16))
        sem_t = _st.enter_context(nc.semaphore("sem_t"))
        sem_v = _st.enter_context(nc.semaphore("sem_v"))
        sem_s = _st.enter_context(nc.semaphore("sem_s"))
        sem_g = _st.enter_context(nc.semaphore("sem_g"))
        sem_y = _st.enter_context(nc.semaphore("sem_y"))
        dma_y = [_st.enter_context(nc.semaphore(f"dma_y{i}")) for i in range(6)]
        dma_g = [_st.enter_context(nc.semaphore(f"dma_g{i}")) for i in range(6)]
        dma_v = [_st.enter_context(nc.semaphore("dma_v0"))]
        dma_s = [_st.enter_context(nc.semaphore("dma_s0"))]
        block = _st.enter_context(nc.Block())
        k = K(nc, block,
              {"t": sem_t, "v": sem_v, "s": sem_s, "g": sem_g, "y": sem_y},
              {"y": dma_y, "g": dma_g, "v": dma_v, "s": dma_s})
        pool = PsumPool(k, [pf0, pf1, pf2, pf3, pf4, pf5])
        bpool = PsumPool(k, [pb0, pb1])
        xf = [xf0, xf1]
        xnT = [xnT0, xnT1]
        bufm = {}

        for src, dst in ((idn_in[:, :], idn_bf[:]), (tri_in[:, :], tri_bf[:]),
                         (oh8_in[:, :], oh8_bf[:]), (rsel_in[:, :, :], rsel_bf[:])):
            k.dma("g", dst, src)
        m_al = k.dma("y", alibi_sb[:], alibi_in[:, :])
        for i, src in enumerate((abias["bq"], abias["bk"], abias["bv"], abias["bo"])):
            k.dma("g", brows[0:1, i, :], src.rearrange("(a d) -> a d", a=1))
        k.dma("g", brows[0:1, 4, 0:NEXP], f_rb.rearrange("(a d) -> a d", a=1))
        m_consts_g = k.snap("g")
        nc.const_aps.register(F32, 0.0, cvals[:, 0:1])
        nc.const_aps.register(F32, EPS, cvals[:, 1:2])
        k.op("v", lambda v: v.memset(cvals[:, 0:1], 0.0))
        k.op("v", lambda v: v.memset(cvals[:, 1:2], EPS))
        k.op("v", lambda v: v.memset(ones_bf[:], 1.0))
        k.op("v", lambda v: v.drain())
        m_consts_v = k.op("v", lambda v: v.memset(neg_bf[:], NEG), inc=True)

        m_prestage = 0
        if run_ffn:
            for e in range(NE_ALL):
                s1 = f_w1[e] if e < NEXP else f_s1[e - NEXP]
                s3 = f_w3[e] if e < NEXP else f_s3[e - NEXP]
                s2 = f_w2[e] if e < NEXP else f_s2[e - NEXP]
                k.dma("g", ffw13[e, 0], s1.rearrange("(a p) h -> p a h", p=128))
                k.dma("g", ffw13[e, 1], s3.rearrange("(a p) h -> p a h", p=128))
                for b_ in range(2):
                    k.dma("g", ffw2[e, b_],
                          s2[:, 512 * b_:512 * b_ + 512].rearrange(
                              "(a p) d -> p a d", p=128))
            m_prestage = k.snap("g")

        def ln_chunk(src, c, slot):
            k.wait("y", "v", bufm.get(("xf_v", slot), 0))
            k.wait_dmas("y", bufm.get("src_stores", ()))
            mld = k.dma("y", xf[slot][:, :, :],
                        src[c * CH:(c + 1) * CH, :].rearrange("(j p) d -> p j d", p=128))
            k.wait_dma("v", mld)
            k.wait("v", "s", bufm.get("lnscalar", 0))
            for j in range(NT):
                k.op("v", lambda v, j=j: v.bn_stats(stats[:, j, 0:6], xf[slot][:, j, 0:512]))
                k.op("v", lambda v, j=j: v.bn_stats(stats[:, j, 6:12], xf[slot][:, j, 512:1024]))
            k.op("v", lambda v: v.drain())
            for j in range(NT):
                k.op("v", lambda v, j=j: v.bn_aggr(mv[:, j, :], stats[:, j, :]),
                     inc=(j == NT - 1))
            m_mv = k.cnt["v"]
            k.wait("s", "v", m_mv)
            k.op("s", lambda s: s.activation(lntmp[:, :], mv[:, :, 1], AF.Ln, bias=EPS))
            k.op("s", lambda s: s.drain())
            m_rstd = k.op("s", lambda s: s.activation(rstd[:, :], lntmp[:, :], AF.Exp,
                                                      scale=-0.5), inc=True)
            bufm["lnscalar"] = m_rstd
            k.wait("v", "s", m_rstd)
            k.wait("v", "t", bufm.get("xn_t", 0))
            for j in range(NT):
                k.op("v", lambda v, j=j, slot=slot: v.tensor_scalar(
                    xn_sb[:, j, :], xf[slot][:, j, :], mv[:, j, 0:1], rstd[:, j:j + 1],
                    ALU.subtract, ALU.mult), inc=(j == NT - 1))
            m_xn = k.cnt["v"]

            def ln_fin(done=[False], out=[None]):
                if done[0]:
                    return out[0]
                done[0] = True
                k.wait("t", "v", m_xn)
                for j in range(NT):
                    b = bpool.acquire()
                    for dc in range(8):
                        k.op("t", lambda t, j=j, dc=dc, b=b: t.matmul(
                            b["t"][:, 128 * dc:128 * dc + 128],
                            xn_sb[:, j, 128 * dc:128 * dc + 128], idn_bf[:],
                            start=True, stop=True, is_transpose=True), inc=(dc == 7))
                    b["wmark"] = k.cnt["t"]
                    k.wait("v", "t", b["wmark"])
                    k.wait("v", "t", bufm.get(("xnT_t", slot), 0))
                    mm = k.op("v", lambda v, j=j, b=b, slot=slot: v.tensor_copy(
                        xnT[slot][:, :, 128 * j:128 * j + 128],
                        b["t"][:].rearrange("p (a q) -> p a q", a=8)), inc=True)
                    b["readers"].append(("v", mm))
                bufm["xn_t"] = k.cnt["t"]
                out[0] = k.cnt["v"]
                return out[0]
            return mld, ln_fin

        def note_xnT_read(slot, marker):
            key = ("xnT_t", slot)
            bufm[key] = max(bufm.get(key, 0), marker)

        def proj_fm(wview, col0, m_xnT, slot, bias=None):
            p = pool.acquire()
            k.wait("t", "v", m_xnT)
            for ic in range(8):
                last = (ic == 7) and bias is None
                k.op("t", lambda t, ic=ic, p=p, last=last, slot=slot: t.matmul(
                    p["t"][:, 0:CH], wview[:, ic, col0:col0 + 128],
                    xnT[slot][:, ic, :], start=(ic == 0), stop=last), inc=last)
            if bias is not None:
                bi = bias
                k.op("t", lambda t, p=p: t.matmul(
                    p["t"][:, 0:CH], brows[0:1, bi, col0:col0 + 128],
                    ones_bf[0:1, 0:CH], start=False, stop=True), inc=True)
            p["wmark"] = k.cnt["t"]
            note_xnT_read(slot, p["wmark"])
            return p

        def wo_resid_store(wo_view, bias_i, dst, c, slot, m_cat_v):
            k.wait("t", "v", m_cat_v)
            last_v = 0
            for j in range(NT):
                for eh in range(2):
                    p = pool.acquire()
                    for ic in range(8):
                        last = (ic == 7) and bias_i is None
                        k.op("t", lambda t, j=j, eh=eh, ic=ic, p=p, last=last: t.matmul(
                            p["t"][:, 0:512],
                            catT[:, ic, 128 * j:128 * j + 128],
                            wo_view[:, ic, 512 * eh:512 * eh + 512],
                            start=(ic == 0), stop=last), inc=last)
                    if bias_i is not None:
                        k.op("t", lambda t, eh=eh, p=p, bi=bias_i: t.matmul(
                            p["t"][:, 0:512], ones_bf[0:1, 0:128],
                            brows[0:1, bi, 512 * eh:512 * eh + 512],
                            start=False, stop=True), inc=True)
                    p["wmark"] = k.cnt["t"]
                    k.wait("v", "t", p["wmark"])
                    k.wait_dma("v", bufm.get("xout_store"))
                    mv_ = k.op("v", lambda v, j=j, eh=eh, p=p: v.scalar_tensor_tensor(
                        xout[:, j, 512 * eh:512 * eh + 512], p["t"][:, 0:512], 1.0,
                        xf[slot][:, j, 512 * eh:512 * eh + 512],
                        ALU.mult, ALU.add), inc=True)
                    p["readers"].append(("v", mv_))
                    last_v = mv_
            bufm[("xf_v", slot)] = last_v
            k.wait("y", "v", last_v)
            mst = k.dma("y", dst[c * CH:(c + 1) * CH, :].rearrange("(j p) d -> p j d", p=128),
                        xout[:, :, :])
            bufm["xout_store"] = mst
            return mst

        def attn_a(src, dst, prologue=None):
            with contextlib.ExitStack() as _sst:
                wa = _sst.enter_context(nc.sbuf_tensor("a_w", [128, 4, 8, D], BF16))
                QT = _sst.enter_context(nc.sbuf_tensor("a_QT", [128, 8, CH], BF16))
                KT = _sst.enter_context(nc.sbuf_tensor("a_KT", [128, 8, CH], BF16))
                Vaug = _sst.enter_context(nc.sbuf_tensor("a_Vaug", [128, NT, 16, 65], BF16))
                KgT = _sst.enter_context(nc.sbuf_tensor("a_KgT", [128, 8, 8], BF16))
                Vg = _sst.enter_context(nc.sbuf_tensor("a_Vg", [8, 16, 65], BF16))
                BD = _sst.enter_context(nc.sbuf_tensor("a_BD", [128, 8, 128], BF16))
                macc = _sst.enter_context(nc.sbuf_tensor("a_macc", [128, 1024], F32))
                fs = _sst.enter_context(nc.sbuf_tensor("a_fs", [128, 8], F32))
                PTl = _sst.enter_context(nc.sbuf_tensor("a_PTl", [128, 512], BF16))
                PTg = _sst.enter_context(nc.sbuf_tensor("a_PTg", [8, 512], BF16))
                PgT = _sst.enter_context(nc.sbuf_tensor("a_PgT", [128, NT, 128], BF16))
                strip_bf = _sst.enter_context(nc.sbuf_tensor("a_strip", [128, 1024], BF16))
                atok = _sst.enter_context(nc.sbuf_tensor("a_tok", [128, NT, 64], BF16))
                rec4 = _sst.enter_context(nc.sbuf_tensor("a_rec", [128, NT], F32))
                xg = xout[0:8, 1, :]
                xgs = _sst.enter_context(nc.sbuf_tensor("a_xgs", [8, 14], F32))
                xgn = _sst.enter_context(nc.sbuf_tensor("a_xgn", [8, 1024], BF16))
                xgT = _sst.enter_context(nc.sbuf_tensor("a_xgT", [128, 8, 8], BF16))
                cglob = _sst.enter_context(nc.sbuf_tensor("a_cg", [128, 8, 8], BF16))
                saved = _sst.enter_context(nc.sbuf_tensor("a_sv", [128, 8, 8], BF16))
                grmw = xout[0:8, 0, :]
                for i, n in enumerate(("Wq", "Wk", "Wv", "Wo")):
                    k.dma("g", wa[:, i, :, :], aw[n].rearrange("(a p) o -> p a o", p=128))
                mw = k.snap("g")
                Wq, Wk, Wv, Wo = wa[:, 0], wa[:, 1], wa[:, 2], wa[:, 3]
                k.wait_dmas("t", mw)
                k.wait_dmas("t", m_consts_g)
                k.wait("t", "v", m_consts_v)
                k.wait_dma("v", m_al)

                mgl_ = [k.dma("y", xg[g:g + 1, :], src[gpos[g]:gpos[g] + 1, :])
                        for g in range(NG)]
                k.wait_dmas("v", mgl_)
                k.op("v", lambda v: v.bn_stats(xgs[:, 0:6], xg[:, 0:512]))
                k.op("v", lambda v: v.bn_stats(xgs[:, 6:12], xg[:, 512:1024]))
                k.op("v", lambda v: v.drain())
                m1 = k.op("v", lambda v: v.bn_aggr(xgs[:, 12:14], xgs[:, 0:12]), inc=True)
                k.wait("s", "v", m1)
                k.op("s", lambda s: s.activation(xgs[:, 0:1], xgs[:, 13:14], AF.Ln, bias=EPS))
                k.op("s", lambda s: s.drain())
                m2 = k.op("s", lambda s: s.activation(xgs[:, 1:2], xgs[:, 0:1], AF.Exp,
                                                      scale=-0.5), inc=True)
                k.wait("v", "s", m2)
                m3 = k.op("v", lambda v: v.tensor_scalar(
                    xgn[:, :], xg[:, :], xgs[:, 12:13], xgs[:, 1:2],
                    ALU.subtract, ALU.mult), inc=True)
                k.wait("t", "v", m3)
                b = bpool.acquire()
                for ic in range(8):
                    k.op("t", lambda t, ic=ic, b=b: t.matmul(
                        b["t"][:, 8 * ic:8 * ic + 8], xgn[:, 128 * ic:128 * ic + 128],
                        idn_bf[0:8, 0:8], start=True, stop=True, is_transpose=True),
                        inc=(ic == 7))
                b["wmark"] = k.cnt["t"]
                k.wait("v", "t", b["wmark"])
                m4 = k.op("v", lambda v, b=b: v.tensor_copy(
                    xgT[:, :, :], b["t"][:, 0:64].rearrange("p (a q) -> p a q", a=8)),
                    inc=True)
                b["readers"].append(("v", m4))

                k.op("v", lambda v: v.memset(BD[:, :, :], 0.0))
                k.op("v", lambda v: v.memset(Vg[:, :, :], 1.0))
                k.op("v", lambda v: v.drain())
                m_bd0 = k.op("v", lambda v: v.nop(), inc=True)

                k.wait("t", "v", m4)
                for dc in range(8):
                    p = pool.acquire()
                    for ic in range(8):
                        k.op("t", lambda t, dc=dc, ic=ic, p=p: t.matmul(
                            p["t"][:, 0:8], Wq[:, ic, 128 * dc:128 * dc + 128],
                            xgT[:, ic, :], start=(ic == 0), stop=False))
                    mq = k.op("t", lambda t, dc=dc, p=p: t.matmul(
                        p["t"][:, 0:8], brows[0:1, 0, 128 * dc:128 * dc + 128],
                        ones_bf[0:1, 0:8], start=False, stop=True), inc=True)
                    p["wmark"] = mq
                    k.wait("v", "t", mq)
                    for half in range(2):
                        mq2 = k.op("v", lambda v, dc=dc, half=half, p=p: v.tensor_scalar(
                            BD[64 * half:64 * half + 64, dc,
                               16 * dc + 8 * half:16 * dc + 8 * half + 8],
                            p["t"][64 * half:64 * half + 64, 0:8],
                            SCALE, None, ALU.mult), inc=(half == 1))
                        p["readers"].append(("v", mq2))
                    p = pool.acquire()
                    for ic in range(8):
                        k.op("t", lambda t, dc=dc, ic=ic, p=p: t.matmul(
                            p["t"][:, 0:8], Wk[:, ic, 128 * dc:128 * dc + 128],
                            xgT[:, ic, :], start=(ic == 0), stop=False))
                    mk_ = k.op("t", lambda t, dc=dc, p=p: t.matmul(
                        p["t"][:, 0:8], brows[0:1, 1, 128 * dc:128 * dc + 128],
                        ones_bf[0:1, 0:8], start=False, stop=True), inc=True)
                    p["wmark"] = mk_
                    k.wait("v", "t", mk_)
                    mk2 = k.op("v", lambda v, dc=dc, p=p: v.tensor_copy(
                        KgT[:, dc, :], p["t"][:, 0:8]), inc=True)
                    p["readers"].append(("v", mk2))
                k.wait("t", "v", m_bd0)
                for oh in range(2):
                    p = pool.acquire()
                    for ic in range(8):
                        k.op("t", lambda t, oh=oh, ic=ic, p=p: t.matmul(
                            p["t"][0:8, 0:512], xgT[:, ic, :],
                            Wv[:, ic, 512 * oh:512 * oh + 512],
                            start=(ic == 0), stop=False))
                    mvg = k.op("t", lambda t, oh=oh, p=p: t.matmul(
                        p["t"][0:8, 0:512], ones_bf[0:1, 0:8],
                        brows[0:1, 2, 512 * oh:512 * oh + 512],
                        start=False, stop=True), inc=True)
                    p["wmark"] = mvg
                    k.wait("v", "t", mvg)
                    mvg2 = k.op("v", lambda v, oh=oh, p=p: v.tensor_copy(
                        Vg[:, 8 * oh:8 * oh + 8, 0:64],
                        p["t"][0:8, 0:512].rearrange("p (a d) -> p a d", a=8)), inc=True)
                    p["readers"].append(("v", mvg2))

                k.op("v", lambda v: v.memset(macc[:, :], 0.0))
                k.op("v", lambda v: v.memset(fs[:, 1:2], 0.0))
                k.op("v", lambda v: v.memset(fs[:, 0:1], -3.0e38))
                k.op("v", lambda v: v.drain())
                m_f0 = k.op("v", lambda v: v.nop(), inc=True)

                store_marks = []
                lns = {0: carry.pop("ln0")} if "ln0" in carry else \
                    {0: ln_chunk(src, 0, 0)}
                for c in range(nch):
                    slot = c % 2
                    mld, _lnf = lns.pop(c)
                    m_xnT = _lnf()
                    glist = [g for g in range(NG) if gpos[g] // CH == c]

                    k.wait("s", "t", bufm.get("a_qkv_t", 0))
                    k.wait("v", "t", bufm.get("a_qkv_t", 0))
                    for dc in range(8):
                        p = proj_fm(Wq, 128 * dc, m_xnT, slot, bias=0)
                        k.wait("s", "t", p["wmark"])
                        ms = k.op("s", lambda s, dc=dc, p=p: s.activation(
                            QT[:, dc, :], p["t"][:, 0:CH], AF.Copy, scale=SCALE),
                            inc=True)
                        p["readers"].append(("s", ms))
                        p = proj_fm(Wk, 128 * dc, m_xnT, slot, bias=1)
                        k.wait("v", "t", p["wmark"])
                        mv2 = k.op("v", lambda v, dc=dc, p=p: v.tensor_copy(
                            KT[:, dc, :], p["t"][:, 0:CH]), inc=True)
                        p["readers"].append(("v", mv2))
                    for j in range(NT):
                        for oh in range(2):
                            p = pool.acquire()
                            k.wait("t", "v", m_xnT)
                            for ic in range(8):
                                k.op("t", lambda t, j=j, oh=oh, ic=ic, p=p,
                                     slot=slot: t.matmul(
                                    p["t"][:, 0:512],
                                    xnT[slot][:, ic, 128 * j:128 * j + 128],
                                    Wv[:, ic, 512 * oh:512 * oh + 512],
                                    start=(ic == 0), stop=False))
                            mvp = k.op("t", lambda t, oh=oh, p=p: t.matmul(
                                p["t"][:, 0:512], ones_bf[0:1, 0:128],
                                brows[0:1, 2, 512 * oh:512 * oh + 512],
                                start=False, stop=True), inc=True)
                            p["wmark"] = mvp
                            note_xnT_read(slot, mvp)
                            k.wait("v", "t", mvp)
                            mvv = k.op("v", lambda v, j=j, oh=oh, p=p: v.tensor_copy(
                                Vaug[:, j, 8 * oh:8 * oh + 8, 0:64],
                                p["t"][:, 0:512].rearrange("p (a d) -> p a d", a=8)),
                                inc=True)
                            p["readers"].append(("v", mvv))
                    mvo = k.op("v", lambda v: v.memset(Vaug[:, :, :, 64:65], 1.0),
                               inc=True)
                    m_qkv_v = k.cnt["v"]
                    m_qkv_s = k.cnt["s"]
                    if c + 1 < nch:
                        lns[c + 1] = ln_chunk(src, c + 1, (c + 1) % 2)

                    p = pool.acquire()
                    k.wait("t", "v", m_qkv_v)
                    for ic in range(8):
                        k.op("t", lambda t, ic=ic, p=p: t.matmul(
                            p["t"][:, 0:CH], BD[:, ic, :], KT[:, ic, :],
                            start=(ic == 0), stop=(ic == 7)), inc=(ic == 7))
                    p["wmark"] = k.cnt["t"]
                    k.wait("v", "t", p["wmark"])
                    k.wait("v", "s", bufm.get("a_strip_s", 0))
                    k.op("v", lambda v, p=p: v.tensor_reduce(
                        fs[:, 2:3], p["t"][:, 0:CH], AX.X, ALU.max))
                    k.op("v", lambda v: v.drain())
                    k.op("v", lambda v: v.tensor_tensor(
                        fs[:, 3:4], fs[:, 0:1], fs[:, 2:3], ALU.max))
                    k.op("v", lambda v: v.drain())
                    mneg = k.op("v", lambda v: v.tensor_scalar(
                        fs[:, 4:5], fs[:, 3:4], -1.0, None, ALU.mult), inc=True)
                    k.wait("s", "v", mneg)
                    k.op("s", lambda s: s.activation(
                        fs[:, 5:6], fs[:, 0:1], AF.Exp, bias=fs[:, 4:5]))
                    mstr = k.op("s", lambda s, p=p: s.activation(
                        strip_bf[:, 0:CH], p["t"][:, 0:CH], AF.Exp,
                        bias=fs[:, 4:5]), inc=True)
                    bufm["a_strip_s"] = mstr
                    p["readers"].append(("s", mstr))
                    k.wait("v", "s", mstr)
                    k.op("v", lambda v: v.tensor_reduce(
                        fs[:, 6:7], strip_bf[:, 0:CH], AX.X, ALU.add))
                    k.op("v", lambda v: v.drain())
                    k.op("v", lambda v: v.scalar_tensor_tensor(
                        fs[:, 1:2], fs[:, 1:2], fs[:, 5:6], fs[:, 6:7],
                        ALU.mult, ALU.add))
                    k.op("v", lambda v: v.drain())
                    mfl1 = k.op("v", lambda v: v.tensor_copy(fs[:, 0:1], fs[:, 3:4]),
                                inc=True)
                    k.wait("t", "s", mstr)
                    b = bpool.acquire()
                    for j in range(NT):
                        k.op("t", lambda t, j=j, b=b: t.matmul(
                            b["t"][:, 128 * j:128 * j + 128],
                            strip_bf[:, 128 * j:128 * j + 128], idn_bf[:],
                            start=True, stop=True, is_transpose=True), inc=(j == NT - 1))
                    b["wmark"] = k.cnt["t"]
                    bufm["a_stripT_t"] = b["wmark"]
                    k.wait("v", "t", b["wmark"])
                    mpg = k.op("v", lambda v, b=b: v.tensor_copy(
                        PgT[:, :, :], b["t"][:, 0:128 * NT].rearrange(
                            "p (a q) -> p a q", a=NT)), inc=True)
                    b["readers"].append(("v", mpg))
                    k.wait("t", "v", mpg)
                    k.wait("t", "v", mfl1)
                    for dh in range(2):
                        p = pool.acquire()
                        for j in range(NT):
                            k.op("t", lambda t, dh=dh, j=j, p=p: t.matmul(
                                p["t"][:, 0:512].rearrange("p (a d) -> p a d", a=8),
                                PgT[:, j, :],
                                Vaug[:, j, 8 * dh:8 * dh + 8, 0:64],
                                start=(j == 0), stop=(j == NT - 1)), inc=(j == NT - 1))
                        p["wmark"] = k.cnt["t"]
                        k.wait("v", "t", p["wmark"])
                        mfl = k.op("v", lambda v, dh=dh, p=p: v.scalar_tensor_tensor(
                            macc[:, 512 * dh:512 * dh + 512],
                            macc[:, 512 * dh:512 * dh + 512], fs[:, 5:6],
                            p["t"][:, 0:512], ALU.mult, ALU.add), inc=True)
                        p["readers"].append(("v", mfl))

                    def a_tail(pO, h, dc, base):
                        k.wait("v", "t", pO["wmark"])
                        k.wait("v", "t", bufm.get("a_tok_t", 0))
                        k.op("v", lambda v, pO=pO: v.reciprocal(
                            rec4[:, :], pO["t"][:, 0:65 * NT].rearrange(
                                "p (a b) -> p a b", b=65)[:, :, 64]))
                        k.op("v", lambda v: v.drain())
                        for qt in range(NT):
                            mdv = k.op("v", lambda v, qt=qt, pO=pO: v.tensor_scalar(
                                atok[:, qt, :], pO["t"][:, 65 * qt:65 * qt + 64],
                                rec4[:, qt:qt + 1], None, ALU.mult), inc=(qt == NT - 1))
                        pO["readers"].append(("v", mdv))
                        pool.release(pO)
                        k.wait("t", "v", mdv)
                        b = bpool.acquire()
                        for qt in range(NT):
                            k.op("t", lambda t, qt=qt, b=b: t.matmul(
                                b["t"][0:64, 128 * qt:128 * qt + 128],
                                atok[:, qt, :], idn_bf[:],
                                start=True, stop=True, is_transpose=True),
                                inc=(qt == NT - 1))
                        b["wmark"] = k.cnt["t"]
                        bufm["a_tok_t"] = b["wmark"]
                        k.wait("v", "t", b["wmark"])
                        k.wait("v", "t", bufm.get("catT_t", 0))
                        mct = k.op("v", lambda v, dc=dc, base=base, b=b: v.tensor_copy(
                            catT[base:base + 64, dc, :], b["t"][0:64, 0:128 * NT]),
                            inc=True)
                        b["readers"].append(("v", mct))

                    a_pending = None

                    def a_scores(h):
                        dc, base = h // 2, 64 * (h % 2)
                        pA = pool.acquire()
                        pG = pool.acquire()
                        k.wait("t", "s", m_qkv_s)
                        for qt in range(NT):
                            k.op("t", lambda t, qt=qt, pA=pA, dc=dc, base=base: t.matmul(
                                pA["t"][:, 128 * qt:128 * qt + 128],
                                KT[base:base + 64, dc, 128 * qt:128 * qt + 128],
                                QT[base:base + 64, dc, 128 * qt:128 * qt + 128],
                                start=True, stop=False, skip_group_check=True))
                            k.op("t", lambda t, qt=qt, pA=pA: t.matmul(
                                pA["t"][0:64, 128 * qt + 64:128 * qt + 128],
                                ones_bf[0:1, 0:64], neg_bf[0:1, 0:64],
                                start=False, stop=False, skip_group_check=True))
                            k.op("t", lambda t, qt=qt, pA=pA: t.matmul(
                                pA["t"][64:128, 128 * qt:128 * qt + 64],
                                ones_bf[0:1, 0:64], neg_bf[0:1, 0:64],
                                start=False, stop=(qt == NT - 1), skip_group_check=True),
                                inc=(qt == NT - 1))
                        mA = k.cnt["t"]
                        pA["wmark"] = mA
                        for qt in range(NT):
                            last = (qt == NT - 1) and not glist
                            k.op("t", lambda t, qt=qt, pG=pG, dc=dc, base=base, last=last:
                                 t.matmul(pG["t"][0:8, 128 * qt:128 * qt + 128],
                                          KgT[base:base + 64, dc, :],
                                          QT[base:base + 64, dc, 128 * qt:128 * qt + 128],
                                          start=(qt == 0), stop=last,
                                          skip_group_check=True), inc=last)
                        for gi, g in enumerate(glist):
                            off = gpos[g] - c * CH
                            blk0 = (off // 64) * 64
                            last = (gi == len(glist) - 1)
                            k.op("t", lambda t, g=g, blk0=blk0, pG=pG, last=last: t.matmul(
                                pG["t"][0:8, blk0:blk0 + 64],
                                oh8_bf[0:1, 8 * g:8 * g + 8], neg_bf[0:1, 0:64],
                                start=False, stop=last, skip_group_check=True), inc=last)
                        mG = k.cnt["t"]
                        pG["wmark"] = mG
                        return pA, pG, mG

                    def a_headfin(h, pA, pG, mG):
                        dc, base = h // 2, 64 * (h % 2)
                        nonlocal a_pending
                        k.wait("s", "t", mG)
                        k.wait("s", "t", bufm.get("a_PT_pv", 0))
                        msl = k.op("s", lambda s, pA=pA: s.activation(
                            PTl[:, :], pA["t"][:, 0:512], AF.Exp), inc=True)
                        pA["readers"].append(("s", msl))
                        msg = k.op("s", lambda s, pG=pG: s.activation(
                            PTg[:, :], pG["t"][0:8, 0:512], AF.Exp), inc=True)
                        pG["readers"].append(("s", msg))
                        pO = pool.acquire(hold=True)
                        k.wait("t", "s", msg)
                        for qt in range(NT):
                            k.op("t", lambda t, h=h, qt=qt, pO=pO: t.matmul(
                                pO["t"][:, 65 * qt:65 * qt + 65],
                                PTl[:, 128 * qt:128 * qt + 128], Vaug[:, qt, h, :],
                                start=(qt == 0), stop=False, skip_group_check=True))
                            k.op("t", lambda t, h=h, qt=qt, pO=pO: t.matmul(
                                pO["t"][:, 65 * qt:65 * qt + 65],
                                PTg[:, 128 * qt:128 * qt + 128], Vg[:, h, :],
                                start=False, stop=(qt == NT - 1), skip_group_check=True),
                                inc=(qt == NT - 1))
                        pO["wmark"] = k.cnt["t"]
                        bufm["a_PT_pv"] = pO["wmark"]
                        if a_pending is not None:
                            a_tail(*a_pending)
                        a_pending = (pO, h, dc, base)

                    sc_pending = None
                    for h in range(NH):
                        sc = a_scores(h)
                        if sc_pending is not None:
                            a_headfin(*sc_pending)
                        sc_pending = (h,) + sc
                    if sc_pending is not None:
                        a_headfin(*sc_pending)
                        sc_pending = None
                    if a_pending is not None:
                        a_tail(*a_pending)
                        a_pending = None
                    m_cat_v = k.cnt["v"]
                    bufm["a_qkv_t"] = k.cnt["t"]

                    for g in glist:
                        col = gpos[g] - c * CH
                        m_cat_v = k.op("v", lambda v, g=g, col=col: v.tensor_copy(
                            saved[:, :, g], catT[:, :, col]), inc=True)

                    store_marks.append(wo_resid_store(Wo, 3, dst, c, slot, m_cat_v))
                    bufm["catT_t"] = k.cnt["t"]

                k.wait("v", "t", bufm.get("a_stripT_t", 0))
                k.op("v", lambda v: v.reciprocal(fs[:, 7:8], fs[:, 1:2]))
                k.op("v", lambda v: v.drain())
                mnrm = k.op("v", lambda v: v.tensor_scalar(
                    strip_bf[:, :], macc[:, :], fs[:, 7:8], None, ALU.mult), inc=True)
                k.wait("t", "v", mnrm)
                b = bpool.acquire()
                for dcb in range(8):
                    k.op("t", lambda t, dcb=dcb, b=b: t.matmul(
                        b["t"][:, 128 * dcb:128 * dcb + 128],
                        strip_bf[:, 128 * dcb:128 * dcb + 128], idn_bf[:],
                        start=True, stop=True, is_transpose=True), inc=(dcb == 7))
                b["wmark"] = k.cnt["t"]
                k.wait("v", "t", b["wmark"])
                for dcb in range(8):
                    k.op("v", lambda v, dcb=dcb, b=b: v.tensor_copy(
                        cglob[0:64, dcb, :],
                        b["t"][0:64, 128 * dcb + 16 * dcb:128 * dcb + 16 * dcb + 8]))
                    mgl = k.op("v", lambda v, dcb=dcb, b=b: v.tensor_copy(
                        cglob[64:128, dcb, :],
                        b["t"][64:128, 128 * dcb + 16 * dcb + 8:128 * dcb + 16 * dcb + 16]),
                        inc=(dcb == 7))
                b["readers"].append(("v", k.cnt["v"]))
                k.op("v", lambda v: v.drain())
                mdel = k.op("v", lambda v: v.tensor_tensor(
                    cglob[:, :, :], cglob[:, :, :], saved[:, :, :], ALU.subtract),
                    inc=True)
                k.wait("t", "v", mdel)
                k.wait_dmas("y", store_marks)
                mg2 = [k.dma("y", grmw[g:g + 1, :], dst[gpos[g]:gpos[g] + 1, :])
                       for g in range(NG)]
                mfx = 0
                for eh in range(2):
                    p = pool.acquire()
                    for ic in range(8):
                        k.op("t", lambda t, eh=eh, ic=ic, p=p: t.matmul(
                            p["t"][0:8, 0:512], cglob[:, ic, :],
                            Wo[:, ic, 512 * eh:512 * eh + 512],
                            start=(ic == 0), stop=(ic == 7)), inc=(ic == 7))
                    p["wmark"] = k.cnt["t"]
                    k.wait("v", "t", p["wmark"])
                    k.wait_dmas("v", mg2)
                    mfx = k.op("v", lambda v, eh=eh, p=p: v.tensor_tensor(
                        grmw[:, 512 * eh:512 * eh + 512],
                        grmw[:, 512 * eh:512 * eh + 512],
                        p["t"][0:8, 0:512], ALU.add), inc=True)
                    p["readers"].append(("v", mfx))
                k.wait("y", "v", mfx)
                for g in range(NG):
                    k.dma("y", dst[gpos[g]:gpos[g] + 1, :], grmw[g:g + 1, :])
                if prologue:
                    bufm["src_stores"] = k.snap("y")
                    prologue()
            k.barrier()

        def ffn(src, dst, tag, prologue=None):
            with contextlib.ExitStack() as _sst:
                w1s = _sst.enter_context(nc.sbuf_tensor(f"f_w1s{tag}", [128, 2, 8, HID], BF16))
                w3s = _sst.enter_context(nc.sbuf_tensor(f"f_w3s{tag}", [128, 2, 8, HID], BF16))
                w2s = _sst.enter_context(nc.sbuf_tensor(f"f_w2s{tag}", [128, 2, 4, 512], BF16))
                rws = _sst.enter_context(nc.sbuf_tensor(f"f_rws{tag}", [128, 8, 8], BF16))
                H1G = _sst.enter_context(nc.sbuf_tensor(f"f_H1G{tag}", [128, NE_ALL, 4, CH], BF16))
                G = _sst.enter_context(nc.sbuf_tensor(f"f_G{tag}", [128, NT, 32], F32))
                Gbf = _sst.enter_context(nc.sbuf_tensor(f"f_Gbf{tag}", [128, NT, 8], BF16))
                GT = _sst.enter_context(nc.sbuf_tensor(f"f_GT{tag}", [8, CH], BF16))
                sA = _sst.enter_context(nc.sbuf_tensor(f"f_sA{tag}", [128, 512], BF16))
                t1 = _sst.enter_context(nc.sbuf_tensor(f"f_t1{tag}", [128, 512], BF16))
                mrw = k.dma("g", rws[:, :, :], f_rw.rearrange("(a p) e -> p a e", p=128))
                k.wait_dma("t", mrw)
                k.wait_dmas("t", m_prestage)
                k.wait_dmas("t", m_consts_g)
                k.wait("t", "v", m_consts_v)

                lns = {0: carry.pop("ln0")} if "ln0" in carry else \
                    {0: ln_chunk(src, 0, 0)}
                wmk = {}
                w2mk = {}
                for c in range(nch):
                    slot = c % 2
                    mld, _lnf = lns.pop(c)
                    m_xnT = _lnf()

                    k.wait("v", "s", bufm.get("f_gate_s", 0))
                    k.wait("v", "t", bufm.get("f_GT_t", 0))
                    for j in range(NT):
                        p = pool.acquire()
                        k.wait("t", "v", m_xnT)
                        for ic in range(8):
                            k.op("t", lambda t, j=j, ic=ic, p=p, slot=slot: t.matmul(
                                p["t"][:, 0:8],
                                xnT[slot][:, ic, 128 * j:128 * j + 128],
                                rws[:, ic, :], start=(ic == 0), stop=False))
                        mro = k.op("t", lambda t, p=p: t.matmul(
                            p["t"][:, 0:8], ones_bf[0:1, 0:128], brows[0:1, 4, 0:8],
                            start=False, stop=True), inc=True)
                        p["wmark"] = mro
                        note_xnT_read(slot, mro)
                        k.wait("v", "t", mro)
                        k.op("v", lambda v, j=j, p=p: v.tensor_reduce(
                            G[:, j, 24:25], p["t"][:, 0:8], AX.X, ALU.max))
                        k.op("v", lambda v: v.drain())
                        k.op("v", lambda v, j=j, p=p: v.tensor_scalar(
                            G[:, j, 0:8], p["t"][:, 0:8], G[:, j, 24:25], None, ALU.is_ge))
                        k.op("v", lambda v: v.drain())
                        k.op("v", lambda v, j=j: v.tensor_reduce(
                            G[:, j, 25:26], G[:, j, 0:8], AX.X, ALU.add))
                        k.op("v", lambda v, j=j, p=p: v.scalar_tensor_tensor(
                            G[:, j, 8:16], G[:, j, 0:8], NEG, p["t"][:, 0:8],
                            ALU.mult, ALU.add))
                        k.op("v", lambda v: v.drain())
                        k.op("v", lambda v, j=j: v.tensor_reduce(
                            G[:, j, 26:27], G[:, j, 8:16], AX.X, ALU.max))
                        k.op("v", lambda v, j=j: v.tensor_scalar(
                            G[:, j, 27:28], G[:, j, 25:26], 2.0, None, ALU.is_ge))
                        k.op("v", lambda v: v.drain())
                        k.op("v", lambda v, j=j: v.tensor_tensor(
                            G[:, j, 28:29], G[:, j, 24:25], G[:, j, 26:27], ALU.subtract))
                        k.op("v", lambda v: v.drain())
                        k.op("v", lambda v, j=j: v.tensor_tensor(
                            G[:, j, 28:29], G[:, j, 28:29], G[:, j, 27:28], ALU.mult))
                        k.op("v", lambda v: v.drain())
                        k.op("v", lambda v, j=j: v.tensor_tensor(
                            G[:, j, 28:29], G[:, j, 28:29], G[:, j, 26:27], ALU.add))
                        mneg1 = k.op("v", lambda v, j=j: v.tensor_scalar(
                            G[:, j, 29:30], G[:, j, 24:25], -1.0, None, ALU.mult),
                            inc=True)
                        k.wait("s", "v", mneg1)
                        mz = k.op("s", lambda s, j=j, p=p: s.activation(
                            G[:, j, 16:24], p["t"][:, 0:8], AF.Exp,
                            bias=G[:, j, 29:30]), inc=True)
                        bufm["f_gate_s"] = mz
                        p["readers"].append(("s", mz))
                        k.wait("v", "s", mz)
                        mke = k.op("v", lambda v, j=j, p=p: v.tensor_scalar(
                            G[:, j, 0:8], p["t"][:, 0:8], G[:, j, 28:29], None,
                            ALU.is_ge), inc=True)
                        p["readers"].append(("v", mke))
                        k.op("v", lambda v: v.drain())
                        k.op("v", lambda v, j=j: v.tensor_tensor(
                            G[:, j, 8:16], G[:, j, 16:24], G[:, j, 0:8], ALU.mult))
                        k.op("v", lambda v: v.drain())
                        k.op("v", lambda v, j=j: v.tensor_reduce(
                            G[:, j, 30:31], G[:, j, 8:16], AX.X, ALU.add))
                        k.op("v", lambda v: v.drain())
                        k.op("v", lambda v, j=j: v.reciprocal(
                            G[:, j, 31:32], G[:, j, 30:31]))
                        k.op("v", lambda v: v.drain())
                        k.op("v", lambda v, j=j: v.tensor_scalar(
                            Gbf[:, j, :], G[:, j, 8:16], G[:, j, 31:32], None,
                            ALU.mult), inc=(j == NT - 1))
                    m_gates = k.cnt["v"]

                    def gates_fin():
                        k.wait("t", "v", m_gates)
                        b = bpool.acquire()
                        for j in range(NT):
                            k.op("t", lambda t, j=j, b=b: t.matmul(
                                b["t"][0:8, 128 * j:128 * j + 128], Gbf[:, j, :],
                                idn_bf[:], start=True, stop=True,
                                is_transpose=True), inc=(j == NT - 1))
                        b["wmark"] = k.cnt["t"]
                        bufm["f_GT_t"] = b["wmark"]
                        k.wait("v", "t", b["wmark"])
                        mgt = k.op("v", lambda v, b=b: v.tensor_copy(
                            GT[:, :], b["t"][0:8, 0:128 * NT]), inc=True)
                        b["readers"].append(("v", mgt))
                        return mgt
                    mgt = None
                    if c + 1 < nch:
                        lns[c + 1] = ln_chunk(src, c + 1, (c + 1) % 2)

                    EORD = [NEXP, NEXP + 1] + list(range(NEXP))
                    wmk = {}
                    for pos in range(2):
                        k.wait("g", "t", bufm.get(("f_w13_t", pos % 2), 0))
                        wmk[pos] = [
                            k.dma("g", w1s[:, pos % 2, :, :], ffw13[EORD[pos], 0]),
                            k.dma("g", w3s[:, pos % 2, :, :], ffw13[EORD[pos], 1])]
                    mhe = {}
                    for pos in range(NE_ALL):
                        e = EORD[pos]
                        sl = pos % 2
                        routed = e < NEXP
                        pGb = None
                        if routed:
                            if mgt is None:
                                mgt = gates_fin()
                            pGb = pool.acquire(hold=True)
                            k.wait("t", "v", mgt)
                            mgb = k.op("t", lambda t, e=e, pGb=pGb: t.matmul(
                                pGb["t"][:, 0:CH], rsel_bf[:, e, :], GT[:, :],
                                start=True, stop=True), inc=True)
                            pGb["wmark"] = mgb
                        k.wait_dmas("t", wmk[pos])
                        lastw_t = 0
                        for hc in range(4):
                            pA_ = pool.acquire()
                            k.wait("t", "v", m_xnT)
                            for ic in range(8):
                                k.op("t", lambda t, sl=sl, hc=hc, ic=ic, pA_=pA_,
                                     slot=slot: t.matmul(
                                    pA_["t"][:, 0:CH],
                                    w1s[:, sl, ic, 128 * hc:128 * hc + 128],
                                    xnT[slot][:, ic, :], start=(ic == 0), stop=(ic == 7)),
                                    inc=(ic == 7))
                            pA_["wmark"] = k.cnt["t"]
                            pB_ = pool.acquire()
                            for ic in range(8):
                                k.op("t", lambda t, sl=sl, hc=hc, ic=ic, pB_=pB_,
                                     slot=slot: t.matmul(
                                    pB_["t"][:, 0:CH],
                                    w3s[:, sl, ic, 128 * hc:128 * hc + 128],
                                    xnT[slot][:, ic, :], start=(ic == 0), stop=(ic == 7)),
                                    inc=(ic == 7))
                            pB_["wmark"] = k.cnt["t"]
                            lastw_t = pB_["wmark"]
                            note_xnT_read(slot, lastw_t)
                            k.wait("s", "t", pA_["wmark"])
                            k.wait("s", "v", bufm.get("f_sA_v", 0))
                            msa = k.op("s", lambda s, pA_=pA_: s.activation(
                                sA[:, :], pA_["t"][:, 0:CH], AF.Silu), inc=True)
                            pA_["readers"].append(("s", msa))
                            k.wait("v", "s", msa)
                            k.wait("v", "t", pB_["wmark"])
                            k.wait("v", "t", bufm.get("f_H1G_t", 0))
                            k.op("v", lambda v: v.drain())
                            if routed:
                                mt1 = k.op("v", lambda v, pB_=pB_: v.scalar_tensor_tensor(
                                    t1[:, :], pB_["t"][:, 0:CH], 1.0, sA[:, :],
                                    ALU.mult, ALU.mult), inc=True)
                                pB_["readers"].append(("v", mt1))
                                bufm["f_sA_v"] = mt1
                                k.op("v", lambda v: v.drain())
                                mh = k.op("v", lambda v, e=e, hc=hc, pGb=pGb: v.tensor_tensor(
                                    H1G[:, e, hc, :], t1[:, :], pGb["t"][:, 0:CH],
                                    ALU.mult), inc=True)
                                pGb["readers"].append(("v", mh))
                            else:
                                mh = k.op("v", lambda v, e=e, hc=hc, pB_=pB_:
                                          v.scalar_tensor_tensor(
                                              H1G[:, e, hc, :], pB_["t"][:, 0:CH], 1.0,
                                              sA[:, :], ALU.mult, ALU.mult), inc=True)
                                pB_["readers"].append(("v", mh))
                                bufm["f_sA_v"] = mh
                        bufm[("f_w13_t", sl)] = lastw_t
                        mhe[e] = k.cnt["v"]
                        if pGb is not None:
                            pool.release(pGb)
                        if pos + 2 < NE_ALL:
                            np_ = pos + 2
                            k.wait("g", "t", bufm.get(("f_w13_t", np_ % 2), 0))
                            wmk[np_] = [
                                k.dma("g", w1s[:, np_ % 2, :, :], ffw13[EORD[np_], 0]),
                                k.dma("g", w3s[:, np_ % 2, :, :], ffw13[EORD[np_], 1])]
                    m_H1G = k.cnt["v"]

                    w2mk = {}
                    for dh in range(2):
                        for pos in range(2):
                            k.wait("g", "t", bufm.get(("f_w2_t", pos % 2), 0))
                            w2mk[pos] = k.dma("g", w2s[:, pos % 2, :, :],
                                              ffw2[EORD[pos], dh])
                        accs = [pool.acquire(hold=True) for _ in range(NT)]
                        m_B = 0
                        for pos in range(NE_ALL):
                            e = EORD[pos]
                            sl = pos % 2
                            k.wait("t", "v", mhe[e])
                            k.wait_dma("t", w2mk[pos])
                            for j in range(NT):
                                for hc in range(4):
                                    k.op("t", lambda t, e=e, sl=sl, j=j, hc=hc,
                                         pos=pos, a_=accs[j]: t.matmul(
                                             a_["t"][:, 0:512],
                                             H1G[:, e, hc, 128 * j:128 * j + 128],
                                             w2s[:, sl, hc, :],
                                             start=(pos == 0 and hc == 0),
                                             stop=(pos == NE_ALL - 1 and hc == 3),
                                             skip_group_check=True),
                                         inc=(j == NT - 1 and hc == 3))
                            bufm[("f_w2_t", sl)] = k.cnt["t"]
                            m_B = k.cnt["t"]
                            if pos + 2 < NE_ALL:
                                np_ = pos + 2
                                k.wait("g", "t", bufm.get(("f_w2_t", np_ % 2), 0))
                                w2mk[np_] = k.dma("g", w2s[:, np_ % 2, :, :],
                                                  ffw2[EORD[np_], dh])
                        for j in range(NT):
                            accs[j]["wmark"] = m_B
                            k.wait("v", "t", m_B)
                            k.wait_dma("v", bufm.get("xout_store"))
                            mev = k.op("v", lambda v, j=j, dh=dh, a_=accs[j],
                                       slot=slot: v.scalar_tensor_tensor(
                                           xout[:, j, 512 * dh:512 * dh + 512],
                                           a_["t"][:, 0:512], 1.0,
                                           xf[slot][:, j, 512 * dh:512 * dh + 512],
                                           ALU.mult, ALU.add), inc=True)
                            accs[j]["readers"].append(("v", mev))
                            pool.release(accs[j])
                    bufm["f_H1G_t"] = k.cnt["t"]
                    bufm[("xf_v", slot)] = k.cnt["v"]
                    last_v = k.cnt["v"]
                    k.wait("y", "v", last_v)
                    mst = k.dma("y", dst[c * CH:(c + 1) * CH, :]
                                .rearrange("(j p) d -> p j d", p=128), xout[:, :, :])
                    bufm["xout_store"] = mst
                    if prologue and c == nch - 1:
                        bufm["src_stores"] = k.snap("y")
                        prologue()
            k.barrier()

        def attn_b(src, dst, prologue=None):
            with contextlib.ExitStack() as _sst:
                wqT = _sst.enter_context(nc.sbuf_tensor("b_wqT", [128, 8, D], BF16))
                wkvK = _sst.enter_context(nc.sbuf_tensor("b_wkvK", [128, 8, 256], BF16))
                wkvV = _sst.enter_context(nc.sbuf_tensor("b_wkvV", [128, 8, 256], BF16))
                woT = _sst.enter_context(nc.sbuf_tensor("b_woT", [128, 8, D], BF16))
                Kz = _sst.enter_context(nc.sbuf_tensor("b_Kz", [128, 4, S], BF16))
                if 2 * S >= 8 * D:
                    wnat = Kz[:, 0:2, :].rearrange("p a (b d) -> p (a b) d",
                                                   b=S // D)[:, 0:8, :]
                else:
                    wnat = _sst.enter_context(
                        nc.sbuf_tensor("b_wnat", [128, 8, D], BF16))
                Vb = _sst.enter_context(nc.sbuf_tensor("b_Vaug", [128, NKT, 4, 65], BF16))
                QTb = _sst.enter_context(nc.sbuf_tensor("b_QT", [128, 8, CH], BF16))
                PT0 = _sst.enter_context(nc.sbuf_tensor("b_PT0", [128, 512], BF16))
                PT1 = _sst.enter_context(nc.sbuf_tensor("b_PT1", [128, 512], BF16))
                PT2 = _sst.enter_context(nc.sbuf_tensor("b_PT2", [128, 512], BF16))
                PT3 = _sst.enter_context(nc.sbuf_tensor("b_PT3", [128, 512], BF16))
                btok = _sst.enter_context(nc.sbuf_tensor("b_tok", [128, 4, 64], BF16))
                brec = _sst.enter_context(nc.sbuf_tensor("b_rec", [128, 4], F32))
                PTs = [PT0, PT1, PT2, PT3]

                def load_transpose(nat_src, n_oc, evac_plan, region=None,
                                   preloaded=None):
                    region = wnat if region is None else region
                    if preloaded is None:
                        k.wait("g", "t", k.cnt["t"])
                        mldw = k.dma("g", region[:, 0:n_oc, :], nat_src)
                    else:
                        mldw = preloaded
                    k.wait_dma("t", mldw)
                    wnat_ = region
                    for a in range(n_oc):
                        for icb in range(8):
                            b = bpool.acquire()
                            k.op("t", lambda t, a=a, icb=icb, b=b, wnat_=wnat_:
                                 t.matmul(
                                     b["t"][:, 0:128],
                                     wnat_[:, a, 128 * icb:128 * icb + 128],
                                     idn_bf[:], start=True, stop=True,
                                     is_transpose=True), inc=True)
                            b["wmark"] = k.cnt["t"]
                            k.wait("v", "t", b["wmark"])
                            for dst_ap, c0, c1 in evac_plan(a, icb):
                                mev = k.op("v", lambda v, dst_ap=dst_ap, c0=c0, c1=c1,
                                           b=b: v.tensor_copy(dst_ap, b["t"][:, c0:c1]),
                                           inc=True)
                                b["readers"].append(("v", mev))
                    return k.cnt["v"]

                def wq_plan(a, icb):
                    h0, h1 = 2 * a, 2 * a + 1
                    return [(wqT[:, icb, 64 * c64(h0):64 * c64(h0) + 64], 0, 64),
                            (wqT[:, icb, 64 * c64(h1):64 * c64(h1) + 64], 64, 128)]
                wkv_region = Kz[:, 2:3, :].rearrange("p a (b d) -> p (a b) d",
                                                      b=S // D)[:, 0:4, :] \
                    if S >= 4 * D else None
                mq_pre = k.dma("g", wnat[:, 0:8, :],
                               ab_Wq.rearrange("(a p) i -> p a i", p=128))
                mkv_pre = (k.dma("g", wkv_region[:, 0:4, :],
                                 ab_Wkv.rearrange("(a p) i -> p a i", p=128))
                           if wkv_region is not None else None)
                mwq = load_transpose(None, 8, wq_plan, preloaded=mq_pre)

                def wkv_plan(a, icb):
                    return [(wkvK[:, icb, 64 * a:64 * a + 64], 0, 64),
                            (wkvV[:, icb, 64 * a:64 * a + 64], 64, 128)]
                if wkv_region is not None:
                    mwkv = load_transpose(None, 4, wkv_plan, region=wkv_region,
                                          preloaded=mkv_pre)
                else:
                    mwkv = load_transpose(ab_Wkv.rearrange("(a p) i -> p a i", p=128),
                                          4, wkv_plan)

                def wo_plan(a, icb):
                    return [(woT[:, icb, 128 * a:128 * a + 128], 0, 128)]
                mwo = load_transpose(ab_Wo.rearrange("(a p) i -> p a i", p=128), 8, wo_plan)

                k.op("v", lambda v: v.memset(Vb[:, :, :, 64:65], 1.0))
                for g_ in range(4):
                    lo = 64 * (1 - (g_ % 2))
                    k.op("v", lambda v, g_=g_, lo=lo: v.memset(
                        Kz[lo:lo + 64, g_, :], 0.0))
                k.op("v", lambda v: v.drain())
                m_vb1 = k.op("v", lambda v: v.nop(), inc=True)
                k.wait_dma("v", m_al)

                lns = {0: carry.pop("ln0")} if "ln0" in carry else \
                    {0: ln_chunk(src, 0, 0)}
                for c in range(nch):
                    slot = c % 2
                    mld, _lnf = lns.pop(c)
                    m_xnT = _lnf()
                    k.wait("s", "t", bufm.get("b_QT_t", 0))
                    for s_ in range(8):
                        p = proj_fm(wqT, 128 * s_, m_xnT, slot, bias=None)
                        k.wait("s", "t", p["wmark"])
                        msq = k.op("s", lambda s, s_=s_, p=p: s.activation(
                            QTb[:, s_, :], p["t"][:, 0:CH], AF.Copy, scale=SCALE),
                            inc=True)
                        p["readers"].append(("s", msq))
                    for pr in range(2):
                        p = proj_fm(wkvK, 128 * pr, m_xnT, slot, bias=None)
                        k.wait("v", "t", p["wmark"])
                        k.wait("v", "v", m_vb1)
                        k.op("v", lambda v, pr=pr, c=c, p=p: v.tensor_copy(
                            Kz[0:64, 2 * pr, c * CH:(c + 1) * CH],
                            p["t"][0:64, 0:CH]))
                        mkv = k.op("v", lambda v, pr=pr, c=c, p=p: v.tensor_copy(
                            Kz[64:128, 2 * pr + 1, c * CH:(c + 1) * CH],
                            p["t"][64:128, 0:CH]), inc=True)
                        p["readers"].append(("v", mkv))
                    for j in range(NT):
                        p = pool.acquire()
                        k.wait("t", "v", m_xnT)
                        for ic in range(8):
                            k.op("t", lambda t, j=j, ic=ic, p=p, slot=slot: t.matmul(
                                p["t"][:, 0:256],
                                xnT[slot][:, ic, 128 * j:128 * j + 128],
                                wkvV[:, ic, :], start=(ic == 0), stop=(ic == 7)),
                                inc=(ic == 7))
                        note_xnT_read(slot, k.cnt["t"])
                        p["wmark"] = k.cnt["t"]
                        k.wait("v", "t", p["wmark"])
                        k.wait("v", "v", m_vb1)
                        mkv = k.op("v", lambda v, j=j, c=c, p=p: v.tensor_copy(
                            Vb[:, NT * c + j, :, 0:64],
                            p["t"][:, 0:256].rearrange("p (a d) -> p a d", a=4)),
                            inc=True)
                        p["readers"].append(("v", mkv))
                    m_kv = k.cnt["v"]
                    m_q = k.cnt["s"]
                    if c + 1 < nch:
                        lns[c + 1] = ln_chunk(src, c + 1, (c + 1) % 2)

                    def b_tail(pO, h, qci, qc, nqt, qw):
                        k.wait("v", "t", pO["wmark"])
                        k.wait("v", "t", bufm.get("b_tok_t", 0))
                        k.op("v", lambda v, nqt=nqt, pO=pO: v.reciprocal(
                            brec[:, 0:nqt], pO["t"][:, 0:65 * nqt].rearrange(
                                "p (a b) -> p a b", b=65)[:, :, 64]))
                        k.op("v", lambda v: v.drain())
                        for qt in range(nqt):
                            mdv = k.op("v", lambda v, qt=qt, qw=qw, pO=pO:
                                       v.tensor_scalar(
                                           btok[0:qw, qt, :],
                                           pO["t"][0:qw, 65 * qt:65 * qt + 64],
                                           brec[0:qw, qt:qt + 1], None, ALU.mult),
                                       inc=(qt == nqt - 1))
                        pO["readers"].append(("v", mdv))
                        pool.release(pO)
                        k.wait("t", "v", mdv)
                        b = bpool.acquire()
                        for qt in range(nqt):
                            k.op("t", lambda t, qt=qt, qw=qw, b=b: t.matmul(
                                b["t"][0:64, 128 * qt:128 * qt + qw],
                                btok[0:qw, qt, :], idn_bf[0:qw, 0:qw],
                                start=True, stop=True, is_transpose=True),
                                inc=(qt == nqt - 1))
                        b["wmark"] = k.cnt["t"]
                        bufm["b_tok_t"] = b["wmark"]
                        k.wait("v", "t", b["wmark"])
                        k.wait("v", "t", bufm.get("catT_t", 0))
                        mct = k.op("v", lambda v, h=h, qci=qci, qc=qc, nqt=nqt,
                                   qw=qw, b=b: v.tensor_copy(
                                       catT[64 * (h % 2):64 * (h % 2) + 64, h // 2,
                                            qci * qc:qci * qc + qc].rearrange(
                                           "p (a q) -> p a q", a=nqt),
                                       b["t"][0:64, 0:128 * nqt].rearrange(
                                           "p (a q) -> p a q", a=nqt)[:, :, 0:qw]),
                                   inc=True)
                        b["readers"].append(("v", mct))

                    pending = None
                    for h in range(NH):
                        g = h // 4
                        s_, hi = b_slot(h)
                        base = 64 * hi
                        slope = 2.0 ** (-(h + 1) / 2.0)
                        qc = qc_size(h)
                        nqt = max(1, qc // 128)
                        qw = min(qc, 128)
                        for qci in range(CH // qc):
                            q0 = c * CH + qci * qc
                            kts = [kt for kt in range((q0 + qc + 127) // 128)
                                   if kt_keep(slope, q0, qc, kt)]
                            pO = pool.acquire(hold=True)
                            bank_started = [False]

                            def b_scores(ki, kt, q0=q0, qci=qci, qc=qc, g=g, s_=s_,
                                         base=base, h=h, qw=qw):
                                j0 = 128 * max(0, (128 * kt - q0 - qw + 128) // 128)
                                pS = pool.acquire()
                                k.wait("t", "v", m_kv)
                                k.wait("t", "s", m_q)
                                mS = k.op("t", lambda t, g=g, s_=s_, kt=kt, j0=j0,
                                          q0=q0, qci=qci, qc=qc, pS=pS: t.matmul(
                                              pS["t"][:, j0:qc],
                                              Kz[:, g, 128 * kt:128 * kt + 128],
                                              QTb[:, s_,
                                                  qci * qc + j0:qci * qc + qc],
                                              start=True, stop=True,
                                              skip_group_check=True), inc=True)
                                pS["wmark"] = mS
                                if 128 * kt + 127 >= q0 + j0:
                                    k.wait("v", "t", mS)
                                    mtri = k.op("v", lambda v, kt=kt, q0=q0, qc=qc,
                                                j0=j0, pS=pS: v.tensor_tensor(
                                                    pS["t"][:, j0:qc],
                                                    pS["t"][:, j0:qc],
                                                    tri_bf[:, 512 + q0 - 128 * kt + j0:
                                                           512 + q0 - 128 * kt + qc],
                                                    ALU.add), inc=True)
                                    pS["readers"].append(("v", mtri))
                                    k.wait("s", "v", mtri)
                                k.wait("s", "t", mS)
                                k.wait("s", "t", bufm.get(("b_PT_t", ki % 4), 0))
                                col = al_cols[(h, 128 * kt - q0 - qc // 2)]
                                msx = k.op("s", lambda s, qc=qc, col=col, ki=ki,
                                           j0=j0, pS=pS: s.activation(
                                               PTs[ki % 4][:, j0:qc],
                                               pS["t"][:, j0:qc],
                                               AF.Exp, bias=alibi_sb[:, col:col + 1]),
                                           inc=True)
                                pS["readers"].append(("s", msx))
                                return msx

                            def b_pv(ki, kt, msx, q0=q0, qc=qc, g=g, qw=qw, nqt=nqt,
                                     pO=pO):
                                k.wait("t", "s", msx)
                                qts = [qt for qt in range(nqt)
                                       if 128 * kt <= q0 + 128 * qt + qw - 1]
                                for qi, qt in enumerate(qts):
                                    st = not bank_started[0]
                                    bank_started[0] = True
                                    k.op("t", lambda t, qt=qt, kt=kt, ki=ki, g=g,
                                         qw=qw, st=st, pO=pO: t.matmul(
                                             pO["t"][0:qw, 65 * qt:65 * qt + 65],
                                             PTs[ki % 4][:, 128 * qt:128 * qt + qw],
                                             Vb[:, kt, g, :],
                                             start=st, stop=False,
                                             skip_group_check=True),
                                         inc=(qi == len(qts) - 1))
                                bufm[("b_PT_t", ki % 4)] = k.cnt["t"]

                            from collections import deque as _dq
                            pend = _dq()
                            for ki, kt in enumerate(kts):
                                msx = b_scores(ki, kt)
                                pend.append((ki, kt, msx))
                                if len(pend) > 2:
                                    b_pv(*pend.popleft())
                            while pend:
                                b_pv(*pend.popleft())
                            pO["wmark"] = k.cnt["t"]
                            if pending is not None:
                                b_tail(*pending)
                            pending = (pO, h, qci, qc, nqt, qw)
                    if pending is not None:
                        b_tail(*pending)
                        pending = None
                    m_cat_v = k.cnt["v"]
                    bufm["b_QT_t"] = k.cnt["t"]
                    if not os.environ.get("MK_NOWO"):
                        wo_resid_store(woT, None, dst, c, slot, m_cat_v)
                    bufm["catT_t"] = k.cnt["t"]
                    if prologue and c == nch - 1:
                        bufm["src_stores"] = k.snap("y")
                        prologue()
            k.barrier()

        srcs = {0: x_in, 1: Xa, 2: Xb, 3: Xa}
        dsts = {0: Xa, 1: Xb, 2: Xa, 3: y_out}
        order = [s_ for s_ in (0, 1, 2, 3) if s_ in subset]
        carry = {}
        plan = []
        for idx, s_ in enumerate(order):
            src = x_in if idx == 0 else srcs[s_]
            dst = y_out if idx == len(order) - 1 else dsts[s_]
            plan.append((s_, src, dst))
        for idx, (s_, src, dst) in enumerate(plan):
            if idx + 1 < len(plan):
                nxt_src = plan[idx + 1][1]

                def prologue(nxt_src=nxt_src):
                    carry["ln0"] = ln_chunk(nxt_src, 0, 0)
            else:
                prologue = None
            if s_ == 0:
                attn_a(src, dst, prologue)
            elif s_ in (1, 3):
                ffn(src, dst, s_, prologue)
            else:
                attn_b(src, dst, prologue)

        for q in "ygvs":
            for lane, cntv in enumerate(k.dcnt[q]):
                k.wait_dma(q, (q, lane, cntv))
        k.flush_all()

    return nc


def build_inputs(x_core, w, S, CH):
    hc = host_consts(S, CH)
    f32 = np.float32
    return {
        "x": np.ascontiguousarray(x_core, f32),
        "aa_Wq": w["L0_attn_a_Wq"], "aa_bq": w["L0_attn_a_bq"],
        "aa_Wk": w["L0_attn_a_Wk"], "aa_bk": w["L0_attn_a_bk"],
        "aa_Wv": w["L0_attn_a_Wv"], "aa_bv": w["L0_attn_a_bv"],
        "aa_Wo": w["L0_attn_a_Wo"], "aa_bo": w["L0_attn_a_bo"],
        "ab_Wq": w["L0_attn_b_Wq"], "ab_Wkv": w["L0_attn_b_Wkv"],
        "ab_Wo": w["L0_attn_b_Wo"],
        "f_rw": w["L0_ffn_router_w"], "f_rb": w["L0_ffn_router_b"],
        "f_w1": w["L0_ffn_routed_w1"], "f_w3": w["L0_ffn_routed_w3"],
        "f_w2": w["L0_ffn_routed_w2"],
        "f_s1": w["L0_ffn_shared_w1"], "f_s3": w["L0_ffn_shared_w3"],
        "f_s2": w["L0_ffn_shared_w2"],
        "idn": hc["idn"], "tri": hc["tri"], "oh8": hc["oh8"],
        "rowsel": hc["rowsel"], "alibi": hc["alibi"],
    }


_S, _CH = 4096, 512
_NCORES = 8


def kernel(**inputs):
    x = np.ascontiguousarray(inputs["L0_attn_a_x"], np.float32)
    hc = host_consts(_S, _CH)
    shared = {
        "aa_Wq": inputs["L0_attn_a_Wq"], "aa_bq": inputs["L0_attn_a_bq"],
        "aa_Wk": inputs["L0_attn_a_Wk"], "aa_bk": inputs["L0_attn_a_bk"],
        "aa_Wv": inputs["L0_attn_a_Wv"], "aa_bv": inputs["L0_attn_a_bv"],
        "aa_Wo": inputs["L0_attn_a_Wo"], "aa_bo": inputs["L0_attn_a_bo"],
        "ab_Wq": inputs["L0_attn_b_Wq"], "ab_Wkv": inputs["L0_attn_b_Wkv"],
        "ab_Wo": inputs["L0_attn_b_Wo"],
        "f_rw": inputs["L0_ffn_router_w"], "f_rb": inputs["L0_ffn_router_b"],
        "f_w1": inputs["L0_ffn_routed_w1"], "f_w3": inputs["L0_ffn_routed_w3"],
        "f_w2": inputs["L0_ffn_routed_w2"],
        "f_s1": inputs["L0_ffn_shared_w1"], "f_s3": inputs["L0_ffn_shared_w3"],
        "f_s2": inputs["L0_ffn_shared_w2"],
        "idn": hc["idn"], "tri": hc["tri"], "oh8": hc["oh8"],
        "rowsel": hc["rowsel"], "alibi": hc["alibi"],
    }
    shared = {k_: np.ascontiguousarray(v_, np.float32) for k_, v_ in shared.items()}
    nc = build_core(S=_S, CH=_CH, subset=(0, 1, 2, 3))
    in_maps = [dict(shared, x=x[b]) for b in range(_NCORES)]
    res = run_bass_kernel_spmd(nc, in_maps, core_ids=list(range(_NCORES)))
    return np.stack([res.results[b]["y"] for b in range(_NCORES)], axis=0)
